# version 03
# baseline (speedup 1.0000x reference)
import contextlib
import numpy as np
import ml_dtypes
import concourse.bass as bass
import concourse.mybir as mybir
from concourse.bass_utils import run_bass_kernel_spmd

F32 = mybir.dt.float32
BF16 = mybir.dt.bfloat16
AF = mybir.ActivationFunctionType
ALU = mybir.AluOpType
AX = mybir.AxisListType

B, S, D = 16, 4096, 1024
NCORES = 8
SPC = B // NCORES
NT = S // 128
DC = D // 128
NE = 16
CHT = 16
NCH = NT // CHT
EPS = 1e-5
ISC = 1.0 / 32.0

ENG2SEM = {"tensor": "t", "vector": "v", "scalar": "s", "gpsimd": "g"}
QENG = {"qs": "sync", "qc": "scalar", "qg": "gpsimd"}


class Buf:
    __slots__ = ("w", "rs")

    def __init__(self):
        self.w = None
        self.rs = []


class Prog:
    def __init__(self, nc, ctx):
        self.nc = nc
        self.pend = {e: [] for e in ("sync", "tensor", "vector", "scalar", "gpsimd")}
        self.sems = {}
        self.cnt = {}
        self.hw = {e: {} for e in self.pend}
        for sn in ("t", "v", "s", "g", "qs", "qc", "qg"):
            self.sems[sn] = ctx.enter_context(nc.semaphore("sem_" + sn))
            self.cnt[sn] = 0
        self.nops = 0

    def _prune(self, eng, evs):
        best = {}
        for ev in evs:
            if ev is None:
                continue
            sn, val = ev
            if eng == "tensor" and sn == "t":
                continue
            if best.get(sn, 0) < val:
                best[sn] = val
        hw = self.hw[eng]
        out = []
        for sn, val in best.items():
            if hw.get(sn, 0) >= val:
                continue
            hw[sn] = val
            out.append((sn, val))
        return out

    def _gather(self, reads, writes, extra):
        evs = list(extra)
        for b in reads:
            evs.append(b.w)
        for b in writes:
            evs.append(b.w)
            evs.extend(b.rs)
        return evs

    def op(self, eng, emit, reads=(), writes=(), extra=()):
        waits = self._prune(eng, self._gather(reads, writes, extra))
        sn = ENG2SEM[eng]
        self.cnt[sn] += 1
        ev = (sn, self.cnt[sn])
        sems = self.sems

        def run(e, waits=waits, emit=emit, sem=sems[sn], sems=sems):
            for wn, wv in waits:
                e.wait_ge(sems[wn], wv)
            emit(e).then_inc(sem, 1)

        self.pend[eng].append(run)
        self.nops += 1
        for b in reads:
            b.rs.append(ev)
        for b in writes:
            b.w = ev
            b.rs = []
        return ev

    def dma(self, q, out, in_, reads=(), writes=(), extra=()):
        eng = QENG[q]
        waits = self._prune(eng, self._gather(reads, writes, extra))
        self.cnt[q] += 16
        ev = (q, self.cnt[q])
        sems = self.sems

        def run(e, waits=waits, out=out, in_=in_, sem=sems[q], sems=sems):
            for wn, wv in waits:
                e.wait_ge(sems[wn], wv)
            e.dma_start(out=out, in_=in_).then_inc(sem, 16)

        self.pend[eng].append(run)
        self.nops += 1
        for b in reads:
            b.rs.append(ev)
        for b in writes:
            b.w = ev
            b.rs = []
        return ev

    def barrier(self):
        cur = [(sn, v) for sn, v in self.cnt.items() if v > 0]
        sems = self.sems
        for eng in self.pend:
            waits = self._prune(eng, cur)
            if waits:
                def run(e, waits=waits, sems=sems):
                    for wn, wv in waits:
                        e.wait_ge(sems[wn], wv)
                self.pend[eng].append(run)

    def flush(self):
        pend = {e: lst for e, lst in self.pend.items() if lst}
        self.pend = {e: [] for e in self.pend}
        if not pend:
            return
        with self.nc.Block() as blk:
            for eng, lst in pend.items():
                getattr(blk, eng)(lambda e, lst=lst: [f(e) for f in lst])


def make_stats(ctx, nc, name):
    t = ctx.enter_context(nc.sbuf_tensor(name, [128, 18], F32))
    return dict(t=t, bn_a=Buf(), bn_b=Buf(), mv=Buf(), ve=Buf(), lnv=Buf(), rstd=Buf())


def emit_ln(P, st, src, src_bufs, dst, dst_buf, norm_eng="vector"):
    t = st["t"]
    P.op("vector", lambda e, o=t[:, 0:6], i=src[:, 0:512]: e.bn_stats(out=o, in_=i),
         reads=src_bufs, writes=[st["bn_a"]])
    P.op("vector", lambda e, o=t[:, 6:12], i=src[:, 512:1024]: e.bn_stats(out=o, in_=i),
         reads=src_bufs, writes=[st["bn_b"]])
    P.op("vector", lambda e, o=t[:, 12:14], i=t[:, 0:12]: e.bn_aggr(out=o, in_=i),
         reads=[st["bn_a"], st["bn_b"]], writes=[st["mv"]])
    P.op("vector", lambda e, o=t[:, 14:15], i=t[:, 13:14]:
         e.tensor_scalar(out=o, in0=i, scalar1=EPS, scalar2=None, op0=ALU.add),
         reads=[st["mv"]], writes=[st["ve"]])
    P.op("scalar", lambda e, o=t[:, 15:16], i=t[:, 14:15]:
         e.activation(out=o, in_=i, func=AF.Ln),
         reads=[st["ve"]], writes=[st["lnv"]])
    P.op("scalar", lambda e, o=t[:, 16:17], i=t[:, 15:16]:
         e.activation(out=o, in_=i, func=AF.Exp, scale=-0.5),
         reads=[st["lnv"]], writes=[st["rstd"]])
    P.op(norm_eng, lambda e, o=dst, i=src, m=t[:, 12:13], r=t[:, 16:17]:
         e.tensor_scalar(out=o, in0=i, scalar1=m, scalar2=r,
                         op0=ALU.subtract, op1=ALU.mult),
         reads=list(src_bufs) + [st["mv"], st["rstd"]], writes=[dst_buf])


def build(upto="full", taps=()):
    taps = set(taps)
    nc = bass.Bass("TRN2", target_bir_lowering=False)

    Xd = nc.dram_tensor("X", [SPC, S, D], F32, kind="ExternalInput").ap()
    Yd = nc.dram_tensor("Y", [SPC, S, D], F32, kind="ExternalOutput").ap()
    x1s = nc.dram_tensor("x1scr", [SPC, S, D], F32, kind="Internal").ap()
    uTs = nc.dram_tensor("uTscr", [SPC, NT, 128, DC * 128], BF16, kind="Internal").ap()
    Wqd = nc.dram_tensor("Wq", [D, D], F32, kind="ExternalInput").ap()
    Wkd = nc.dram_tensor("Wk", [D, D], F32, kind="ExternalInput").ap()
    Wvd = nc.dram_tensor("Wv", [D, D], F32, kind="ExternalInput").ap()
    bqd = nc.dram_tensor("bq", [1, D], F32, kind="ExternalInput").ap()
    bkd = nc.dram_tensor("bk", [1, D], F32, kind="ExternalInput").ap()
    bvd = nc.dram_tensor("bv", [1, D], F32, kind="ExternalInput").ap()
    CTd = nc.dram_tensor("CT", [D, NE], F32, kind="ExternalInput").ap()
    ebd = nc.dram_tensor("eb", [1, NE], F32, kind="ExternalInput").ap()
    WrTd = nc.dram_tensor("WrT", [NE, D, D], F32, kind="ExternalInput").ap()
    brd = nc.dram_tensor("br", [NE, D], F32, kind="ExternalInput").ap()
    WsTd = nc.dram_tensor("WsT", [2, D, D], F32, kind="ExternalInput").ap()
    bsd = nc.dram_tensor("bs", [2, D], F32, kind="ExternalInput").ap()
    identbd = nc.dram_tensor("identb", [128, 128], BF16, kind="ExternalInput").ap()
    identfd = nc.dram_tensor("identf", [128, 128], F32, kind="ExternalInput").ap()
    maskd = nc.dram_tensor("mask", [128, 128], F32, kind="ExternalInput").ap()
    ones1d = nc.dram_tensor("ones1", [1, 128], BF16, kind="ExternalInput").ap()
    halvesd = nc.dram_tensor("halves", [1, 128], BF16, kind="ExternalInput").ap()

    tap_outs = {}

    def tap(P, name, ap, readbufs, shape, dtype):
        if name in taps:
            t = nc.dram_tensor("tap_" + name, list(shape), dtype, kind="ExternalOutput").ap()
            tap_outs[name] = t
            P.dma("qs", t, ap, reads=list(readbufs))

    with contextlib.ExitStack() as octx:
        P = Prog(nc, octx)
        OE = octx.enter_context

        identb = OE(nc.sbuf_tensor("identb_sb", [128, 128], BF16)); identb_b = Buf()
        ones1 = OE(nc.sbuf_tensor("ones1_sb", [1, 128], BF16)); ones1_b = Buf()
        halves = OE(nc.sbuf_tensor("halves_sb", [1, 128], BF16)); halves_b = Buf()

        psA = OE(nc.psum_tensor("psA", [128, 1024], F32)); psA_b = Buf()
        psB = OE(nc.psum_tensor("psB", [128, 1024], F32)); psB_b = Buf()
        psT = OE(nc.psum_tensor("psT", [128, 1024], BF16)); psT_b = Buf()
        psT2 = OE(nc.psum_tensor("psT2", [128, 1024], BF16)); psT2_b = Buf()
        psS0 = OE(nc.psum_tensor("psS0", [128, 512], F32)); psS0_b = Buf()
        psS1 = OE(nc.psum_tensor("psS1", [128, 512], F32)); psS1_b = Buf()
        psS = [psS0, psS1]; psS_b = [psS0_b, psS1_b]

        P.dma("qs", identb[:], identbd, writes=[identb_b])
        P.dma("qs", ones1[:], ones1d, writes=[ones1_b])
        P.dma("qs", halves[:], halvesd, writes=[halves_b])

        for seq in range(SPC):
            with contextlib.ExitStack() as sctx:
                SE = sctx.enter_context
                KT = SE(nc.sbuf_tensor(f"KT{seq}", [128, DC * S], BF16))
                Vb = SE(nc.sbuf_tensor(f"V{seq}", [128, NT * D], BF16))
                kt_tb = [Buf() for _ in range(NT)]
                v_tb = [Buf() for _ in range(NT)]

                with contextlib.ExitStack() as actx:
                    AE = actx.enter_context
                    wk = AE(nc.sbuf_tensor(f"wk{seq}", [128, DC * 1024], BF16)); wk_b = Buf()
                    wv = AE(nc.sbuf_tensor(f"wv{seq}", [128, DC * 1024], BF16)); wv_b = Buf()
                    bkr = AE(nc.sbuf_tensor(f"bkr{seq}", [1, D], BF16)); bkr_b = Buf()
                    bvr = AE(nc.sbuf_tensor(f"bvr{seq}", [1, D], BF16)); bvr_b = Buf()
                    xt = [AE(nc.sbuf_tensor(f"xtA{seq}_{i}", [128, D], F32)) for i in range(2)]
                    xt_b = [Buf(), Buf()]
                    usb = [AE(nc.sbuf_tensor(f"uA{seq}_{i}", [128, D], BF16)) for i in range(2)]
                    u_b = [Buf(), Buf()]
                    uT = [AE(nc.sbuf_tensor(f"uTA{seq}_{i}", [128, DC * 128], BF16)) for i in range(2)]
                    uT_b = [Buf(), Buf()]
                    ktmp = [AE(nc.sbuf_tensor(f"ktmpA{seq}_{i}", [128, D], BF16)) for i in range(2)]
                    ktmp_b = [Buf(), Buf()]
                    stA = [make_stats(actx, nc, f"stA{seq}_{i}") for i in range(2)]

                    P.dma("qg", wk[:].rearrange("p (c n) -> p c n", n=1024),
                          Wkd.rearrange("(c p) n -> p c n", p=128), writes=[wk_b])
                    P.dma("qg", wv[:].rearrange("p (c n) -> p c n", n=1024),
                          Wvd.rearrange("(c p) n -> p c n", p=128), writes=[wv_b])
                    P.dma("qg", bkr[:], bkd, writes=[bkr_b])
                    P.dma("qg", bvr[:], bvd, writes=[bvr_b])
                    P.dma("qs", xt[0][:], Xd[seq, 0:128, :], writes=[xt_b[0]])

                    def proj_emit(ps, uTap, W, brow):
                        def em(e, ps=ps, uTap=uTap, W=W, brow=brow):
                            for c in range(DC):
                                for n in range(2):
                                    e.matmul(ps[:, n * 512:(n + 1) * 512],
                                             uTap[:, c * 128:(c + 1) * 128],
                                             W[:, c * 1024 + n * 512: c * 1024 + n * 512 + 512],
                                             start=(c == 0), stop=False)
                            for n in range(2):
                                last = e.matmul(ps[:, n * 512:(n + 1) * 512], ones1[:],
                                                brow[:, n * 512:n * 512 + 512],
                                                start=False, stop=True)
                            return last
                        return em

                    def tr8_emit(psd, src, ident):
                        def em(e, psd=psd, src=src, ident=ident):
                            for c in range(DC):
                                last = e.transpose(psd[:, c * 128:(c + 1) * 128],
                                                   src[:, c * 128:(c + 1) * 128], ident)
                            return last
                        return em

                    emit_ln(P, stA[0], xt[0][:], [xt_b[0]], usb[0][:], u_b[0],
                            norm_eng="gpsimd")
                    for t in range(NT):
                        par = t % 2
                        if t + 1 < NT:
                            P.dma("qs", xt[1 - par][:], Xd[seq, (t + 1) * 128:(t + 2) * 128, :],
                                  writes=[xt_b[1 - par]])
                            emit_ln(P, stA[1 - par], xt[1 - par][:], [xt_b[1 - par]],
                                    usb[1 - par][:], u_b[1 - par], norm_eng="gpsimd")
                        P.op("tensor", tr8_emit(psT, usb[par][:], identb[:]),
                             reads=[u_b[par], identb_b], writes=[psT_b])
                        P.op("scalar", lambda e, o=uT[par][:], i=psT[:]: e.copy(out=o, in_=i),
                             reads=[psT_b], writes=[uT_b[par]])
                        P.dma("qc", uTs[seq, t], uT[par][:], reads=[uT_b[par]])
                        P.op("tensor", proj_emit(psA[:], uT[par][:], wk[:], bkr[:]),
                             reads=[uT_b[par], wk_b, ones1_b, bkr_b], writes=[psA_b])
                        P.op("scalar", lambda e, o=ktmp[par][:], i=psA[:]:
                             e.activation(out=o, in_=i, func=AF.Copy),
                             reads=[psA_b], writes=[ktmp_b[par]])
                        P.op("tensor", proj_emit(psB[:], uT[par][:], wv[:], bvr[:]),
                             reads=[uT_b[par], wv_b, ones1_b, bvr_b], writes=[psB_b])
                        P.op("tensor", tr8_emit(psT2, ktmp[par][:], identb[:]),
                             reads=[ktmp_b[par], identb_b], writes=[psT2_b])
                        P.op("scalar", lambda e,
                             o=KT[:].rearrange("p (c s) -> p c s", s=S)[:, :, t * 128:(t + 1) * 128],
                             i=psT2[:].rearrange("p (c q) -> p c q", q=128):
                             e.copy(out=o, in_=i),
                             reads=[psT2_b], writes=[kt_tb[t]])
                        P.op("scalar", lambda e, o=Vb[:, t * D:(t + 1) * D], i=psB[:]:
                             e.activation(out=o, in_=i, func=AF.Copy),
                             reads=[psB_b], writes=[v_tb[t]])

                    tap(P, f"KT{seq}", KT[:], kt_tb, [128, DC * S], BF16)
                    tap(P, f"V{seq}", Vb[:], v_tb, [128, NT * D], BF16)
                    P.barrier()
                    P.flush()

                if upto == "A":
                    P.barrier()
                    P.flush()
                    return nc, tap_outs

                with contextlib.ExitStack() as bctx:
                    BE = bctx.enter_context
                    wq = BE(nc.sbuf_tensor(f"wq{seq}", [128, DC * 1024], BF16)); wq_b = Buf()
                    bqr = BE(nc.sbuf_tensor(f"bqr{seq}", [1, D], BF16)); bqr_b = Buf()
                    maskt = BE(nc.sbuf_tensor(f"maskt{seq}", [128, 128], F32)); mask_b = Buf()
                    scores = BE(nc.sbuf_tensor(f"scores{seq}", [128, S], F32))
                    sc_b = [Buf() for _ in range(8)]
                    prs = [BE(nc.sbuf_tensor(f"pr{seq}_{i}", [128, 1024], BF16)) for i in range(2)]
                    pr_b = [Buf(), Buf()]
                    PT = [BE(nc.sbuf_tensor(f"PT{seq}_{i}", [128, 1024], BF16)) for i in range(2)]
                    PT_b = [Buf(), Buf()]
                    xt = [BE(nc.sbuf_tensor(f"xtB{seq}_{i}", [128, D], F32)) for i in range(3)]
                    xt_b = [Buf(), Buf(), Buf()]
                    uTt = [BE(nc.sbuf_tensor(f"uTt{seq}_{i}", [128, DC * 128], BF16)) for i in range(2)]
                    uTt_b = [Buf(), Buf()]
                    Qsb = [BE(nc.sbuf_tensor(f"Q{seq}_{i}", [128, D], BF16)) for i in range(2)]
                    Q_b = [Buf(), Buf()]
                    QT = [BE(nc.sbuf_tensor(f"QT{seq}_{i}", [128, DC * 128], BF16)) for i in range(2)]
                    QT_b = [Buf(), Buf()]
                    att = BE(nc.sbuf_tensor(f"att{seq}", [128, D], F32)); att_b = Buf()
                    sm = [BE(nc.sbuf_tensor(f"smB{seq}_{i}", [128, 16], F32)) for i in range(2)]
                    pmg_b = [[Buf() for _ in range(8)], [Buf() for _ in range(8)]]
                    rm_b = [Buf(), Buf()]; nb_b = [Buf(), Buf()]
                    part_b = [Buf(), Buf()]; rs_b = [Buf(), Buf()]; rinv_b = [Buf(), Buf()]
                    stB2 = [make_stats(bctx, nc, f"stB2{seq}_{i}") for i in range(2)]

                    P.dma("qg", wq[:].rearrange("p (c n) -> p c n", n=1024),
                          Wqd.rearrange("(c p) n -> p c n", p=128), writes=[wq_b])
                    P.dma("qg", bqr[:], bqd, writes=[bqr_b])
                    P.dma("qs", maskt[:], maskd, writes=[mask_b])

                    def frontLN(i):
                        P.dma("qs", xt[i % 3][:], Xd[seq, i * 128:(i + 1) * 128, :],
                              writes=[xt_b[i % 3]])
                        P.dma("qc", uTt[i % 2][:], uTs[seq, i], writes=[uTt_b[i % 2]])

                    def frontQ(i):
                        par = i % 2
                        P.op("tensor", proj_emit(psB[:], uTt[par][:], wq[:], bqr[:]),
                             reads=[uTt_b[par], wq_b, ones1_b, bqr_b], writes=[psB_b])
                        P.op("scalar", lambda e, o=Qsb[par][:], i=psB[:]:
                             e.activation(out=o, in_=i, func=AF.Copy),
                             reads=[psB_b], writes=[Q_b[par]])
                        P.op("tensor", tr8_emit(psT, Qsb[par][:], identb[:]),
                             reads=[Q_b[par], identb_b], writes=[psT_b])
                        P.op("scalar", lambda e, o=QT[par][:], i=psT[:]: e.copy(out=o, in_=i),
                             reads=[psT_b], writes=[QT_b[par]])

                    frontLN(0)
                    frontLN(1)
                    frontQ(0)
                    for i in range(NT):
                        par = i % 2
                        keys = (i + 1) * 128
                        ngrp = (keys + 511) // 512
                        for g in range(ngrp):
                            w = min(512, keys - g * 512)
                            gs = g % 2

                            def sc_emit(e, ps=psS[gs], qt=QT[par][:], g=g, w=w):
                                for c in range(DC):
                                    last = e.matmul(
                                        ps[:, 0:w], qt[:, c * 128:(c + 1) * 128],
                                        KT[:, c * S + g * 512: c * S + g * 512 + w],
                                        start=(c == 0), stop=(c == DC - 1))
                                return last
                            P.op("tensor", sc_emit,
                                 reads=[QT_b[par]] + kt_tb[g * 4:g * 4 + (w + 127) // 128],
                                 writes=[psS_b[gs]])
                            if g < ngrp - 1:
                                P.op("scalar", lambda e, o=scores[:, g * 512:g * 512 + w],
                                     i=psS[gs][:, 0:w]: e.activation(out=o, in_=i, func=AF.Copy),
                                     reads=[psS_b[gs]], writes=[sc_b[g]])
                            else:
                                lead = w - 128
                                if lead:
                                    P.op("scalar", lambda e, o=scores[:, g * 512:g * 512 + lead],
                                         i=psS[gs][:, 0:lead]: e.activation(out=o, in_=i, func=AF.Copy),
                                         reads=[psS_b[gs]], writes=[sc_b[g]])
                                P.op("vector", lambda e, o=scores[:, g * 512 + lead:g * 512 + w],
                                     i=psS[gs][:, lead:w], m=maskt[:]:
                                     e.tensor_tensor(out=o, in0=i, in1=m, op=ALU.add),
                                     reads=[psS_b[gs], mask_b], writes=[sc_b[g]])
                            P.op("vector", lambda e, o=sm[par][:, g:g + 1],
                                 i=scores[:, g * 512:g * 512 + w]:
                                 e.reduce_max(out=o, in_=i, axis=AX.X),
                                 reads=[sc_b[g]], writes=[pmg_b[par][g]])
                        P.op("vector", lambda e, o=sm[par][:, 8:9], i=sm[par][:, 0:ngrp]:
                             e.reduce_max(out=o, in_=i, axis=AX.X),
                             reads=pmg_b[par][0:ngrp], writes=[rm_b[par]])
                        P.op("vector", lambda e, o=sm[par][:, 9:10], i=sm[par][:, 8:9]:
                             e.tensor_scalar(out=o, in0=i, scalar1=-ISC, scalar2=None, op0=ALU.mult),
                             reads=[rm_b[par]], writes=[nb_b[par]])
                        nchunk = (keys + 1023) // 1024
                        for ch in range(nchunk):
                            w2 = min(1024, keys - ch * 1024)
                            ps_ = ch % 2
                            P.op("scalar", lambda e, o=prs[ps_][:, 0:w2],
                                 i=scores[:, ch * 1024:ch * 1024 + w2], b=sm[par][:, 9:10]:
                                 e.activation(out=o, in_=i, func=AF.Exp, bias=b, scale=ISC),
                                 reads=sc_b[2 * ch:2 * ch + (w2 + 511) // 512] + [nb_b[par]],
                                 writes=[pr_b[ps_]])
                            P.op("vector", lambda e, o=sm[par][:, 10 + ch:11 + ch], i=prs[ps_][:, 0:w2]:
                                 e.reduce_sum(out=o, in_=i, axis=AX.X),
                                 reads=[pr_b[ps_]], writes=[part_b[par]])
                            if ch == 0:
                                if i + 2 < NT:
                                    frontLN(i + 2)
                                if i + 1 < NT:
                                    frontQ(i + 1)
                            nkt = w2 // 128

                            def ptr_emit(e, psd=psT2, src=prs[ps_][:], nkt=nkt):
                                for c in range(nkt):
                                    last = e.transpose(psd[:, c * 128:(c + 1) * 128],
                                                       src[:, c * 128:(c + 1) * 128], identb[:])
                                return last
                            P.op("tensor", ptr_emit, reads=[pr_b[ps_], identb_b], writes=[psT2_b])
                            P.op("scalar", lambda e, o=PT[ps_][:, 0:nkt * 128], i=psT2[:, 0:nkt * 128]:
                                 e.copy(out=o, in_=i),
                                 reads=[psT2_b], writes=[PT_b[ps_]])

                            def pv_emit(e, pt=PT[ps_][:], ch=ch, nkt=nkt, i=i):
                                for jj in range(nkt):
                                    j = ch * 8 + jj
                                    for n in range(2):
                                        last = e.matmul(
                                            psA[:, n * 512:(n + 1) * 512],
                                            pt[:, jj * 128:(jj + 1) * 128],
                                            Vb[:, j * D + n * 512: j * D + n * 512 + 512],
                                            start=(j == 0), stop=(j == i))
                                return last
                            P.op("tensor", pv_emit,
                                 reads=[PT_b[ps_]] + v_tb[ch * 8:ch * 8 + nkt],
                                 writes=[psA_b])
                        P.op("vector", lambda e, o=sm[par][:, 14:15], i=sm[par][:, 10:10 + nchunk]:
                             e.reduce_sum(out=o, in_=i, axis=AX.X),
                             reads=[part_b[par]], writes=[rs_b[par]])
                        P.op("vector", lambda e, o=sm[par][:, 15:16], i=sm[par][:, 14:15]:
                             e.reciprocal(out=o, in_=i),
                             reads=[rs_b[par]], writes=[rinv_b[par]])
                        P.op("scalar", lambda e, o=att[:], i=psA[:], r=sm[par][:, 15:16]:
                             e.activation(out=o, in_=i, func=AF.Copy, scale=r),
                             reads=[psA_b, rinv_b[par]], writes=[att_b])
                        emit_ln(P, stB2[par], att[:], [att_b], att[:], att_b,
                                norm_eng="gpsimd")
                        P.op("gpsimd", lambda e, o=att[:], i=xt[i % 3][:]:
                             e.tensor_tensor(out=o, in0=o, in1=i, op=ALU.add),
                             reads=[xt_b[i % 3]], writes=[att_b])
                        P.dma("qs", x1s[seq, i * 128:(i + 1) * 128, :], att[:], reads=[att_b])

                    tap(P, f"x1_{seq}", x1s[seq], [], [S, D], F32)
                    P.barrier()
                    P.flush()

            if upto == "B":
                P.barrier()
                P.flush()
                return nc, tap_outs

            with contextlib.ExitStack() as cctx:
                CE = cctx.enter_context
                acc = CE(nc.sbuf_tensor(f"acc{seq}", [128, CHT * D], F32))
                acc_tb = [Buf() for _ in range(CHT)]
                u2T = CE(nc.sbuf_tensor(f"u2T{seq}", [128, CHT * DC * 128], BF16))
                u2T_tb = [Buf() for _ in range(CHT)]
                wr = [CE(nc.sbuf_tensor(f"wr{seq}_{i}", [128, DC * 1024], BF16)) for i in range(2)]
                wr_b = [Buf(), Buf()]
                x1td = [CE(nc.sbuf_tensor(f"x1td{seq}_{i}", [128, D], F32)) for i in range(2)]
                x1_b = [Buf(), Buf()]
                u2t = [CE(nc.sbuf_tensor(f"u2t{seq}_{i}", [128, D], BF16)) for i in range(2)]
                u2_b = [Buf(), Buf()]
                yt = [CE(nc.sbuf_tensor(f"yt{seq}_{i}", [128, D], F32)) for i in range(2)]
                yt_b = [Buf(), Buf()]
                g_ch = CE(nc.sbuf_tensor(f"g_ch{seq}", [128, CHT * NE], F32))
                g_tb = [Buf() for _ in range(CHT)]
                gT_ch = CE(nc.sbuf_tensor(f"gT_ch{seq}", [16, CHT * 128], BF16))
                gT_tb = [Buf() for _ in range(CHT // 4)]
                rl_ch = CE(nc.sbuf_tensor(f"rl_ch{seq}", [128, CHT * NE], F32))
                rl_tb = [Buf() for _ in range(CHT // 4)]
                sdash_ch = CE(nc.sbuf_tensor(f"sdash_ch{seq}", [128, CHT * NE], F32))
                sdc_tb = [Buf() for _ in range(CHT)]
                gtmp = [CE(nc.sbuf_tensor(f"gtmp{seq}_{i}", [128, NE], F32)) for i in range(2)]
                gtm_b = [Buf(), Buf()]
                gsm = [CE(nc.sbuf_tensor(f"gsm{seq}_{i}", [128, 6], F32)) for i in range(2)]
                nbr_b = [Buf(), Buf()]; zr_b = [Buf(), Buf()]; rz_b = [Buf(), Buf()]
                m1_b = [Buf(), Buf()]; m2_b = [Buf(), Buf()]
                CTs = CE(nc.sbuf_tensor(f"CTs{seq}", [128, DC * NE], BF16)); CT_b = Buf()
                brs = CE(nc.sbuf_tensor(f"brs{seq}", [16, D], BF16)); br_b = Buf()
                ebr = CE(nc.sbuf_tensor(f"ebr{seq}", [1, NE], BF16)); eb_b = Buf()
                bs0r = CE(nc.sbuf_tensor(f"bs0r{seq}", [1, D], BF16)); bs0_b = Buf()
                bs1r = CE(nc.sbuf_tensor(f"bs1r{seq}", [1, D], BF16)); bs1_b = Buf()
                identf = CE(nc.sbuf_tensor(f"identf_sb{seq}", [128, 128], F32)); identf_b = Buf()
                Wsh = CE(nc.sbuf_tensor(f"Wsh{seq}", [128, DC * 1024], BF16)); Wsh_b = Buf()
                wtmp = [CE(nc.sbuf_tensor(f"wtmp{seq}_{i}", [128, 1024], F32)) for i in range(2)]
                wtmp_b = [Buf(), Buf()]
                stC = [make_stats(cctx, nc, f"stC{seq}_{i}") for i in range(2)]

                P.dma("qg", CTs[:].rearrange("p (c e) -> p c e", e=NE),
                      CTd.rearrange("(c p) e -> p c e", p=128), writes=[CT_b])
                P.dma("qg", brs[:], brd, writes=[br_b])
                P.dma("qg", ebr[:], ebd, writes=[eb_b])
                P.dma("qg", bs0r[:], bsd[0:1, :], writes=[bs0_b])
                P.dma("qg", bs1r[:], bsd[1:2, :], writes=[bs1_b])
                P.dma("qs", identf[:], identfd, writes=[identf_b])
                for c in range(DC):
                    P.dma("qs", wtmp[0][:], WsTd[0, c * 128:(c + 1) * 128, :], writes=[wtmp_b[0]])
                    P.dma("qc", wtmp[1][:], WsTd[1, c * 128:(c + 1) * 128, :], writes=[wtmp_b[1]])
                    P.op("vector", lambda e, a=wtmp[0][:], b2=wtmp[1][:]:
                         e.tensor_tensor(out=a, in0=a, in1=b2, op=ALU.add),
                         reads=[wtmp_b[1]], writes=[wtmp_b[0]])
                    P.op("vector", lambda e, o=Wsh[:, c * 1024:(c + 1) * 1024], i=wtmp[0][:]:
                         e.tensor_scalar(out=o, in0=i, scalar1=0.5, scalar2=None, op0=ALU.mult),
                         reads=[wtmp_b[0]], writes=[Wsh_b])

                def emit_final(t, ch):
                    par = t % 2
                    gt = ch * CHT + t
                    P.dma("qs", x1td[par][:], x1s[seq, gt * 128:(gt + 1) * 128, :],
                          writes=[x1_b[par]])
                    st = stC[par]
                    tt = st["t"]
                    P.op("vector", lambda e, o=tt[:, 0:6], i=acc[:, t * D:t * D + 512]:
                         e.bn_stats(out=o, in_=i), reads=[acc_tb[t]], writes=[st["bn_a"]])
                    P.op("vector", lambda e, o=tt[:, 6:12], i=acc[:, t * D + 512:t * D + 1024]:
                         e.bn_stats(out=o, in_=i), reads=[acc_tb[t]], writes=[st["bn_b"]])
                    P.op("vector", lambda e, o=tt[:, 12:14], i=tt[:, 0:12]:
                         e.bn_aggr(out=o, in_=i),
                         reads=[st["bn_a"], st["bn_b"]], writes=[st["mv"]])
                    P.op("vector", lambda e, o=tt[:, 14:15], i=tt[:, 13:14]:
                         e.tensor_scalar(out=o, in0=i, scalar1=EPS, scalar2=None, op0=ALU.add),
                         reads=[st["mv"]], writes=[st["ve"]])
                    P.op("scalar", lambda e, o=tt[:, 15:16], i=tt[:, 14:15]:
                         e.activation(out=o, in_=i, func=AF.Ln),
                         reads=[st["ve"]], writes=[st["lnv"]])
                    P.op("scalar", lambda e, o=tt[:, 16:17], i=tt[:, 15:16]:
                         e.activation(out=o, in_=i, func=AF.Exp, scale=-0.5),
                         reads=[st["lnv"]], writes=[st["rstd"]])
                    P.op("gpsimd", lambda e, o=yt[par][:], i=acc[:, t * D:(t + 1) * D],
                         m=tt[:, 12:13], r=tt[:, 16:17]:
                         e.tensor_scalar(out=o, in0=i, scalar1=m, scalar2=r,
                                         op0=ALU.subtract, op1=ALU.mult),
                         reads=[acc_tb[t], st["mv"], st["rstd"]], writes=[yt_b[par]])
                    P.op("gpsimd", lambda e, o=yt[par][:], i=x1td[par][:]:
                         e.tensor_tensor(out=o, in0=o, in1=i, op=ALU.add),
                         reads=[x1_b[par]], writes=[yt_b[par]])
                    P.dma("qs", Yd[seq, gt * 128:(gt + 1) * 128, :], yt[par][:],
                          reads=[yt_b[par]])

                for ch in range(NCH):
                    base = ch * CHT
                    P.dma("qs", x1td[0][:], x1s[seq, base * 128:(base + 1) * 128, :],
                          writes=[x1_b[0]])
                    emit_ln(P, stC[0], x1td[0][:], [x1_b[0]], u2t[0][:], u2_b[0],
                            norm_eng="gpsimd")
                    for t in range(CHT):
                        par = t % 2
                        gt = base + t
                        if t + 1 < CHT:
                            P.dma("qs", x1td[1 - par][:],
                                  x1s[seq, (gt + 1) * 128:(gt + 2) * 128, :],
                                  writes=[x1_b[1 - par]])
                            emit_ln(P, stC[1 - par], x1td[1 - par][:], [x1_b[1 - par]],
                                    u2t[1 - par][:], u2_b[1 - par], norm_eng="gpsimd")
                        P.op("tensor", tr8_emit(psT, u2t[par][:], identb[:]),
                             reads=[u2_b[par], identb_b], writes=[psT_b])
                        P.op("scalar", lambda e, o=u2T[:, t * 1024:(t + 1) * 1024], i=psT[:]:
                             e.copy(out=o, in_=i),
                             reads=[psT_b], writes=[u2T_tb[t]])
                        P.op("scalar", lambda e, o=acc[:, t * D:(t + 1) * D], i=u2t[par][:]:
                             e.copy(out=o, in_=i),
                             reads=[u2_b[par]], writes=[acc_tb[t]])
                        if t % 4 == 3:
                            t0 = t - 3

                            def rt4_emit(e, t0=t0):
                                for tt_ in range(t0, t0 + 4):
                                    sl = (tt_ - t0) * NE
                                    for c in range(DC):
                                        e.matmul(psS0[:, sl:sl + NE],
                                                 u2T[:, tt_ * 1024 + c * 128: tt_ * 1024 + (c + 1) * 128],
                                                 CTs[:, c * NE:(c + 1) * NE],
                                                 start=(c == 0), stop=False)
                                    last = e.matmul(psS0[:, sl:sl + NE], ones1[:], ebr[:],
                                                    start=False, stop=True)
                                return last
                            P.op("tensor", rt4_emit,
                                 reads=[u2T_tb[t0], u2T_tb[t0 + 1], u2T_tb[t0 + 2],
                                        u2T_tb[t0 + 3], CT_b, ones1_b, eb_b],
                                 writes=[psS0_b])
                            P.op("scalar", lambda e, o=rl_ch[:, t0 * NE:(t0 + 4) * NE],
                                 i=psS0[:, 0:4 * NE]: e.copy(out=o, in_=i),
                                 reads=[psS0_b], writes=[rl_tb[t0 // 4]])
                            for tt_ in range(t0, t0 + 4):
                                pr2 = tt_ % 2
                                P.op("vector", lambda e, o=gsm[pr2][:, 0:1],
                                     i=rl_ch[:, tt_ * NE:(tt_ + 1) * NE]:
                                     e.reduce_max(out=o, in_=i, axis=AX.X, negate=True),
                                     reads=[rl_tb[t0 // 4]], writes=[nbr_b[pr2]])
                                P.op("scalar", lambda e, o=sdash_ch[:, tt_ * NE:(tt_ + 1) * NE],
                                     i=rl_ch[:, tt_ * NE:(tt_ + 1) * NE], b=gsm[pr2][:, 0:1]:
                                     e.activation(out=o, in_=i, func=AF.Exp, bias=b),
                                     reads=[rl_tb[t0 // 4], nbr_b[pr2]], writes=[sdc_tb[tt_]])
                    for t in range(CHT):
                        par = t % 2
                        sd_ap = sdash_ch[:, t * NE:(t + 1) * NE]
                        P.op("vector", lambda e, o=gsm[par][:, 1:2], i=sd_ap:
                             e.reduce_sum(out=o, in_=i, axis=AX.X),
                             reads=[sdc_tb[t]], writes=[zr_b[par]])
                        P.op("vector", lambda e, o=gsm[par][:, 2:3], i=gsm[par][:, 1:2]:
                             e.reciprocal(out=o, in_=i),
                             reads=[zr_b[par]], writes=[rz_b[par]])
                        P.op("vector", lambda e, o=gsm[par][:, 3:4], i=sd_ap:
                             e.reduce_max(out=o, in_=i, axis=AX.X),
                             reads=[sdc_tb[t]], writes=[m1_b[par]])
                        P.op("vector", lambda e, o=gtmp[par][:], i=sd_ap, m=gsm[par][:, 3:4]:
                             e.tensor_scalar(out=o, in0=i, scalar1=m, scalar2=-2.0,
                                             op0=ALU.is_ge, op1=ALU.mult),
                             reads=[sdc_tb[t], m1_b[par]], writes=[gtm_b[par]])
                        P.op("vector", lambda e, o=gtmp[par][:], i=sd_ap:
                             e.tensor_tensor(out=o, in0=i, in1=o, op=ALU.add),
                             reads=[sdc_tb[t]], writes=[gtm_b[par]])
                        P.op("vector", lambda e, o=gsm[par][:, 4:5], i=gtmp[par][:]:
                             e.reduce_max(out=o, in_=i, axis=AX.X),
                             reads=[gtm_b[par]], writes=[m2_b[par]])
                        P.op("vector", lambda e, o=gtmp[par][:], i=sd_ap, m=gsm[par][:, 4:5]:
                             e.tensor_scalar(out=o, in0=i, scalar1=m, scalar2=None, op0=ALU.is_ge),
                             reads=[sdc_tb[t], m2_b[par]], writes=[gtm_b[par]])
                        P.op("vector", lambda e, o=g_ch[:, t * NE:(t + 1) * NE], i=sd_ap,
                             r=gsm[par][:, 2:3], m=gtmp[par][:]:
                             e.scalar_tensor_tensor(out=o, in0=i, scalar=r, in1=m,
                                                    op0=ALU.mult, op1=ALU.mult),
                             reads=[sdc_tb[t], rz_b[par], gtm_b[par]], writes=[g_tb[t]])
                        if t % 4 == 3:
                            t0 = t - 3

                            def gt4_emit(e, t0=t0):
                                for tt_ in range(t0, t0 + 4):
                                    last = e.transpose(
                                        psS1[0:16, (tt_ - t0) * 128:(tt_ - t0 + 1) * 128],
                                        g_ch[:, tt_ * NE:(tt_ + 1) * NE], identf[:])
                                return last
                            P.op("tensor", gt4_emit,
                                 reads=[g_tb[t0], g_tb[t0 + 1], g_tb[t0 + 2], g_tb[t0 + 3],
                                        identf_b],
                                 writes=[psS1_b])
                            P.op("vector", lambda e, o=gT_ch[0:16, t0 * 128:(t0 + 4) * 128],
                                 i=psS1[0:16, 0:512]: e.tensor_copy(out=o, in_=i),
                                 reads=[psS1_b], writes=[gT_tb[t0 // 4]])
                    P.dma("qg", wr[0][:].rearrange("p (c n) -> p c n", n=1024),
                          WrTd[0].rearrange("(c p) n -> p c n", p=128), writes=[wr_b[0]])
                    P.dma("qg", wr[1][:].rearrange("p (c n) -> p c n", n=1024),
                          WrTd[1].rearrange("(c p) n -> p c n", p=128), writes=[wr_b[1]])
                    for t in range(CHT):
                        psx, psx_b = (psA, psA_b) if t % 2 == 0 else (psB, psB_b)

                        def ai_emit(e, ps=psx[:], t=t):
                            for c in range(DC):
                                for n in range(2):
                                    e.matmul(ps[:, n * 512:(n + 1) * 512],
                                             u2T[:, t * 1024 + c * 128: t * 1024 + (c + 1) * 128],
                                             Wsh[:, c * 1024 + n * 512: c * 1024 + n * 512 + 512],
                                             start=(c == 0), stop=False)
                            for n in range(2):
                                e.matmul(ps[:, n * 512:(n + 1) * 512],
                                         gT_ch[0:16, t * 128:(t + 1) * 128],
                                         brs[:, n * 512:n * 512 + 512], start=False, stop=False)
                                e.matmul(ps[:, n * 512:(n + 1) * 512], halves[:],
                                         bs0r[:, n * 512:n * 512 + 512], start=False, stop=False)
                                last = e.matmul(ps[:, n * 512:(n + 1) * 512], halves[:],
                                                bs1r[:, n * 512:n * 512 + 512],
                                                start=False, stop=True)
                            return last
                        P.op("tensor", ai_emit,
                             reads=[u2T_tb[t], Wsh_b, gT_tb[t // 4], br_b, halves_b,
                                    bs0_b, bs1_b],
                             writes=[psx_b])
                        P.op("vector", lambda e, o=acc[:, t * D:(t + 1) * D], i=psx[:]:
                             e.tensor_tensor(out=o, in0=i, in1=o, op=ALU.add),
                             reads=[psx_b], writes=[acc_tb[t]])
                    for e_i in range(NE):
                        sl = e_i % 2
                        for t in range(CHT):
                            psx, psx_b = (psA, psA_b) if t % 2 == 0 else (psB, psB_b)

                            def ex_emit(e, ps=psx[:], t=t, sl=sl):
                                for c in range(DC):
                                    for n in range(2):
                                        last = e.matmul(
                                            ps[:, n * 512:(n + 1) * 512],
                                            u2T[:, t * 1024 + c * 128: t * 1024 + (c + 1) * 128],
                                            wr[sl][:, c * 1024 + n * 512: c * 1024 + n * 512 + 512],
                                            start=(c == 0), stop=(c == DC - 1))
                                return last
                            P.op("tensor", ex_emit,
                                 reads=[u2T_tb[t], wr_b[sl]], writes=[psx_b])
                            P.op("vector", lambda e, o=acc[:, t * D:(t + 1) * D], i=psx[:],
                                 gsc=g_ch[:, t * NE + e_i: t * NE + e_i + 1]:
                                 e.scalar_tensor_tensor(out=o, in0=i, scalar=gsc, in1=o,
                                                        op0=ALU.mult, op1=ALU.add),
                                 reads=[psx_b, g_tb[t]], writes=[acc_tb[t]])
                            if e_i == NE - 1:
                                emit_final(t, ch)
                        if e_i + 2 < NE:
                            P.dma("qg", wr[sl][:].rearrange("p (c n) -> p c n", n=1024),
                                  WrTd[e_i + 2].rearrange("(c p) n -> p c n", p=128),
                                  writes=[wr_b[sl]])

                    if f"acc{seq}_{ch}" in taps:
                        tap(P, f"acc{seq}_{ch}", acc[:], acc_tb, [128, CHT * D], F32)
                    if f"g{seq}_{ch}" in taps:
                        tap(P, f"g{seq}_{ch}", g_ch[:], g_tb, [128, CHT * NE], F32)
                    P.flush()

        P.barrier()
        P.flush()
    return nc, tap_outs


def _host_inputs(inputs):
    f32 = np.float32
    bf16 = ml_dtypes.bfloat16
    X = np.ascontiguousarray(np.asarray(inputs["L0_attn_X"], f32))
    shared = {
        "Wq": np.ascontiguousarray(np.asarray(inputs["L0_attn_Wq"], f32)),
        "Wk": np.ascontiguousarray(np.asarray(inputs["L0_attn_Wk"], f32)),
        "Wv": np.ascontiguousarray(np.asarray(inputs["L0_attn_Wv"], f32)),
        "bq": np.ascontiguousarray(np.asarray(inputs["L0_attn_bq"], f32).reshape(1, D)),
        "bk": np.ascontiguousarray(np.asarray(inputs["L0_attn_bk"], f32).reshape(1, D)),
        "bv": np.ascontiguousarray(np.asarray(inputs["L0_attn_bv"], f32).reshape(1, D)),
        "CT": np.ascontiguousarray(np.asarray(inputs["L0_ffn_centroids"], f32).T),
        "eb": np.ascontiguousarray(np.asarray(inputs["L0_ffn_expert_biases"], f32).reshape(1, NE)),
        "WrT": np.ascontiguousarray(np.asarray(inputs["L0_ffn_Wr"], f32).transpose(0, 2, 1)),
        "br": np.ascontiguousarray(np.asarray(inputs["L0_ffn_br"], f32)),
        "WsT": np.ascontiguousarray(np.asarray(inputs["L0_ffn_Ws"], f32).transpose(0, 2, 1)),
        "bs": np.ascontiguousarray(np.asarray(inputs["L0_ffn_bs"], f32)),
        "identb": np.eye(128, dtype=bf16),
        "identf": np.eye(128, dtype=f32),
        "mask": np.triu(np.full((128, 128), -1.0e30, f32), 1),
        "ones1": np.ones((1, 128), dtype=bf16),
        "halves": np.full((1, 128), 0.5, dtype=bf16),
    }
    in_maps = []
    for c in range(NCORES):
        m = dict(shared)
        m["X"] = np.ascontiguousarray(X[c * SPC:(c + 1) * SPC])
        in_maps.append(m)
    return in_maps


def kernel(**inputs):
    in_maps = _host_inputs(inputs)
    nc, _ = build()
    res = run_bass_kernel_spmd(nc, in_maps, core_ids=list(range(NCORES)))
    out = np.concatenate([np.asarray(res.results[c]["Y"]) for c in range(NCORES)], axis=0)
    return out.astype(np.float32, copy=False)


# version 04
# speedup vs baseline: 1.4381x; 1.4381x over previous
import contextlib
import numpy as np
import ml_dtypes
import concourse.bass as bass
import concourse.mybir as mybir
from concourse.bass_utils import run_bass_kernel_spmd

F32 = mybir.dt.float32
BF16 = mybir.dt.bfloat16
AF = mybir.ActivationFunctionType
ALU = mybir.AluOpType
AX = mybir.AxisListType

B, S, D = 16, 4096, 1024
NCORES = 8
SPC = B // NCORES
NT = S // 128
DC = D // 128
NE = 16
CHT = 16
NCH = NT // CHT
EPS = 1e-5
ISC = 1.0 / 32.0

ENG2SEM = {"tensor": "t", "vector": "v", "scalar": "s", "gpsimd": "g"}
QENG = {"qs": "sync", "qc": "scalar", "qg": "gpsimd"}


class Buf:
    __slots__ = ("w", "rs")

    def __init__(self):
        self.w = None
        self.rs = []


class Prog:
    def __init__(self, nc, ctx):
        self.nc = nc
        self.pend = {e: [] for e in ("sync", "tensor", "vector", "scalar", "gpsimd")}
        self.sems = {}
        self.cnt = {}
        self.hw = {e: {} for e in self.pend}
        for sn in ("t", "v", "s", "g", "qs", "qc", "qg"):
            self.sems[sn] = ctx.enter_context(nc.semaphore("sem_" + sn))
            self.cnt[sn] = 0
        self.nops = 0

    def _prune(self, eng, evs):
        best = {}
        for ev in evs:
            if ev is None:
                continue
            sn, val = ev
            if eng == "tensor" and sn == "t":
                continue
            if best.get(sn, 0) < val:
                best[sn] = val
        hw = self.hw[eng]
        out = []
        for sn, val in best.items():
            if hw.get(sn, 0) >= val:
                continue
            hw[sn] = val
            out.append((sn, val))
        return out

    def _gather(self, reads, writes, extra):
        evs = list(extra)
        for b in reads:
            evs.append(b.w)
        for b in writes:
            evs.append(b.w)
            evs.extend(b.rs)
        return evs

    def op(self, eng, emit, reads=(), writes=(), extra=()):
        waits = self._prune(eng, self._gather(reads, writes, extra))
        sn = ENG2SEM[eng]
        self.cnt[sn] += 1
        ev = (sn, self.cnt[sn])
        sems = self.sems

        def run(e, waits=waits, emit=emit, sem=sems[sn], sems=sems):
            for wn, wv in waits:
                e.wait_ge(sems[wn], wv)
            emit(e).then_inc(sem, 1)

        self.pend[eng].append(run)
        self.nops += 1
        for b in reads:
            b.rs.append(ev)
        for b in writes:
            b.w = ev
            b.rs = []
        return ev

    def dma(self, q, out, in_, reads=(), writes=(), extra=()):
        eng = QENG[q]
        waits = self._prune(eng, self._gather(reads, writes, extra))
        self.cnt[q] += 16
        ev = (q, self.cnt[q])
        sems = self.sems

        def run(e, waits=waits, out=out, in_=in_, sem=sems[q], sems=sems):
            for wn, wv in waits:
                e.wait_ge(sems[wn], wv)
            e.dma_start(out=out, in_=in_).then_inc(sem, 16)

        self.pend[eng].append(run)
        self.nops += 1
        for b in reads:
            b.rs.append(ev)
        for b in writes:
            b.w = ev
            b.rs = []
        return ev

    def barrier(self):
        cur = [(sn, v) for sn, v in self.cnt.items() if v > 0]
        sems = self.sems
        for eng in self.pend:
            waits = self._prune(eng, cur)
            if waits:
                def run(e, waits=waits, sems=sems):
                    for wn, wv in waits:
                        e.wait_ge(sems[wn], wv)
                self.pend[eng].append(run)

    def flush(self):
        pend = {e: lst for e, lst in self.pend.items() if lst}
        self.pend = {e: [] for e in self.pend}
        if not pend:
            return
        with self.nc.Block() as blk:
            for eng, lst in pend.items():
                getattr(blk, eng)(lambda e, lst=lst: [f(e) for f in lst])


def make_stats(ctx, nc, name):
    t = ctx.enter_context(nc.sbuf_tensor(name, [128, 18], F32))
    return dict(t=t, bn_a=Buf(), bn_b=Buf(), mv=Buf(), ve=Buf(), lnv=Buf(), rstd=Buf())


def emit_ln(P, st, src, src_bufs, dst, dst_buf, norm_eng="vector"):
    t = st["t"]
    P.op("vector", lambda e, o=t[:, 0:6], i=src[:, 0:512]: e.bn_stats(out=o, in_=i),
         reads=src_bufs, writes=[st["bn_a"]])
    P.op("vector", lambda e, o=t[:, 6:12], i=src[:, 512:1024]: e.bn_stats(out=o, in_=i),
         reads=src_bufs, writes=[st["bn_b"]])
    P.op("vector", lambda e, o=t[:, 12:14], i=t[:, 0:12]: e.bn_aggr(out=o, in_=i),
         reads=[st["bn_a"], st["bn_b"]], writes=[st["mv"]])
    P.op("vector", lambda e, o=t[:, 14:15], i=t[:, 13:14]:
         e.tensor_scalar(out=o, in0=i, scalar1=EPS, scalar2=None, op0=ALU.add),
         reads=[st["mv"]], writes=[st["ve"]])
    P.op("scalar", lambda e, o=t[:, 15:16], i=t[:, 14:15]:
         e.activation(out=o, in_=i, func=AF.Ln),
         reads=[st["ve"]], writes=[st["lnv"]])
    P.op("scalar", lambda e, o=t[:, 16:17], i=t[:, 15:16]:
         e.activation(out=o, in_=i, func=AF.Exp, scale=-0.5),
         reads=[st["lnv"]], writes=[st["rstd"]])
    P.op(norm_eng, lambda e, o=dst, i=src, m=t[:, 12:13], r=t[:, 16:17]:
         e.tensor_scalar(out=o, in0=i, scalar1=m, scalar2=r,
                         op0=ALU.subtract, op1=ALU.mult),
         reads=list(src_bufs) + [st["mv"], st["rstd"]], writes=[dst_buf])


def build(upto="full", taps=()):
    taps = set(taps)
    nc = bass.Bass("TRN2", target_bir_lowering=False)

    Xd = nc.dram_tensor("X", [SPC, S, D], F32, kind="ExternalInput").ap()
    Yd = nc.dram_tensor("Y", [SPC, S, D], F32, kind="ExternalOutput").ap()
    x1s = nc.dram_tensor("x1scr", [SPC, S, D], F32, kind="Internal").ap()
    uTs = nc.dram_tensor("uTscr", [SPC, NT, 128, DC * 128], BF16, kind="Internal").ap()
    Wqd = nc.dram_tensor("Wq", [D, D], F32, kind="ExternalInput").ap()
    Wkd = nc.dram_tensor("Wk", [D, D], F32, kind="ExternalInput").ap()
    Wvd = nc.dram_tensor("Wv", [D, D], F32, kind="ExternalInput").ap()
    bqd = nc.dram_tensor("bq", [1, D], F32, kind="ExternalInput").ap()
    bkd = nc.dram_tensor("bk", [1, D], F32, kind="ExternalInput").ap()
    bvd = nc.dram_tensor("bv", [1, D], F32, kind="ExternalInput").ap()
    CTd = nc.dram_tensor("CT", [D, NE], F32, kind="ExternalInput").ap()
    ebd = nc.dram_tensor("eb", [1, NE], F32, kind="ExternalInput").ap()
    WrTd = nc.dram_tensor("WrT", [NE, D, D], F32, kind="ExternalInput").ap()
    brd = nc.dram_tensor("br", [NE, D], F32, kind="ExternalInput").ap()
    WsTd = nc.dram_tensor("WsT", [2, D, D], F32, kind="ExternalInput").ap()
    bsd = nc.dram_tensor("bs", [2, D], F32, kind="ExternalInput").ap()
    identbd = nc.dram_tensor("identb", [128, 128], BF16, kind="ExternalInput").ap()
    identfd = nc.dram_tensor("identf", [128, 128], F32, kind="ExternalInput").ap()
    maskd = nc.dram_tensor("mask", [128, 128], F32, kind="ExternalInput").ap()
    ones1d = nc.dram_tensor("ones1", [1, 128], BF16, kind="ExternalInput").ap()
    halvesd = nc.dram_tensor("halves", [1, 128], BF16, kind="ExternalInput").ap()

    tap_outs = {}

    def tap(P, name, ap, readbufs, shape, dtype):
        if name in taps:
            t = nc.dram_tensor("tap_" + name, list(shape), dtype, kind="ExternalOutput").ap()
            tap_outs[name] = t
            P.dma("qs", t, ap, reads=list(readbufs))

    with contextlib.ExitStack() as octx:
        P = Prog(nc, octx)
        OE = octx.enter_context

        identb = OE(nc.sbuf_tensor("identb_sb", [128, 128], BF16)); identb_b = Buf()
        ones1 = OE(nc.sbuf_tensor("ones1_sb", [1, 128], BF16)); ones1_b = Buf()
        halves = OE(nc.sbuf_tensor("halves_sb", [1, 128], BF16)); halves_b = Buf()

        psA = OE(nc.psum_tensor("psA", [128, 1024], F32)); psA_b = Buf()
        psB = OE(nc.psum_tensor("psB", [128, 1024], F32)); psB_b = Buf()
        psT = OE(nc.psum_tensor("psT", [128, 1024], BF16)); psT_b = Buf()
        psT2 = OE(nc.psum_tensor("psT2", [128, 1024], BF16)); psT2_b = Buf()
        psS0 = OE(nc.psum_tensor("psS0", [128, 512], F32)); psS0_b = Buf()
        psS1 = OE(nc.psum_tensor("psS1", [128, 512], F32)); psS1_b = Buf()
        psS = [psS0, psS1]; psS_b = [psS0_b, psS1_b]

        P.dma("qs", identb[:], identbd, writes=[identb_b])
        P.dma("qs", ones1[:], ones1d, writes=[ones1_b])
        P.dma("qs", halves[:], halvesd, writes=[halves_b])

        for seq in range(SPC):
            with contextlib.ExitStack() as sctx:
                SE = sctx.enter_context
                KT = SE(nc.sbuf_tensor(f"KT{seq}", [128, DC * S], BF16))
                Vb = SE(nc.sbuf_tensor(f"V{seq}", [128, NT * D], BF16))
                kt_tb = [Buf() for _ in range(NT)]
                v_tb = [Buf() for _ in range(NT)]

                with contextlib.ExitStack() as actx:
                    AE = actx.enter_context
                    wk = AE(nc.sbuf_tensor(f"wk{seq}", [128, DC * 1024], BF16)); wk_b = Buf()
                    wv = AE(nc.sbuf_tensor(f"wv{seq}", [128, DC * 1024], BF16)); wv_b = Buf()
                    bkr = AE(nc.sbuf_tensor(f"bkr{seq}", [1, D], BF16)); bkr_b = Buf()
                    bvr = AE(nc.sbuf_tensor(f"bvr{seq}", [1, D], BF16)); bvr_b = Buf()
                    xt = [AE(nc.sbuf_tensor(f"xtA{seq}_{i}", [128, D], F32)) for i in range(2)]
                    xt_b = [Buf(), Buf()]
                    usb = [AE(nc.sbuf_tensor(f"uA{seq}_{i}", [128, D], BF16)) for i in range(2)]
                    u_b = [Buf(), Buf()]
                    uT = [AE(nc.sbuf_tensor(f"uTA{seq}_{i}", [128, DC * 128], BF16)) for i in range(2)]
                    uT_b = [Buf(), Buf()]
                    ktmp = [AE(nc.sbuf_tensor(f"ktmpA{seq}_{i}", [128, D], BF16)) for i in range(2)]
                    ktmp_b = [Buf(), Buf()]
                    stA = [make_stats(actx, nc, f"stA{seq}_{i}") for i in range(2)]

                    P.dma("qg", wk[:].rearrange("p (c n) -> p c n", n=1024),
                          Wkd.rearrange("(c p) n -> p c n", p=128), writes=[wk_b])
                    P.dma("qg", wv[:].rearrange("p (c n) -> p c n", n=1024),
                          Wvd.rearrange("(c p) n -> p c n", p=128), writes=[wv_b])
                    P.dma("qg", bkr[:], bkd, writes=[bkr_b])
                    P.dma("qg", bvr[:], bvd, writes=[bvr_b])
                    P.dma("qs", xt[0][:], Xd[seq, 0:128, :], writes=[xt_b[0]])

                    def proj_emit(ps, uTap, W, brow):
                        def em(e, ps=ps, uTap=uTap, W=W, brow=brow):
                            for c in range(DC):
                                for n in range(2):
                                    e.matmul(ps[:, n * 512:(n + 1) * 512],
                                             uTap[:, c * 128:(c + 1) * 128],
                                             W[:, c * 1024 + n * 512: c * 1024 + n * 512 + 512],
                                             start=(c == 0), stop=False)
                            for n in range(2):
                                last = e.matmul(ps[:, n * 512:(n + 1) * 512], ones1[:],
                                                brow[:, n * 512:n * 512 + 512],
                                                start=False, stop=True)
                            return last
                        return em

                    def tr8_emit(psd, src, ident):
                        def em(e, psd=psd, src=src, ident=ident):
                            for c in range(DC):
                                last = e.transpose(psd[:, c * 128:(c + 1) * 128],
                                                   src[:, c * 128:(c + 1) * 128], ident)
                            return last
                        return em

                    emit_ln(P, stA[0], xt[0][:], [xt_b[0]], usb[0][:], u_b[0])
                    for t in range(NT):
                        par = t % 2
                        if t + 1 < NT:
                            P.dma("qs", xt[1 - par][:], Xd[seq, (t + 1) * 128:(t + 2) * 128, :],
                                  writes=[xt_b[1 - par]])
                            emit_ln(P, stA[1 - par], xt[1 - par][:], [xt_b[1 - par]],
                                    usb[1 - par][:], u_b[1 - par])
                        P.op("tensor", tr8_emit(psT, usb[par][:], identb[:]),
                             reads=[u_b[par], identb_b], writes=[psT_b])
                        P.op("scalar", lambda e, o=uT[par][:], i=psT[:]: e.copy(out=o, in_=i),
                             reads=[psT_b], writes=[uT_b[par]])
                        P.dma("qc", uTs[seq, t], uT[par][:], reads=[uT_b[par]])
                        P.op("tensor", proj_emit(psA[:], uT[par][:], wk[:], bkr[:]),
                             reads=[uT_b[par], wk_b, ones1_b, bkr_b], writes=[psA_b])
                        P.op("scalar", lambda e, o=ktmp[par][:], i=psA[:]:
                             e.activation(out=o, in_=i, func=AF.Copy),
                             reads=[psA_b], writes=[ktmp_b[par]])
                        P.op("tensor", proj_emit(psB[:], uT[par][:], wv[:], bvr[:]),
                             reads=[uT_b[par], wv_b, ones1_b, bvr_b], writes=[psB_b])
                        P.op("tensor", tr8_emit(psT2, ktmp[par][:], identb[:]),
                             reads=[ktmp_b[par], identb_b], writes=[psT2_b])
                        P.op("scalar", lambda e,
                             o=KT[:].rearrange("p (c s) -> p c s", s=S)[:, :, t * 128:(t + 1) * 128],
                             i=psT2[:].rearrange("p (c q) -> p c q", q=128):
                             e.copy(out=o, in_=i),
                             reads=[psT2_b], writes=[kt_tb[t]])
                        P.op("scalar", lambda e, o=Vb[:, t * D:(t + 1) * D], i=psB[:]:
                             e.activation(out=o, in_=i, func=AF.Copy),
                             reads=[psB_b], writes=[v_tb[t]])

                    tap(P, f"KT{seq}", KT[:], kt_tb, [128, DC * S], BF16)
                    tap(P, f"V{seq}", Vb[:], v_tb, [128, NT * D], BF16)
                    P.barrier()
                    P.flush()

                if upto == "A":
                    P.barrier()
                    P.flush()
                    return nc, tap_outs

                with contextlib.ExitStack() as bctx:
                    BE = bctx.enter_context
                    wq = BE(nc.sbuf_tensor(f"wq{seq}", [128, DC * 1024], BF16)); wq_b = Buf()
                    bqr = BE(nc.sbuf_tensor(f"bqr{seq}", [1, D], BF16)); bqr_b = Buf()
                    maskt = BE(nc.sbuf_tensor(f"maskt{seq}", [128, 128], F32)); mask_b = Buf()
                    scores = BE(nc.sbuf_tensor(f"scores{seq}", [128, S], F32))
                    sc_b = [Buf() for _ in range(8)]
                    prs = [BE(nc.sbuf_tensor(f"pr{seq}_{i}", [128, 1024], BF16)) for i in range(2)]
                    pr_b = [Buf(), Buf()]
                    PT = [BE(nc.sbuf_tensor(f"PT{seq}_{i}", [128, 1024], BF16)) for i in range(2)]
                    PT_b = [Buf(), Buf()]
                    xt = [BE(nc.sbuf_tensor(f"xtB{seq}_{i}", [128, D], F32)) for i in range(3)]
                    xt_b = [Buf(), Buf(), Buf()]
                    uTt = [BE(nc.sbuf_tensor(f"uTt{seq}_{i}", [128, DC * 128], BF16)) for i in range(2)]
                    uTt_b = [Buf(), Buf()]
                    Qsb = [BE(nc.sbuf_tensor(f"Q{seq}_{i}", [128, D], BF16)) for i in range(2)]
                    Q_b = [Buf(), Buf()]
                    QT = [BE(nc.sbuf_tensor(f"QT{seq}_{i}", [128, DC * 128], BF16)) for i in range(2)]
                    QT_b = [Buf(), Buf()]
                    att = BE(nc.sbuf_tensor(f"att{seq}", [128, D], F32)); att_b = Buf()
                    sm = [BE(nc.sbuf_tensor(f"smB{seq}_{i}", [128, 16], F32)) for i in range(2)]
                    pmg_b = [[Buf() for _ in range(8)], [Buf() for _ in range(8)]]
                    rm_b = [Buf(), Buf()]; nb_b = [Buf(), Buf()]
                    part_b = [Buf(), Buf()]; rs_b = [Buf(), Buf()]; rinv_b = [Buf(), Buf()]
                    stB2 = [make_stats(bctx, nc, f"stB2{seq}_{i}") for i in range(2)]

                    P.dma("qg", wq[:].rearrange("p (c n) -> p c n", n=1024),
                          Wqd.rearrange("(c p) n -> p c n", p=128), writes=[wq_b])
                    P.dma("qg", bqr[:], bqd, writes=[bqr_b])
                    P.dma("qs", maskt[:], maskd, writes=[mask_b])

                    def frontLN(i):
                        P.dma("qs", xt[i % 3][:], Xd[seq, i * 128:(i + 1) * 128, :],
                              writes=[xt_b[i % 3]])
                        P.dma("qc", uTt[i % 2][:], uTs[seq, i], writes=[uTt_b[i % 2]])

                    def frontQ(i):
                        par = i % 2
                        P.op("tensor", proj_emit(psB[:], uTt[par][:], wq[:], bqr[:]),
                             reads=[uTt_b[par], wq_b, ones1_b, bqr_b], writes=[psB_b])
                        P.op("scalar", lambda e, o=Qsb[par][:], i=psB[:]:
                             e.activation(out=o, in_=i, func=AF.Copy),
                             reads=[psB_b], writes=[Q_b[par]])
                        P.op("tensor", tr8_emit(psT, Qsb[par][:], identb[:]),
                             reads=[Q_b[par], identb_b], writes=[psT_b])
                        P.op("scalar", lambda e, o=QT[par][:], i=psT[:]: e.copy(out=o, in_=i),
                             reads=[psT_b], writes=[QT_b[par]])

                    frontLN(0)
                    frontLN(1)
                    frontQ(0)
                    for i in range(NT):
                        par = i % 2
                        keys = (i + 1) * 128
                        ngrp = (keys + 511) // 512
                        for g in range(ngrp):
                            w = min(512, keys - g * 512)
                            gs = g % 2

                            def sc_emit(e, ps=psS[gs], qt=QT[par][:], g=g, w=w):
                                for c in range(DC):
                                    last = e.matmul(
                                        ps[:, 0:w], qt[:, c * 128:(c + 1) * 128],
                                        KT[:, c * S + g * 512: c * S + g * 512 + w],
                                        start=(c == 0), stop=(c == DC - 1))
                                return last
                            P.op("tensor", sc_emit,
                                 reads=[QT_b[par]] + kt_tb[g * 4:g * 4 + (w + 127) // 128],
                                 writes=[psS_b[gs]])
                            if g < ngrp - 1:
                                P.op("scalar", lambda e, o=scores[:, g * 512:g * 512 + w],
                                     i=psS[gs][:, 0:w]: e.activation(out=o, in_=i, func=AF.Copy),
                                     reads=[psS_b[gs]], writes=[sc_b[g]])
                            else:
                                lead = w - 128
                                if lead:
                                    P.op("scalar", lambda e, o=scores[:, g * 512:g * 512 + lead],
                                         i=psS[gs][:, 0:lead]: e.activation(out=o, in_=i, func=AF.Copy),
                                         reads=[psS_b[gs]], writes=[sc_b[g]])
                                P.op("vector", lambda e, o=scores[:, g * 512 + lead:g * 512 + w],
                                     i=psS[gs][:, lead:w], m=maskt[:]:
                                     e.tensor_tensor(out=o, in0=i, in1=m, op=ALU.add),
                                     reads=[psS_b[gs], mask_b], writes=[sc_b[g]])
                            P.op("vector", lambda e, o=sm[par][:, g:g + 1],
                                 i=scores[:, g * 512:g * 512 + w]:
                                 e.reduce_max(out=o, in_=i, axis=AX.X),
                                 reads=[sc_b[g]], writes=[pmg_b[par][g]])
                        P.op("vector", lambda e, o=sm[par][:, 8:9], i=sm[par][:, 0:ngrp]:
                             e.reduce_max(out=o, in_=i, axis=AX.X),
                             reads=pmg_b[par][0:ngrp], writes=[rm_b[par]])
                        P.op("vector", lambda e, o=sm[par][:, 9:10], i=sm[par][:, 8:9]:
                             e.tensor_scalar(out=o, in0=i, scalar1=-ISC, scalar2=None, op0=ALU.mult),
                             reads=[rm_b[par]], writes=[nb_b[par]])
                        nchunk = (keys + 1023) // 1024
                        for ch in range(nchunk):
                            w2 = min(1024, keys - ch * 1024)
                            ps_ = ch % 2
                            P.op("scalar", lambda e, o=prs[ps_][:, 0:w2],
                                 i=scores[:, ch * 1024:ch * 1024 + w2], b=sm[par][:, 9:10]:
                                 e.activation(out=o, in_=i, func=AF.Exp, bias=b, scale=ISC),
                                 reads=sc_b[2 * ch:2 * ch + (w2 + 511) // 512] + [nb_b[par]],
                                 writes=[pr_b[ps_]])
                            P.op("vector", lambda e, o=sm[par][:, 10 + ch:11 + ch], i=prs[ps_][:, 0:w2]:
                                 e.reduce_sum(out=o, in_=i, axis=AX.X),
                                 reads=[pr_b[ps_]], writes=[part_b[par]])
                            if ch == 0:
                                if i + 2 < NT:
                                    frontLN(i + 2)
                                if i + 1 < NT:
                                    frontQ(i + 1)
                            nkt = w2 // 128

                            def ptr_emit(e, psd=psT2, src=prs[ps_][:], nkt=nkt):
                                for c in range(nkt):
                                    last = e.transpose(psd[:, c * 128:(c + 1) * 128],
                                                       src[:, c * 128:(c + 1) * 128], identb[:])
                                return last
                            P.op("tensor", ptr_emit, reads=[pr_b[ps_], identb_b], writes=[psT2_b])
                            P.op("scalar", lambda e, o=PT[ps_][:, 0:nkt * 128], i=psT2[:, 0:nkt * 128]:
                                 e.copy(out=o, in_=i),
                                 reads=[psT2_b], writes=[PT_b[ps_]])

                            def pv_emit(e, pt=PT[ps_][:], ch=ch, nkt=nkt, i=i):
                                for jj in range(nkt):
                                    j = ch * 8 + jj
                                    for n in range(2):
                                        last = e.matmul(
                                            psA[:, n * 512:(n + 1) * 512],
                                            pt[:, jj * 128:(jj + 1) * 128],
                                            Vb[:, j * D + n * 512: j * D + n * 512 + 512],
                                            start=(j == 0), stop=(j == i))
                                return last
                            P.op("tensor", pv_emit,
                                 reads=[PT_b[ps_]] + v_tb[ch * 8:ch * 8 + nkt],
                                 writes=[psA_b])
                        P.op("vector", lambda e, o=sm[par][:, 14:15], i=sm[par][:, 10:10 + nchunk]:
                             e.reduce_sum(out=o, in_=i, axis=AX.X),
                             reads=[part_b[par]], writes=[rs_b[par]])
                        P.op("vector", lambda e, o=sm[par][:, 15:16], i=sm[par][:, 14:15]:
                             e.reciprocal(out=o, in_=i),
                             reads=[rs_b[par]], writes=[rinv_b[par]])
                        P.op("scalar", lambda e, o=att[:], i=psA[:], r=sm[par][:, 15:16]:
                             e.activation(out=o, in_=i, func=AF.Copy, scale=r),
                             reads=[psA_b, rinv_b[par]], writes=[att_b])
                        emit_ln(P, stB2[par], att[:], [att_b], att[:], att_b)
                        P.op("vector", lambda e, o=att[:], i=xt[i % 3][:]:
                             e.tensor_tensor(out=o, in0=o, in1=i, op=ALU.add),
                             reads=[xt_b[i % 3]], writes=[att_b])
                        P.dma("qs", x1s[seq, i * 128:(i + 1) * 128, :], att[:], reads=[att_b])

                    tap(P, f"x1_{seq}", x1s[seq], [], [S, D], F32)
                    P.barrier()
                    P.flush()

            if upto == "B":
                P.barrier()
                P.flush()
                return nc, tap_outs

            with contextlib.ExitStack() as cctx:
                CE = cctx.enter_context
                acc = CE(nc.sbuf_tensor(f"acc{seq}", [128, CHT * D], F32))
                acc_tb = [Buf() for _ in range(CHT)]
                u2T = CE(nc.sbuf_tensor(f"u2T{seq}", [128, CHT * DC * 128], BF16))
                u2T_tb = [Buf() for _ in range(CHT)]
                wr = [CE(nc.sbuf_tensor(f"wr{seq}_{i}", [128, DC * 1024], BF16)) for i in range(2)]
                wr_b = [Buf(), Buf()]
                x1td = [CE(nc.sbuf_tensor(f"x1td{seq}_{i}", [128, D], F32)) for i in range(2)]
                x1_b = [Buf(), Buf()]
                u2t = [CE(nc.sbuf_tensor(f"u2t{seq}_{i}", [128, D], BF16)) for i in range(2)]
                u2_b = [Buf(), Buf()]
                yt = [CE(nc.sbuf_tensor(f"yt{seq}_{i}", [128, D], F32)) for i in range(2)]
                yt_b = [Buf(), Buf()]
                g_ch = CE(nc.sbuf_tensor(f"g_ch{seq}", [128, CHT * NE], F32))
                g_tb = [Buf() for _ in range(CHT)]
                gT_ch = CE(nc.sbuf_tensor(f"gT_ch{seq}", [16, CHT * 128], BF16))
                gT_tb = [Buf() for _ in range(CHT // 4)]
                rl_ch = CE(nc.sbuf_tensor(f"rl_ch{seq}", [128, CHT * NE], F32))
                rl_tb = [Buf() for _ in range(CHT // 4)]
                sdash_ch = CE(nc.sbuf_tensor(f"sdash_ch{seq}", [128, CHT * NE], F32))
                sdc_tb = [Buf() for _ in range(CHT)]
                gtmp = [CE(nc.sbuf_tensor(f"gtmp{seq}_{i}", [128, NE], F32)) for i in range(2)]
                gtm_b = [Buf(), Buf()]
                gsm = [CE(nc.sbuf_tensor(f"gsm{seq}_{i}", [128, 6], F32)) for i in range(2)]
                nbr_b = [Buf(), Buf()]; zr_b = [Buf(), Buf()]; rz_b = [Buf(), Buf()]
                m1_b = [Buf(), Buf()]; m2_b = [Buf(), Buf()]
                CTs = CE(nc.sbuf_tensor(f"CTs{seq}", [128, DC * NE], BF16)); CT_b = Buf()
                brs = CE(nc.sbuf_tensor(f"brs{seq}", [16, D], BF16)); br_b = Buf()
                ebr = CE(nc.sbuf_tensor(f"ebr{seq}", [1, NE], BF16)); eb_b = Buf()
                bs0r = CE(nc.sbuf_tensor(f"bs0r{seq}", [1, D], BF16)); bs0_b = Buf()
                bs1r = CE(nc.sbuf_tensor(f"bs1r{seq}", [1, D], BF16)); bs1_b = Buf()
                identf = CE(nc.sbuf_tensor(f"identf_sb{seq}", [128, 128], F32)); identf_b = Buf()
                Wsh = CE(nc.sbuf_tensor(f"Wsh{seq}", [128, DC * 1024], BF16)); Wsh_b = Buf()
                wtmp = [CE(nc.sbuf_tensor(f"wtmp{seq}_{i}", [128, 1024], F32)) for i in range(2)]
                wtmp_b = [Buf(), Buf()]
                stC = [make_stats(cctx, nc, f"stC{seq}_{i}") for i in range(2)]

                P.dma("qg", CTs[:].rearrange("p (c e) -> p c e", e=NE),
                      CTd.rearrange("(c p) e -> p c e", p=128), writes=[CT_b])
                P.dma("qg", brs[:], brd, writes=[br_b])
                P.dma("qg", ebr[:], ebd, writes=[eb_b])
                P.dma("qg", bs0r[:], bsd[0:1, :], writes=[bs0_b])
                P.dma("qg", bs1r[:], bsd[1:2, :], writes=[bs1_b])
                P.dma("qs", identf[:], identfd, writes=[identf_b])
                for c in range(DC):
                    P.dma("qs", wtmp[0][:], WsTd[0, c * 128:(c + 1) * 128, :], writes=[wtmp_b[0]])
                    P.dma("qc", wtmp[1][:], WsTd[1, c * 128:(c + 1) * 128, :], writes=[wtmp_b[1]])
                    P.op("vector", lambda e, a=wtmp[0][:], b2=wtmp[1][:]:
                         e.tensor_tensor(out=a, in0=a, in1=b2, op=ALU.add),
                         reads=[wtmp_b[1]], writes=[wtmp_b[0]])
                    P.op("vector", lambda e, o=Wsh[:, c * 1024:(c + 1) * 1024], i=wtmp[0][:]:
                         e.tensor_scalar(out=o, in0=i, scalar1=0.5, scalar2=None, op0=ALU.mult),
                         reads=[wtmp_b[0]], writes=[Wsh_b])

                def emit_final(t, ch):
                    par = t % 2
                    gt = ch * CHT + t
                    P.dma("qs", x1td[par][:], x1s[seq, gt * 128:(gt + 1) * 128, :],
                          writes=[x1_b[par]])
                    st = stC[par]
                    tt = st["t"]
                    P.op("vector", lambda e, o=tt[:, 0:6], i=acc[:, t * D:t * D + 512]:
                         e.bn_stats(out=o, in_=i), reads=[acc_tb[t]], writes=[st["bn_a"]])
                    P.op("vector", lambda e, o=tt[:, 6:12], i=acc[:, t * D + 512:t * D + 1024]:
                         e.bn_stats(out=o, in_=i), reads=[acc_tb[t]], writes=[st["bn_b"]])
                    P.op("vector", lambda e, o=tt[:, 12:14], i=tt[:, 0:12]:
                         e.bn_aggr(out=o, in_=i),
                         reads=[st["bn_a"], st["bn_b"]], writes=[st["mv"]])
                    P.op("vector", lambda e, o=tt[:, 14:15], i=tt[:, 13:14]:
                         e.tensor_scalar(out=o, in0=i, scalar1=EPS, scalar2=None, op0=ALU.add),
                         reads=[st["mv"]], writes=[st["ve"]])
                    P.op("scalar", lambda e, o=tt[:, 15:16], i=tt[:, 14:15]:
                         e.activation(out=o, in_=i, func=AF.Ln),
                         reads=[st["ve"]], writes=[st["lnv"]])
                    P.op("scalar", lambda e, o=tt[:, 16:17], i=tt[:, 15:16]:
                         e.activation(out=o, in_=i, func=AF.Exp, scale=-0.5),
                         reads=[st["lnv"]], writes=[st["rstd"]])
                    P.op("vector", lambda e, o=yt[par][:], i=acc[:, t * D:(t + 1) * D],
                         m=tt[:, 12:13], r=tt[:, 16:17]:
                         e.tensor_scalar(out=o, in0=i, scalar1=m, scalar2=r,
                                         op0=ALU.subtract, op1=ALU.mult),
                         reads=[acc_tb[t], st["mv"], st["rstd"]], writes=[yt_b[par]])
                    P.op("vector", lambda e, o=yt[par][:], i=x1td[par][:]:
                         e.tensor_tensor(out=o, in0=o, in1=i, op=ALU.add),
                         reads=[x1_b[par]], writes=[yt_b[par]])
                    P.dma("qs", Yd[seq, gt * 128:(gt + 1) * 128, :], yt[par][:],
                          reads=[yt_b[par]])

                for ch in range(NCH):
                    base = ch * CHT
                    P.dma("qs", x1td[0][:], x1s[seq, base * 128:(base + 1) * 128, :],
                          writes=[x1_b[0]])
                    emit_ln(P, stC[0], x1td[0][:], [x1_b[0]], u2t[0][:], u2_b[0])
                    for t in range(CHT):
                        par = t % 2
                        gt = base + t
                        if t + 1 < CHT:
                            P.dma("qs", x1td[1 - par][:],
                                  x1s[seq, (gt + 1) * 128:(gt + 2) * 128, :],
                                  writes=[x1_b[1 - par]])
                            emit_ln(P, stC[1 - par], x1td[1 - par][:], [x1_b[1 - par]],
                                    u2t[1 - par][:], u2_b[1 - par])
                        P.op("tensor", tr8_emit(psT, u2t[par][:], identb[:]),
                             reads=[u2_b[par], identb_b], writes=[psT_b])
                        P.op("scalar", lambda e, o=u2T[:, t * 1024:(t + 1) * 1024], i=psT[:]:
                             e.copy(out=o, in_=i),
                             reads=[psT_b], writes=[u2T_tb[t]])
                        P.op("scalar", lambda e, o=acc[:, t * D:(t + 1) * D], i=u2t[par][:]:
                             e.copy(out=o, in_=i),
                             reads=[u2_b[par]], writes=[acc_tb[t]])
                        if t % 4 == 3:
                            t0 = t - 3

                            def rt4_emit(e, t0=t0):
                                for tt_ in range(t0, t0 + 4):
                                    sl = (tt_ - t0) * NE
                                    for c in range(DC):
                                        e.matmul(psS0[:, sl:sl + NE],
                                                 u2T[:, tt_ * 1024 + c * 128: tt_ * 1024 + (c + 1) * 128],
                                                 CTs[:, c * NE:(c + 1) * NE],
                                                 start=(c == 0), stop=False)
                                    last = e.matmul(psS0[:, sl:sl + NE], ones1[:], ebr[:],
                                                    start=False, stop=True)
                                return last
                            P.op("tensor", rt4_emit,
                                 reads=[u2T_tb[t0], u2T_tb[t0 + 1], u2T_tb[t0 + 2],
                                        u2T_tb[t0 + 3], CT_b, ones1_b, eb_b],
                                 writes=[psS0_b])
                            P.op("scalar", lambda e, o=rl_ch[:, t0 * NE:(t0 + 4) * NE],
                                 i=psS0[:, 0:4 * NE]: e.copy(out=o, in_=i),
                                 reads=[psS0_b], writes=[rl_tb[t0 // 4]])
                            for tt_ in range(t0, t0 + 4):
                                pr2 = tt_ % 2
                                P.op("vector", lambda e, o=gsm[pr2][:, 0:1],
                                     i=rl_ch[:, tt_ * NE:(tt_ + 1) * NE]:
                                     e.reduce_max(out=o, in_=i, axis=AX.X, negate=True),
                                     reads=[rl_tb[t0 // 4]], writes=[nbr_b[pr2]])
                                P.op("scalar", lambda e, o=sdash_ch[:, tt_ * NE:(tt_ + 1) * NE],
                                     i=rl_ch[:, tt_ * NE:(tt_ + 1) * NE], b=gsm[pr2][:, 0:1]:
                                     e.activation(out=o, in_=i, func=AF.Exp, bias=b),
                                     reads=[rl_tb[t0 // 4], nbr_b[pr2]], writes=[sdc_tb[tt_]])
                    for t in range(CHT):
                        par = t % 2
                        sd_ap = sdash_ch[:, t * NE:(t + 1) * NE]
                        P.op("vector", lambda e, o=gsm[par][:, 1:2], i=sd_ap:
                             e.reduce_sum(out=o, in_=i, axis=AX.X),
                             reads=[sdc_tb[t]], writes=[zr_b[par]])
                        P.op("vector", lambda e, o=gsm[par][:, 2:3], i=gsm[par][:, 1:2]:
                             e.reciprocal(out=o, in_=i),
                             reads=[zr_b[par]], writes=[rz_b[par]])
                        P.op("vector", lambda e, o=gsm[par][:, 3:4], i=sd_ap:
                             e.reduce_max(out=o, in_=i, axis=AX.X),
                             reads=[sdc_tb[t]], writes=[m1_b[par]])
                        P.op("vector", lambda e, o=gtmp[par][:], i=sd_ap, m=gsm[par][:, 3:4]:
                             e.tensor_scalar(out=o, in0=i, scalar1=m, scalar2=-2.0,
                                             op0=ALU.is_ge, op1=ALU.mult),
                             reads=[sdc_tb[t], m1_b[par]], writes=[gtm_b[par]])
                        P.op("vector", lambda e, o=gtmp[par][:], i=sd_ap:
                             e.tensor_tensor(out=o, in0=i, in1=o, op=ALU.add),
                             reads=[sdc_tb[t]], writes=[gtm_b[par]])
                        P.op("vector", lambda e, o=gsm[par][:, 4:5], i=gtmp[par][:]:
                             e.reduce_max(out=o, in_=i, axis=AX.X),
                             reads=[gtm_b[par]], writes=[m2_b[par]])
                        P.op("vector", lambda e, o=gtmp[par][:], i=sd_ap, m=gsm[par][:, 4:5]:
                             e.tensor_scalar(out=o, in0=i, scalar1=m, scalar2=None, op0=ALU.is_ge),
                             reads=[sdc_tb[t], m2_b[par]], writes=[gtm_b[par]])
                        P.op("vector", lambda e, o=g_ch[:, t * NE:(t + 1) * NE], i=sd_ap,
                             r=gsm[par][:, 2:3], m=gtmp[par][:]:
                             e.scalar_tensor_tensor(out=o, in0=i, scalar=r, in1=m,
                                                    op0=ALU.mult, op1=ALU.mult),
                             reads=[sdc_tb[t], rz_b[par], gtm_b[par]], writes=[g_tb[t]])
                        if t % 4 == 3:
                            t0 = t - 3

                            def gt4_emit(e, t0=t0):
                                for tt_ in range(t0, t0 + 4):
                                    last = e.transpose(
                                        psS1[0:16, (tt_ - t0) * 128:(tt_ - t0 + 1) * 128],
                                        g_ch[:, tt_ * NE:(tt_ + 1) * NE], identf[:])
                                return last
                            P.op("tensor", gt4_emit,
                                 reads=[g_tb[t0], g_tb[t0 + 1], g_tb[t0 + 2], g_tb[t0 + 3],
                                        identf_b],
                                 writes=[psS1_b])
                            P.op("vector", lambda e, o=gT_ch[0:16, t0 * 128:(t0 + 4) * 128],
                                 i=psS1[0:16, 0:512]: e.tensor_copy(out=o, in_=i),
                                 reads=[psS1_b], writes=[gT_tb[t0 // 4]])
                    P.dma("qg", wr[0][:].rearrange("p (c n) -> p c n", n=1024),
                          WrTd[0].rearrange("(c p) n -> p c n", p=128), writes=[wr_b[0]])
                    P.dma("qg", wr[1][:].rearrange("p (c n) -> p c n", n=1024),
                          WrTd[1].rearrange("(c p) n -> p c n", p=128), writes=[wr_b[1]])
                    for t in range(CHT):
                        psx, psx_b = (psA, psA_b) if t % 2 == 0 else (psB, psB_b)

                        def ai_emit(e, ps=psx[:], t=t):
                            for c in range(DC):
                                for n in range(2):
                                    e.matmul(ps[:, n * 512:(n + 1) * 512],
                                             u2T[:, t * 1024 + c * 128: t * 1024 + (c + 1) * 128],
                                             Wsh[:, c * 1024 + n * 512: c * 1024 + n * 512 + 512],
                                             start=(c == 0), stop=False)
                            for n in range(2):
                                e.matmul(ps[:, n * 512:(n + 1) * 512],
                                         gT_ch[0:16, t * 128:(t + 1) * 128],
                                         brs[:, n * 512:n * 512 + 512], start=False, stop=False)
                                e.matmul(ps[:, n * 512:(n + 1) * 512], halves[:],
                                         bs0r[:, n * 512:n * 512 + 512], start=False, stop=False)
                                last = e.matmul(ps[:, n * 512:(n + 1) * 512], halves[:],
                                                bs1r[:, n * 512:n * 512 + 512],
                                                start=False, stop=True)
                            return last
                        P.op("tensor", ai_emit,
                             reads=[u2T_tb[t], Wsh_b, gT_tb[t // 4], br_b, halves_b,
                                    bs0_b, bs1_b],
                             writes=[psx_b])
                        P.op("vector", lambda e, o=acc[:, t * D:(t + 1) * D], i=psx[:]:
                             e.tensor_tensor(out=o, in0=i, in1=o, op=ALU.add),
                             reads=[psx_b], writes=[acc_tb[t]])
                    for e_i in range(NE):
                        sl = e_i % 2
                        for t in range(CHT):
                            psx, psx_b = (psA, psA_b) if t % 2 == 0 else (psB, psB_b)

                            def ex_emit(e, ps=psx[:], t=t, sl=sl):
                                for c in range(DC):
                                    for n in range(2):
                                        last = e.matmul(
                                            ps[:, n * 512:(n + 1) * 512],
                                            u2T[:, t * 1024 + c * 128: t * 1024 + (c + 1) * 128],
                                            wr[sl][:, c * 1024 + n * 512: c * 1024 + n * 512 + 512],
                                            start=(c == 0), stop=(c == DC - 1))
                                return last
                            P.op("tensor", ex_emit,
                                 reads=[u2T_tb[t], wr_b[sl]], writes=[psx_b])
                            P.op("vector", lambda e, o=acc[:, t * D:(t + 1) * D], i=psx[:],
                                 gsc=g_ch[:, t * NE + e_i: t * NE + e_i + 1]:
                                 e.scalar_tensor_tensor(out=o, in0=i, scalar=gsc, in1=o,
                                                        op0=ALU.mult, op1=ALU.add),
                                 reads=[psx_b, g_tb[t]], writes=[acc_tb[t]])
                            if e_i == NE - 1:
                                emit_final(t, ch)
                        if e_i + 2 < NE:
                            P.dma("qg", wr[sl][:].rearrange("p (c n) -> p c n", n=1024),
                                  WrTd[e_i + 2].rearrange("(c p) n -> p c n", p=128),
                                  writes=[wr_b[sl]])

                    if f"acc{seq}_{ch}" in taps:
                        tap(P, f"acc{seq}_{ch}", acc[:], acc_tb, [128, CHT * D], F32)
                    if f"g{seq}_{ch}" in taps:
                        tap(P, f"g{seq}_{ch}", g_ch[:], g_tb, [128, CHT * NE], F32)
                    P.flush()

        P.barrier()
        P.flush()
    return nc, tap_outs


def _host_inputs(inputs):
    f32 = np.float32
    bf16 = ml_dtypes.bfloat16
    X = np.ascontiguousarray(np.asarray(inputs["L0_attn_X"], f32))
    shared = {
        "Wq": np.ascontiguousarray(np.asarray(inputs["L0_attn_Wq"], f32)),
        "Wk": np.ascontiguousarray(np.asarray(inputs["L0_attn_Wk"], f32)),
        "Wv": np.ascontiguousarray(np.asarray(inputs["L0_attn_Wv"], f32)),
        "bq": np.ascontiguousarray(np.asarray(inputs["L0_attn_bq"], f32).reshape(1, D)),
        "bk": np.ascontiguousarray(np.asarray(inputs["L0_attn_bk"], f32).reshape(1, D)),
        "bv": np.ascontiguousarray(np.asarray(inputs["L0_attn_bv"], f32).reshape(1, D)),
        "CT": np.ascontiguousarray(np.asarray(inputs["L0_ffn_centroids"], f32).T),
        "eb": np.ascontiguousarray(np.asarray(inputs["L0_ffn_expert_biases"], f32).reshape(1, NE)),
        "WrT": np.ascontiguousarray(np.asarray(inputs["L0_ffn_Wr"], f32).transpose(0, 2, 1)),
        "br": np.ascontiguousarray(np.asarray(inputs["L0_ffn_br"], f32)),
        "WsT": np.ascontiguousarray(np.asarray(inputs["L0_ffn_Ws"], f32).transpose(0, 2, 1)),
        "bs": np.ascontiguousarray(np.asarray(inputs["L0_ffn_bs"], f32)),
        "identb": np.eye(128, dtype=bf16),
        "identf": np.eye(128, dtype=f32),
        "mask": np.triu(np.full((128, 128), -1.0e30, f32), 1),
        "ones1": np.ones((1, 128), dtype=bf16),
        "halves": np.full((1, 128), 0.5, dtype=bf16),
    }
    in_maps = []
    for c in range(NCORES):
        m = dict(shared)
        m["X"] = np.ascontiguousarray(X[c * SPC:(c + 1) * SPC])
        in_maps.append(m)
    return in_maps


def kernel(**inputs):
    in_maps = _host_inputs(inputs)
    nc, _ = build()
    res = run_bass_kernel_spmd(nc, in_maps, core_ids=list(range(NCORES)))
    out = np.concatenate([np.asarray(res.results[c]["Y"]) for c in range(NCORES)], axis=0)
    return out.astype(np.float32, copy=False)


# version 05
# speedup vs baseline: 1.4574x; 1.0134x over previous
import contextlib
import numpy as np
import ml_dtypes
import concourse.bass as bass
import concourse.mybir as mybir
from concourse.bass_utils import run_bass_kernel_spmd

F32 = mybir.dt.float32
BF16 = mybir.dt.bfloat16
AF = mybir.ActivationFunctionType
ALU = mybir.AluOpType
AX = mybir.AxisListType

B, S, D = 16, 4096, 1024
NCORES = 8
SPC = B // NCORES
NT = S // 128
DC = D // 128
NE = 16
CHT = 16
NCH = NT // CHT
EPS = 1e-5
ISC = 1.0 / 32.0

ENG2SEM = {"tensor": "t", "vector": "v", "scalar": "s", "gpsimd": "g"}
QENG = {"qs": "sync", "qc": "scalar", "qg": "gpsimd"}


class Buf:
    __slots__ = ("w", "rs")

    def __init__(self):
        self.w = None
        self.rs = []


class Prog:
    def __init__(self, nc, ctx):
        self.nc = nc
        self.pend = {e: [] for e in ("sync", "tensor", "vector", "scalar", "gpsimd")}
        self.sems = {}
        self.cnt = {}
        self.hw = {e: {} for e in self.pend}
        for sn in ("t", "v", "s", "g", "qs", "qc", "qg"):
            self.sems[sn] = ctx.enter_context(nc.semaphore("sem_" + sn))
            self.cnt[sn] = 0
        self.nops = 0

    def _prune(self, eng, evs):
        best = {}
        for ev in evs:
            if ev is None:
                continue
            sn, val = ev
            if eng == "tensor" and sn == "t":
                continue
            if best.get(sn, 0) < val:
                best[sn] = val
        hw = self.hw[eng]
        out = []
        for sn, val in best.items():
            if hw.get(sn, 0) >= val:
                continue
            hw[sn] = val
            out.append((sn, val))
        return out

    def _gather(self, reads, writes, extra):
        evs = list(extra)
        for b in reads:
            evs.append(b.w)
        for b in writes:
            evs.append(b.w)
            evs.extend(b.rs)
        return evs

    def op(self, eng, emit, reads=(), writes=(), extra=()):
        waits = self._prune(eng, self._gather(reads, writes, extra))
        sn = ENG2SEM[eng]
        self.cnt[sn] += 1
        ev = (sn, self.cnt[sn])
        sems = self.sems

        def run(e, waits=waits, emit=emit, sem=sems[sn], sems=sems):
            for wn, wv in waits:
                e.wait_ge(sems[wn], wv)
            emit(e).then_inc(sem, 1)

        self.pend[eng].append(run)
        self.nops += 1
        for b in reads:
            b.rs.append(ev)
        for b in writes:
            b.w = ev
            b.rs = []
        return ev

    def dma(self, q, out, in_, reads=(), writes=(), extra=()):
        eng = QENG[q]
        waits = self._prune(eng, self._gather(reads, writes, extra))
        self.cnt[q] += 16
        ev = (q, self.cnt[q])
        sems = self.sems

        def run(e, waits=waits, out=out, in_=in_, sem=sems[q], sems=sems):
            for wn, wv in waits:
                e.wait_ge(sems[wn], wv)
            e.dma_start(out=out, in_=in_).then_inc(sem, 16)

        self.pend[eng].append(run)
        self.nops += 1
        for b in reads:
            b.rs.append(ev)
        for b in writes:
            b.w = ev
            b.rs = []
        return ev

    def barrier(self):
        cur = [(sn, v) for sn, v in self.cnt.items() if v > 0]
        sems = self.sems
        for eng in self.pend:
            waits = self._prune(eng, cur)
            if waits:
                def run(e, waits=waits, sems=sems):
                    for wn, wv in waits:
                        e.wait_ge(sems[wn], wv)
                self.pend[eng].append(run)

    def flush(self):
        pend = {e: lst for e, lst in self.pend.items() if lst}
        self.pend = {e: [] for e in self.pend}
        if not pend:
            return
        with self.nc.Block() as blk:
            for eng, lst in pend.items():
                getattr(blk, eng)(lambda e, lst=lst: [f(e) for f in lst])


def make_stats(ctx, nc, name):
    t = ctx.enter_context(nc.sbuf_tensor(name, [128, 18], F32))
    return dict(t=t, bn_a=Buf(), bn_b=Buf(), mv=Buf(), ve=Buf(), lnv=Buf(), rstd=Buf())


def emit_ln(P, st, src, src_bufs, dst, dst_buf, norm_eng="vector"):
    t = st["t"]
    P.op("vector", lambda e, o=t[:, 0:6], i=src[:, 0:512]: e.bn_stats(out=o, in_=i),
         reads=src_bufs, writes=[st["bn_a"]])
    P.op("vector", lambda e, o=t[:, 6:12], i=src[:, 512:1024]: e.bn_stats(out=o, in_=i),
         reads=src_bufs, writes=[st["bn_b"]])
    P.op("vector", lambda e, o=t[:, 12:14], i=t[:, 0:12]: e.bn_aggr(out=o, in_=i),
         reads=[st["bn_a"], st["bn_b"]], writes=[st["mv"]])
    P.op("vector", lambda e, o=t[:, 14:15], i=t[:, 13:14]:
         e.tensor_scalar(out=o, in0=i, scalar1=EPS, scalar2=None, op0=ALU.add),
         reads=[st["mv"]], writes=[st["ve"]])
    P.op("scalar", lambda e, o=t[:, 15:16], i=t[:, 14:15]:
         e.activation(out=o, in_=i, func=AF.Ln),
         reads=[st["ve"]], writes=[st["lnv"]])
    P.op("scalar", lambda e, o=t[:, 16:17], i=t[:, 15:16]:
         e.activation(out=o, in_=i, func=AF.Exp, scale=-0.5),
         reads=[st["lnv"]], writes=[st["rstd"]])
    P.op(norm_eng, lambda e, o=dst, i=src, m=t[:, 12:13], r=t[:, 16:17]:
         e.tensor_scalar(out=o, in0=i, scalar1=m, scalar2=r,
                         op0=ALU.subtract, op1=ALU.mult),
         reads=list(src_bufs) + [st["mv"], st["rstd"]], writes=[dst_buf])


def build(upto="full", taps=()):
    taps = set(taps)
    nc = bass.Bass("TRN2", target_bir_lowering=False)

    Xd = nc.dram_tensor("X", [SPC, S, D], F32, kind="ExternalInput").ap()
    Yd = nc.dram_tensor("Y", [SPC, S, D], F32, kind="ExternalOutput").ap()
    x1s = nc.dram_tensor("x1scr", [SPC, S, D], F32, kind="Internal").ap()
    uTs = nc.dram_tensor("uTscr", [SPC, NT, 128, DC * 128], BF16, kind="Internal").ap()
    Wqd = nc.dram_tensor("Wq", [D, D], F32, kind="ExternalInput").ap()
    Wkd = nc.dram_tensor("Wk", [D, D], F32, kind="ExternalInput").ap()
    Wvd = nc.dram_tensor("Wv", [D, D], F32, kind="ExternalInput").ap()
    bqd = nc.dram_tensor("bq", [1, D], F32, kind="ExternalInput").ap()
    bkd = nc.dram_tensor("bk", [1, D], F32, kind="ExternalInput").ap()
    bvd = nc.dram_tensor("bv", [1, D], F32, kind="ExternalInput").ap()
    CTd = nc.dram_tensor("CT", [D, NE], F32, kind="ExternalInput").ap()
    ebd = nc.dram_tensor("eb", [1, NE], F32, kind="ExternalInput").ap()
    WrTd = nc.dram_tensor("WrT", [NE, D, D], F32, kind="ExternalInput").ap()
    brd = nc.dram_tensor("br", [NE, D], F32, kind="ExternalInput").ap()
    WsTd = nc.dram_tensor("WsT", [2, D, D], F32, kind="ExternalInput").ap()
    bsd = nc.dram_tensor("bs", [2, D], F32, kind="ExternalInput").ap()
    identbd = nc.dram_tensor("identb", [128, 128], BF16, kind="ExternalInput").ap()
    identfd = nc.dram_tensor("identf", [128, 128], F32, kind="ExternalInput").ap()
    maskd = nc.dram_tensor("mask", [128, 128], F32, kind="ExternalInput").ap()
    ones1d = nc.dram_tensor("ones1", [1, 128], BF16, kind="ExternalInput").ap()
    halvesd = nc.dram_tensor("halves", [1, 128], BF16, kind="ExternalInput").ap()

    tap_outs = {}

    def tap(P, name, ap, readbufs, shape, dtype):
        if name in taps:
            t = nc.dram_tensor("tap_" + name, list(shape), dtype, kind="ExternalOutput").ap()
            tap_outs[name] = t
            P.dma("qs", t, ap, reads=list(readbufs))

    with contextlib.ExitStack() as octx:
        P = Prog(nc, octx)
        OE = octx.enter_context

        identb = OE(nc.sbuf_tensor("identb_sb", [128, 128], BF16)); identb_b = Buf()
        ones1 = OE(nc.sbuf_tensor("ones1_sb", [1, 128], BF16)); ones1_b = Buf()
        halves = OE(nc.sbuf_tensor("halves_sb", [1, 128], BF16)); halves_b = Buf()

        psA = OE(nc.psum_tensor("psA", [128, 1024], F32)); psA_b = Buf()
        psB = OE(nc.psum_tensor("psB", [128, 1024], F32)); psB_b = Buf()
        psT = OE(nc.psum_tensor("psT", [128, 1024], BF16)); psT_b = Buf()
        psT2 = OE(nc.psum_tensor("psT2", [128, 1024], BF16)); psT2_b = Buf()
        psS0 = OE(nc.psum_tensor("psS0", [128, 512], F32)); psS0_b = Buf()
        psS1 = OE(nc.psum_tensor("psS1", [128, 512], F32)); psS1_b = Buf()
        psS = [psS0, psS1]; psS_b = [psS0_b, psS1_b]

        P.dma("qs", identb[:], identbd, writes=[identb_b])
        P.dma("qs", ones1[:], ones1d, writes=[ones1_b])
        P.dma("qs", halves[:], halvesd, writes=[halves_b])

        for seq in range(SPC):
            with contextlib.ExitStack() as sctx:
                SE = sctx.enter_context
                KT = SE(nc.sbuf_tensor(f"KT{seq}", [128, DC * S], BF16))
                Vb = SE(nc.sbuf_tensor(f"V{seq}", [128, NT * D], BF16))
                kt_tb = [Buf() for _ in range(NT)]
                v_tb = [Buf() for _ in range(NT)]

                with contextlib.ExitStack() as actx:
                    AE = actx.enter_context
                    wk = AE(nc.sbuf_tensor(f"wk{seq}", [128, DC * 1024], BF16)); wk_b = Buf()
                    wv = AE(nc.sbuf_tensor(f"wv{seq}", [128, DC * 1024], BF16)); wv_b = Buf()
                    bkr = AE(nc.sbuf_tensor(f"bkr{seq}", [1, D], BF16)); bkr_b = Buf()
                    bvr = AE(nc.sbuf_tensor(f"bvr{seq}", [1, D], BF16)); bvr_b = Buf()
                    xt = [AE(nc.sbuf_tensor(f"xtA{seq}_{i}", [128, D], F32)) for i in range(2)]
                    xt_b = [Buf(), Buf()]
                    usb = [AE(nc.sbuf_tensor(f"uA{seq}_{i}", [128, D], BF16)) for i in range(2)]
                    u_b = [Buf(), Buf()]
                    uT = [AE(nc.sbuf_tensor(f"uTA{seq}_{i}", [128, DC * 128], BF16)) for i in range(2)]
                    uT_b = [Buf(), Buf()]
                    ktmp = [AE(nc.sbuf_tensor(f"ktmpA{seq}_{i}", [128, D], BF16)) for i in range(2)]
                    ktmp_b = [Buf(), Buf()]
                    stA = [make_stats(actx, nc, f"stA{seq}_{i}") for i in range(2)]

                    P.dma("qg", wk[:].rearrange("p (c n) -> p c n", n=1024),
                          Wkd.rearrange("(c p) n -> p c n", p=128), writes=[wk_b])
                    P.dma("qg", wv[:].rearrange("p (c n) -> p c n", n=1024),
                          Wvd.rearrange("(c p) n -> p c n", p=128), writes=[wv_b])
                    P.dma("qg", bkr[:], bkd, writes=[bkr_b])
                    P.dma("qg", bvr[:], bvd, writes=[bvr_b])
                    P.dma("qs", xt[0][:], Xd[seq, 0:128, :], writes=[xt_b[0]])

                    def proj_emit(ps, uTap, W, brow):
                        def em(e, ps=ps, uTap=uTap, W=W, brow=brow):
                            for c in range(DC):
                                for n in range(2):
                                    e.matmul(ps[:, n * 512:(n + 1) * 512],
                                             uTap[:, c * 128:(c + 1) * 128],
                                             W[:, c * 1024 + n * 512: c * 1024 + n * 512 + 512],
                                             start=(c == 0), stop=False)
                            for n in range(2):
                                last = e.matmul(ps[:, n * 512:(n + 1) * 512], ones1[:],
                                                brow[:, n * 512:n * 512 + 512],
                                                start=False, stop=True)
                            return last
                        return em

                    def tr8_emit(psd, src, ident):
                        def em(e, psd=psd, src=src, ident=ident):
                            for c in range(DC):
                                last = e.transpose(psd[:, c * 128:(c + 1) * 128],
                                                   src[:, c * 128:(c + 1) * 128], ident)
                            return last
                        return em

                    emit_ln(P, stA[0], xt[0][:], [xt_b[0]], usb[0][:], u_b[0])
                    for t in range(NT):
                        par = t % 2
                        if t + 1 < NT:
                            P.dma("qs", xt[1 - par][:], Xd[seq, (t + 1) * 128:(t + 2) * 128, :],
                                  writes=[xt_b[1 - par]])
                            emit_ln(P, stA[1 - par], xt[1 - par][:], [xt_b[1 - par]],
                                    usb[1 - par][:], u_b[1 - par])
                        P.op("tensor", tr8_emit(psT, usb[par][:], identb[:]),
                             reads=[u_b[par], identb_b], writes=[psT_b])
                        P.op("scalar", lambda e, o=uT[par][:], i=psT[:]: e.copy(out=o, in_=i),
                             reads=[psT_b], writes=[uT_b[par]])
                        P.dma("qc", uTs[seq, t], uT[par][:], reads=[uT_b[par]])
                        P.op("tensor", proj_emit(psA[:], uT[par][:], wk[:], bkr[:]),
                             reads=[uT_b[par], wk_b, ones1_b, bkr_b], writes=[psA_b])
                        P.op("scalar", lambda e, o=ktmp[par][:], i=psA[:]:
                             e.activation(out=o, in_=i, func=AF.Copy),
                             reads=[psA_b], writes=[ktmp_b[par]])
                        P.op("tensor", proj_emit(psB[:], uT[par][:], wv[:], bvr[:]),
                             reads=[uT_b[par], wv_b, ones1_b, bvr_b], writes=[psB_b])
                        P.op("tensor", tr8_emit(psT2, ktmp[par][:], identb[:]),
                             reads=[ktmp_b[par], identb_b], writes=[psT2_b])
                        P.op("scalar", lambda e,
                             o=KT[:].rearrange("p (c s) -> p c s", s=S)[:, :, t * 128:(t + 1) * 128],
                             i=psT2[:].rearrange("p (c q) -> p c q", q=128):
                             e.copy(out=o, in_=i),
                             reads=[psT2_b], writes=[kt_tb[t]])
                        P.op("scalar", lambda e, o=Vb[:, t * D:(t + 1) * D], i=psB[:]:
                             e.activation(out=o, in_=i, func=AF.Copy),
                             reads=[psB_b], writes=[v_tb[t]])

                    tap(P, f"KT{seq}", KT[:], kt_tb, [128, DC * S], BF16)
                    tap(P, f"V{seq}", Vb[:], v_tb, [128, NT * D], BF16)
                    P.barrier()
                    P.flush()

                if upto == "A":
                    P.barrier()
                    P.flush()
                    return nc, tap_outs

                with contextlib.ExitStack() as bctx:
                    BE = bctx.enter_context
                    wq = BE(nc.sbuf_tensor(f"wq{seq}", [128, DC * 1024], BF16)); wq_b = Buf()
                    bqr = BE(nc.sbuf_tensor(f"bqr{seq}", [1, D], BF16)); bqr_b = Buf()
                    maskt = BE(nc.sbuf_tensor(f"maskt{seq}", [128, 128], F32)); mask_b = Buf()
                    scores = BE(nc.sbuf_tensor(f"scores{seq}", [128, S], F32))
                    sc_b = [Buf() for _ in range(8)]
                    prs = [BE(nc.sbuf_tensor(f"pr{seq}_{i}", [128, 1024], BF16)) for i in range(2)]
                    pr_b = [Buf(), Buf()]
                    PT = [BE(nc.sbuf_tensor(f"PT{seq}_{i}", [128, 1024], BF16)) for i in range(2)]
                    PT_b = [Buf(), Buf()]
                    xt = [BE(nc.sbuf_tensor(f"xtB{seq}_{i}", [128, D], F32)) for i in range(3)]
                    xt_b = [Buf(), Buf(), Buf()]
                    uTt = [BE(nc.sbuf_tensor(f"uTt{seq}_{i}", [128, DC * 128], BF16)) for i in range(2)]
                    uTt_b = [Buf(), Buf()]
                    Qsb = [BE(nc.sbuf_tensor(f"Q{seq}_{i}", [128, D], BF16)) for i in range(2)]
                    Q_b = [Buf(), Buf()]
                    QT = [BE(nc.sbuf_tensor(f"QT{seq}_{i}", [128, DC * 128], BF16)) for i in range(2)]
                    QT_b = [Buf(), Buf()]
                    att = BE(nc.sbuf_tensor(f"att{seq}", [128, D], F32)); att_b = Buf()
                    sm = [BE(nc.sbuf_tensor(f"smB{seq}_{i}", [128, 16], F32)) for i in range(2)]
                    pmg_b = [[Buf() for _ in range(8)], [Buf() for _ in range(8)]]
                    rm_b = [Buf(), Buf()]; nb_b = [Buf(), Buf()]
                    part_b = [Buf(), Buf()]; rs_b = [Buf(), Buf()]; rinv_b = [Buf(), Buf()]
                    stB2 = [make_stats(bctx, nc, f"stB2{seq}_{i}") for i in range(2)]

                    P.dma("qg", wq[:].rearrange("p (c n) -> p c n", n=1024),
                          Wqd.rearrange("(c p) n -> p c n", p=128), writes=[wq_b])
                    P.dma("qg", bqr[:], bqd, writes=[bqr_b])
                    P.dma("qs", maskt[:], maskd, writes=[mask_b])

                    def frontLN(i):
                        P.dma("qs", xt[i % 3][:], Xd[seq, i * 128:(i + 1) * 128, :],
                              writes=[xt_b[i % 3]])
                        P.dma("qc", uTt[i % 2][:], uTs[seq, i], writes=[uTt_b[i % 2]])

                    def frontQ(i):
                        par = i % 2
                        P.op("tensor", proj_emit(psB[:], uTt[par][:], wq[:], bqr[:]),
                             reads=[uTt_b[par], wq_b, ones1_b, bqr_b], writes=[psB_b])
                        P.op("scalar", lambda e, o=Qsb[par][:], i=psB[:]:
                             e.activation(out=o, in_=i, func=AF.Copy),
                             reads=[psB_b], writes=[Q_b[par]])
                        P.op("tensor", tr8_emit(psT, Qsb[par][:], identb[:]),
                             reads=[Q_b[par], identb_b], writes=[psT_b])
                        P.op("vector", lambda e, o=QT[par][:], i=psT[:]: e.tensor_copy(out=o, in_=i),
                             reads=[psT_b], writes=[QT_b[par]])

                    frontLN(0)
                    frontLN(1)
                    frontQ(0)
                    for i in range(NT):
                        par = i % 2
                        keys = (i + 1) * 128
                        ngrp = (keys + 511) // 512
                        for g in range(ngrp):
                            w = min(512, keys - g * 512)
                            gs = g % 2

                            def sc_emit(e, ps=psS[gs], qt=QT[par][:], g=g, w=w):
                                for c in range(DC):
                                    last = e.matmul(
                                        ps[:, 0:w], qt[:, c * 128:(c + 1) * 128],
                                        KT[:, c * S + g * 512: c * S + g * 512 + w],
                                        start=(c == 0), stop=(c == DC - 1))
                                return last
                            P.op("tensor", sc_emit,
                                 reads=[QT_b[par]] + kt_tb[g * 4:g * 4 + (w + 127) // 128],
                                 writes=[psS_b[gs]])
                            if g < ngrp - 1:
                                P.op("scalar", lambda e, o=scores[:, g * 512:g * 512 + w],
                                     i=psS[gs][:, 0:w]: e.activation(out=o, in_=i, func=AF.Copy),
                                     reads=[psS_b[gs]], writes=[sc_b[g]])
                            else:
                                lead = w - 128
                                if lead:
                                    P.op("scalar", lambda e, o=scores[:, g * 512:g * 512 + lead],
                                         i=psS[gs][:, 0:lead]: e.activation(out=o, in_=i, func=AF.Copy),
                                         reads=[psS_b[gs]], writes=[sc_b[g]])
                                P.op("vector", lambda e, o=scores[:, g * 512 + lead:g * 512 + w],
                                     i=psS[gs][:, lead:w], m=maskt[:]:
                                     e.tensor_tensor(out=o, in0=i, in1=m, op=ALU.add),
                                     reads=[psS_b[gs], mask_b], writes=[sc_b[g]])
                            P.op("vector", lambda e, o=sm[par][:, g:g + 1],
                                 i=scores[:, g * 512:g * 512 + w]:
                                 e.reduce_max(out=o, in_=i, axis=AX.X),
                                 reads=[sc_b[g]], writes=[pmg_b[par][g]])
                        P.op("vector", lambda e, o=sm[par][:, 8:9], i=sm[par][:, 0:ngrp]:
                             e.reduce_max(out=o, in_=i, axis=AX.X),
                             reads=pmg_b[par][0:ngrp], writes=[rm_b[par]])
                        P.op("vector", lambda e, o=sm[par][:, 9:10], i=sm[par][:, 8:9]:
                             e.tensor_scalar(out=o, in0=i, scalar1=-ISC, scalar2=None, op0=ALU.mult),
                             reads=[rm_b[par]], writes=[nb_b[par]])
                        nchunk = (keys + 1023) // 1024
                        for ch in range(nchunk):
                            w2 = min(1024, keys - ch * 1024)
                            ps_ = ch % 2
                            P.op("scalar", lambda e, o=prs[ps_][:, 0:w2],
                                 i=scores[:, ch * 1024:ch * 1024 + w2], b=sm[par][:, 9:10]:
                                 e.activation(out=o, in_=i, func=AF.Exp, bias=b, scale=ISC),
                                 reads=sc_b[2 * ch:2 * ch + (w2 + 511) // 512] + [nb_b[par]],
                                 writes=[pr_b[ps_]])
                            P.op("vector", lambda e, o=sm[par][:, 10 + ch:11 + ch], i=prs[ps_][:, 0:w2]:
                                 e.reduce_sum(out=o, in_=i, axis=AX.X),
                                 reads=[pr_b[ps_]], writes=[part_b[par]])
                            if ch == 0:
                                if i + 2 < NT:
                                    frontLN(i + 2)
                                if i + 1 < NT:
                                    frontQ(i + 1)
                            nkt = w2 // 128

                            def ptr_emit(e, psd=psT2, src=prs[ps_][:], nkt=nkt):
                                for c in range(nkt):
                                    last = e.transpose(psd[:, c * 128:(c + 1) * 128],
                                                       src[:, c * 128:(c + 1) * 128], identb[:])
                                return last
                            P.op("tensor", ptr_emit, reads=[pr_b[ps_], identb_b], writes=[psT2_b])
                            P.op("vector", lambda e, o=PT[ps_][:, 0:nkt * 128], i=psT2[:, 0:nkt * 128]:
                                 e.tensor_copy(out=o, in_=i),
                                 reads=[psT2_b], writes=[PT_b[ps_]])

                            def pv_emit(e, pt=PT[ps_][:], ch=ch, nkt=nkt, i=i):
                                for jj in range(nkt):
                                    j = ch * 8 + jj
                                    for n in range(2):
                                        last = e.matmul(
                                            psA[:, n * 512:(n + 1) * 512],
                                            pt[:, jj * 128:(jj + 1) * 128],
                                            Vb[:, j * D + n * 512: j * D + n * 512 + 512],
                                            start=(j == 0), stop=(j == i))
                                return last
                            P.op("tensor", pv_emit,
                                 reads=[PT_b[ps_]] + v_tb[ch * 8:ch * 8 + nkt],
                                 writes=[psA_b])
                        P.op("vector", lambda e, o=sm[par][:, 14:15], i=sm[par][:, 10:10 + nchunk]:
                             e.reduce_sum(out=o, in_=i, axis=AX.X),
                             reads=[part_b[par]], writes=[rs_b[par]])
                        P.op("vector", lambda e, o=sm[par][:, 15:16], i=sm[par][:, 14:15]:
                             e.reciprocal(out=o, in_=i),
                             reads=[rs_b[par]], writes=[rinv_b[par]])
                        P.op("scalar", lambda e, o=att[:], i=psA[:], r=sm[par][:, 15:16]:
                             e.activation(out=o, in_=i, func=AF.Copy, scale=r),
                             reads=[psA_b, rinv_b[par]], writes=[att_b])
                        emit_ln(P, stB2[par], att[:], [att_b], att[:], att_b)
                        P.op("vector", lambda e, o=att[:], i=xt[i % 3][:]:
                             e.tensor_tensor(out=o, in0=o, in1=i, op=ALU.add),
                             reads=[xt_b[i % 3]], writes=[att_b])
                        P.dma("qs", x1s[seq, i * 128:(i + 1) * 128, :], att[:], reads=[att_b])

                    tap(P, f"x1_{seq}", x1s[seq], [], [S, D], F32)
                    P.barrier()
                    P.flush()

            if upto == "B":
                P.barrier()
                P.flush()
                return nc, tap_outs

            with contextlib.ExitStack() as cctx:
                CE = cctx.enter_context
                acc = CE(nc.sbuf_tensor(f"acc{seq}", [128, CHT * D], F32))
                acc_tb = [Buf() for _ in range(CHT)]
                u2T = CE(nc.sbuf_tensor(f"u2T{seq}", [128, CHT * DC * 128], BF16))
                u2T_tb = [Buf() for _ in range(CHT)]
                wr = [CE(nc.sbuf_tensor(f"wr{seq}_{i}", [128, DC * 1024], BF16)) for i in range(2)]
                wr_b = [Buf(), Buf()]
                x1td = [CE(nc.sbuf_tensor(f"x1td{seq}_{i}", [128, D], F32)) for i in range(2)]
                x1_b = [Buf(), Buf()]
                u2t = [CE(nc.sbuf_tensor(f"u2t{seq}_{i}", [128, D], BF16)) for i in range(2)]
                u2_b = [Buf(), Buf()]
                yt = [CE(nc.sbuf_tensor(f"yt{seq}_{i}", [128, D], F32)) for i in range(2)]
                yt_b = [Buf(), Buf()]
                g_ch = CE(nc.sbuf_tensor(f"g_ch{seq}", [128, CHT * NE], F32))
                g_tb = [Buf() for _ in range(CHT)]
                gT_ch = CE(nc.sbuf_tensor(f"gT_ch{seq}", [16, CHT * 128], BF16))
                gT_tb = [Buf() for _ in range(CHT // 4)]
                rl_ch = CE(nc.sbuf_tensor(f"rl_ch{seq}", [128, CHT * NE], F32))
                rl_tb = [Buf() for _ in range(CHT // 4)]
                sdash_ch = CE(nc.sbuf_tensor(f"sdash_ch{seq}", [128, CHT * NE], F32))
                sdc_tb = [Buf() for _ in range(CHT)]
                gtmp = [CE(nc.sbuf_tensor(f"gtmp{seq}_{i}", [128, NE], F32)) for i in range(2)]
                gtm_b = [Buf(), Buf()]
                gsm = [CE(nc.sbuf_tensor(f"gsm{seq}_{i}", [128, 6], F32)) for i in range(2)]
                nbr_b = [Buf(), Buf()]; zr_b = [Buf(), Buf()]; rz_b = [Buf(), Buf()]
                m1_b = [Buf(), Buf()]; m2_b = [Buf(), Buf()]
                CTs = CE(nc.sbuf_tensor(f"CTs{seq}", [128, DC * NE], BF16)); CT_b = Buf()
                brs = CE(nc.sbuf_tensor(f"brs{seq}", [16, D], BF16)); br_b = Buf()
                ebr = CE(nc.sbuf_tensor(f"ebr{seq}", [1, NE], BF16)); eb_b = Buf()
                bs0r = CE(nc.sbuf_tensor(f"bs0r{seq}", [1, D], BF16)); bs0_b = Buf()
                bs1r = CE(nc.sbuf_tensor(f"bs1r{seq}", [1, D], BF16)); bs1_b = Buf()
                identf = CE(nc.sbuf_tensor(f"identf_sb{seq}", [128, 128], F32)); identf_b = Buf()
                Wsh = CE(nc.sbuf_tensor(f"Wsh{seq}", [128, DC * 1024], BF16)); Wsh_b = Buf()
                wtmp = [CE(nc.sbuf_tensor(f"wtmp{seq}_{i}", [128, 1024], F32)) for i in range(2)]
                wtmp_b = [Buf(), Buf()]
                stC = [make_stats(cctx, nc, f"stC{seq}_{i}") for i in range(2)]

                P.dma("qg", CTs[:].rearrange("p (c e) -> p c e", e=NE),
                      CTd.rearrange("(c p) e -> p c e", p=128), writes=[CT_b])
                P.dma("qg", brs[:], brd, writes=[br_b])
                P.dma("qg", ebr[:], ebd, writes=[eb_b])
                P.dma("qg", bs0r[:], bsd[0:1, :], writes=[bs0_b])
                P.dma("qg", bs1r[:], bsd[1:2, :], writes=[bs1_b])
                P.dma("qs", identf[:], identfd, writes=[identf_b])
                for c in range(DC):
                    P.dma("qs", wtmp[0][:], WsTd[0, c * 128:(c + 1) * 128, :], writes=[wtmp_b[0]])
                    P.dma("qc", wtmp[1][:], WsTd[1, c * 128:(c + 1) * 128, :], writes=[wtmp_b[1]])
                    P.op("vector", lambda e, a=wtmp[0][:], b2=wtmp[1][:]:
                         e.tensor_tensor(out=a, in0=a, in1=b2, op=ALU.add),
                         reads=[wtmp_b[1]], writes=[wtmp_b[0]])
                    P.op("vector", lambda e, o=Wsh[:, c * 1024:(c + 1) * 1024], i=wtmp[0][:]:
                         e.tensor_scalar(out=o, in0=i, scalar1=0.5, scalar2=None, op0=ALU.mult),
                         reads=[wtmp_b[0]], writes=[Wsh_b])

                def emit_final(t, ch):
                    par = t % 2
                    gt = ch * CHT + t
                    P.dma("qs", x1td[par][:], x1s[seq, gt * 128:(gt + 1) * 128, :],
                          writes=[x1_b[par]])
                    st = stC[par]
                    tt = st["t"]
                    P.op("vector", lambda e, o=tt[:, 0:6], i=acc[:, t * D:t * D + 512]:
                         e.bn_stats(out=o, in_=i), reads=[acc_tb[t]], writes=[st["bn_a"]])
                    P.op("vector", lambda e, o=tt[:, 6:12], i=acc[:, t * D + 512:t * D + 1024]:
                         e.bn_stats(out=o, in_=i), reads=[acc_tb[t]], writes=[st["bn_b"]])
                    P.op("vector", lambda e, o=tt[:, 12:14], i=tt[:, 0:12]:
                         e.bn_aggr(out=o, in_=i),
                         reads=[st["bn_a"], st["bn_b"]], writes=[st["mv"]])
                    P.op("vector", lambda e, o=tt[:, 14:15], i=tt[:, 13:14]:
                         e.tensor_scalar(out=o, in0=i, scalar1=EPS, scalar2=None, op0=ALU.add),
                         reads=[st["mv"]], writes=[st["ve"]])
                    P.op("scalar", lambda e, o=tt[:, 15:16], i=tt[:, 14:15]:
                         e.activation(out=o, in_=i, func=AF.Ln),
                         reads=[st["ve"]], writes=[st["lnv"]])
                    P.op("scalar", lambda e, o=tt[:, 16:17], i=tt[:, 15:16]:
                         e.activation(out=o, in_=i, func=AF.Exp, scale=-0.5),
                         reads=[st["lnv"]], writes=[st["rstd"]])
                    P.op("vector", lambda e, o=yt[par][:], i=acc[:, t * D:(t + 1) * D],
                         m=tt[:, 12:13], r=tt[:, 16:17]:
                         e.tensor_scalar(out=o, in0=i, scalar1=m, scalar2=r,
                                         op0=ALU.subtract, op1=ALU.mult),
                         reads=[acc_tb[t], st["mv"], st["rstd"]], writes=[yt_b[par]])
                    P.op("vector", lambda e, o=yt[par][:], i=x1td[par][:]:
                         e.tensor_tensor(out=o, in0=o, in1=i, op=ALU.add),
                         reads=[x1_b[par]], writes=[yt_b[par]])
                    P.dma("qs", Yd[seq, gt * 128:(gt + 1) * 128, :], yt[par][:],
                          reads=[yt_b[par]])

                for ch in range(NCH):
                    base = ch * CHT
                    P.dma("qs", x1td[0][:], x1s[seq, base * 128:(base + 1) * 128, :],
                          writes=[x1_b[0]])
                    emit_ln(P, stC[0], x1td[0][:], [x1_b[0]], u2t[0][:], u2_b[0])
                    for t in range(CHT):
                        par = t % 2
                        gt = base + t
                        if t + 1 < CHT:
                            P.dma("qs", x1td[1 - par][:],
                                  x1s[seq, (gt + 1) * 128:(gt + 2) * 128, :],
                                  writes=[x1_b[1 - par]])
                            emit_ln(P, stC[1 - par], x1td[1 - par][:], [x1_b[1 - par]],
                                    u2t[1 - par][:], u2_b[1 - par])
                        P.op("tensor", tr8_emit(psT, u2t[par][:], identb[:]),
                             reads=[u2_b[par], identb_b], writes=[psT_b])
                        P.op("scalar", lambda e, o=u2T[:, t * 1024:(t + 1) * 1024], i=psT[:]:
                             e.copy(out=o, in_=i),
                             reads=[psT_b], writes=[u2T_tb[t]])
                        P.op("scalar", lambda e, o=acc[:, t * D:(t + 1) * D], i=u2t[par][:]:
                             e.copy(out=o, in_=i),
                             reads=[u2_b[par]], writes=[acc_tb[t]])
                        if t % 4 == 3:
                            t0 = t - 3

                            def rt4_emit(e, t0=t0):
                                for tt_ in range(t0, t0 + 4):
                                    sl = (tt_ - t0) * NE
                                    for c in range(DC):
                                        e.matmul(psS0[:, sl:sl + NE],
                                                 u2T[:, tt_ * 1024 + c * 128: tt_ * 1024 + (c + 1) * 128],
                                                 CTs[:, c * NE:(c + 1) * NE],
                                                 start=(c == 0), stop=False)
                                    last = e.matmul(psS0[:, sl:sl + NE], ones1[:], ebr[:],
                                                    start=False, stop=True)
                                return last
                            P.op("tensor", rt4_emit,
                                 reads=[u2T_tb[t0], u2T_tb[t0 + 1], u2T_tb[t0 + 2],
                                        u2T_tb[t0 + 3], CT_b, ones1_b, eb_b],
                                 writes=[psS0_b])
                            P.op("scalar", lambda e, o=rl_ch[:, t0 * NE:(t0 + 4) * NE],
                                 i=psS0[:, 0:4 * NE]: e.copy(out=o, in_=i),
                                 reads=[psS0_b], writes=[rl_tb[t0 // 4]])
                            for tt_ in range(t0, t0 + 4):
                                pr2 = tt_ % 2
                                P.op("vector", lambda e, o=gsm[pr2][:, 0:1],
                                     i=rl_ch[:, tt_ * NE:(tt_ + 1) * NE]:
                                     e.reduce_max(out=o, in_=i, axis=AX.X, negate=True),
                                     reads=[rl_tb[t0 // 4]], writes=[nbr_b[pr2]])
                                P.op("scalar", lambda e, o=sdash_ch[:, tt_ * NE:(tt_ + 1) * NE],
                                     i=rl_ch[:, tt_ * NE:(tt_ + 1) * NE], b=gsm[pr2][:, 0:1]:
                                     e.activation(out=o, in_=i, func=AF.Exp, bias=b),
                                     reads=[rl_tb[t0 // 4], nbr_b[pr2]], writes=[sdc_tb[tt_]])
                    for t in range(CHT):
                        par = t % 2
                        sd_ap = sdash_ch[:, t * NE:(t + 1) * NE]
                        P.op("vector", lambda e, o=gsm[par][:, 1:2], i=sd_ap:
                             e.reduce_sum(out=o, in_=i, axis=AX.X),
                             reads=[sdc_tb[t]], writes=[zr_b[par]])
                        P.op("vector", lambda e, o=gsm[par][:, 2:3], i=gsm[par][:, 1:2]:
                             e.reciprocal(out=o, in_=i),
                             reads=[zr_b[par]], writes=[rz_b[par]])
                        P.op("vector", lambda e, o=gsm[par][:, 3:4], i=sd_ap:
                             e.reduce_max(out=o, in_=i, axis=AX.X),
                             reads=[sdc_tb[t]], writes=[m1_b[par]])
                        P.op("vector", lambda e, o=gtmp[par][:], i=sd_ap, m=gsm[par][:, 3:4]:
                             e.tensor_scalar(out=o, in0=i, scalar1=m, scalar2=-2.0,
                                             op0=ALU.is_ge, op1=ALU.mult),
                             reads=[sdc_tb[t], m1_b[par]], writes=[gtm_b[par]])
                        P.op("vector", lambda e, o=gtmp[par][:], i=sd_ap:
                             e.tensor_tensor(out=o, in0=i, in1=o, op=ALU.add),
                             reads=[sdc_tb[t]], writes=[gtm_b[par]])
                        P.op("vector", lambda e, o=gsm[par][:, 4:5], i=gtmp[par][:]:
                             e.reduce_max(out=o, in_=i, axis=AX.X),
                             reads=[gtm_b[par]], writes=[m2_b[par]])
                        P.op("vector", lambda e, o=gtmp[par][:], i=sd_ap, m=gsm[par][:, 4:5]:
                             e.tensor_scalar(out=o, in0=i, scalar1=m, scalar2=None, op0=ALU.is_ge),
                             reads=[sdc_tb[t], m2_b[par]], writes=[gtm_b[par]])
                        P.op("vector", lambda e, o=g_ch[:, t * NE:(t + 1) * NE], i=sd_ap,
                             r=gsm[par][:, 2:3], m=gtmp[par][:]:
                             e.scalar_tensor_tensor(out=o, in0=i, scalar=r, in1=m,
                                                    op0=ALU.mult, op1=ALU.mult),
                             reads=[sdc_tb[t], rz_b[par], gtm_b[par]], writes=[g_tb[t]])
                        if t % 4 == 3:
                            t0 = t - 3

                            def gt4_emit(e, t0=t0):
                                for tt_ in range(t0, t0 + 4):
                                    last = e.transpose(
                                        psS1[0:16, (tt_ - t0) * 128:(tt_ - t0 + 1) * 128],
                                        g_ch[:, tt_ * NE:(tt_ + 1) * NE], identf[:])
                                return last
                            P.op("tensor", gt4_emit,
                                 reads=[g_tb[t0], g_tb[t0 + 1], g_tb[t0 + 2], g_tb[t0 + 3],
                                        identf_b],
                                 writes=[psS1_b])
                            P.op("vector", lambda e, o=gT_ch[0:16, t0 * 128:(t0 + 4) * 128],
                                 i=psS1[0:16, 0:512]: e.tensor_copy(out=o, in_=i),
                                 reads=[psS1_b], writes=[gT_tb[t0 // 4]])
                    P.dma("qg", wr[0][:].rearrange("p (c n) -> p c n", n=1024),
                          WrTd[0].rearrange("(c p) n -> p c n", p=128), writes=[wr_b[0]])
                    P.dma("qg", wr[1][:].rearrange("p (c n) -> p c n", n=1024),
                          WrTd[1].rearrange("(c p) n -> p c n", p=128), writes=[wr_b[1]])
                    for t in range(CHT):
                        psx, psx_b = (psA, psA_b) if t % 2 == 0 else (psB, psB_b)

                        def ai_emit(e, ps=psx[:], t=t):
                            for c in range(DC):
                                for n in range(2):
                                    e.matmul(ps[:, n * 512:(n + 1) * 512],
                                             u2T[:, t * 1024 + c * 128: t * 1024 + (c + 1) * 128],
                                             Wsh[:, c * 1024 + n * 512: c * 1024 + n * 512 + 512],
                                             start=(c == 0), stop=False)
                            for n in range(2):
                                e.matmul(ps[:, n * 512:(n + 1) * 512],
                                         gT_ch[0:16, t * 128:(t + 1) * 128],
                                         brs[:, n * 512:n * 512 + 512], start=False, stop=False)
                                e.matmul(ps[:, n * 512:(n + 1) * 512], halves[:],
                                         bs0r[:, n * 512:n * 512 + 512], start=False, stop=False)
                                last = e.matmul(ps[:, n * 512:(n + 1) * 512], halves[:],
                                                bs1r[:, n * 512:n * 512 + 512],
                                                start=False, stop=True)
                            return last
                        P.op("tensor", ai_emit,
                             reads=[u2T_tb[t], Wsh_b, gT_tb[t // 4], br_b, halves_b,
                                    bs0_b, bs1_b],
                             writes=[psx_b])
                        P.op("vector", lambda e, o=acc[:, t * D:(t + 1) * D], i=psx[:]:
                             e.tensor_tensor(out=o, in0=i, in1=o, op=ALU.add),
                             reads=[psx_b], writes=[acc_tb[t]])
                    for e_i in range(NE):
                        sl = e_i % 2
                        for t in range(CHT):
                            psx, psx_b = (psA, psA_b) if t % 2 == 0 else (psB, psB_b)

                            def ex_emit(e, ps=psx[:], t=t, sl=sl):
                                for c in range(DC):
                                    for n in range(2):
                                        last = e.matmul(
                                            ps[:, n * 512:(n + 1) * 512],
                                            u2T[:, t * 1024 + c * 128: t * 1024 + (c + 1) * 128],
                                            wr[sl][:, c * 1024 + n * 512: c * 1024 + n * 512 + 512],
                                            start=(c == 0), stop=(c == DC - 1))
                                return last
                            P.op("tensor", ex_emit,
                                 reads=[u2T_tb[t], wr_b[sl]], writes=[psx_b])
                            P.op("vector", lambda e, o=acc[:, t * D:(t + 1) * D], i=psx[:],
                                 gsc=g_ch[:, t * NE + e_i: t * NE + e_i + 1]:
                                 e.scalar_tensor_tensor(out=o, in0=i, scalar=gsc, in1=o,
                                                        op0=ALU.mult, op1=ALU.add),
                                 reads=[psx_b, g_tb[t]], writes=[acc_tb[t]])
                            if e_i == NE - 1:
                                emit_final(t, ch)
                        if e_i + 2 < NE:
                            P.dma("qg", wr[sl][:].rearrange("p (c n) -> p c n", n=1024),
                                  WrTd[e_i + 2].rearrange("(c p) n -> p c n", p=128),
                                  writes=[wr_b[sl]])

                    if f"acc{seq}_{ch}" in taps:
                        tap(P, f"acc{seq}_{ch}", acc[:], acc_tb, [128, CHT * D], F32)
                    if f"g{seq}_{ch}" in taps:
                        tap(P, f"g{seq}_{ch}", g_ch[:], g_tb, [128, CHT * NE], F32)
                    P.flush()

        P.barrier()
        P.flush()
    return nc, tap_outs


def _host_inputs(inputs):
    f32 = np.float32
    bf16 = ml_dtypes.bfloat16
    X = np.ascontiguousarray(np.asarray(inputs["L0_attn_X"], f32))
    shared = {
        "Wq": np.ascontiguousarray(np.asarray(inputs["L0_attn_Wq"], f32)),
        "Wk": np.ascontiguousarray(np.asarray(inputs["L0_attn_Wk"], f32)),
        "Wv": np.ascontiguousarray(np.asarray(inputs["L0_attn_Wv"], f32)),
        "bq": np.ascontiguousarray(np.asarray(inputs["L0_attn_bq"], f32).reshape(1, D)),
        "bk": np.ascontiguousarray(np.asarray(inputs["L0_attn_bk"], f32).reshape(1, D)),
        "bv": np.ascontiguousarray(np.asarray(inputs["L0_attn_bv"], f32).reshape(1, D)),
        "CT": np.ascontiguousarray(np.asarray(inputs["L0_ffn_centroids"], f32).T),
        "eb": np.ascontiguousarray(np.asarray(inputs["L0_ffn_expert_biases"], f32).reshape(1, NE)),
        "WrT": np.ascontiguousarray(np.asarray(inputs["L0_ffn_Wr"], f32).transpose(0, 2, 1)),
        "br": np.ascontiguousarray(np.asarray(inputs["L0_ffn_br"], f32)),
        "WsT": np.ascontiguousarray(np.asarray(inputs["L0_ffn_Ws"], f32).transpose(0, 2, 1)),
        "bs": np.ascontiguousarray(np.asarray(inputs["L0_ffn_bs"], f32)),
        "identb": np.eye(128, dtype=bf16),
        "identf": np.eye(128, dtype=f32),
        "mask": np.triu(np.full((128, 128), -1.0e30, f32), 1),
        "ones1": np.ones((1, 128), dtype=bf16),
        "halves": np.full((1, 128), 0.5, dtype=bf16),
    }
    in_maps = []
    for c in range(NCORES):
        m = dict(shared)
        m["X"] = np.ascontiguousarray(X[c * SPC:(c + 1) * SPC])
        in_maps.append(m)
    return in_maps


def kernel(**inputs):
    in_maps = _host_inputs(inputs)
    nc, _ = build()
    res = run_bass_kernel_spmd(nc, in_maps, core_ids=list(range(NCORES)))
    out = np.concatenate([np.asarray(res.results[c]["Y"]) for c in range(NCORES)], axis=0)
    return out.astype(np.float32, copy=False)


# version 07
# speedup vs baseline: 1.4578x; 1.0003x over previous
import contextlib
import numpy as np
import ml_dtypes
import concourse.bass as bass
import concourse.mybir as mybir
from concourse.bass_utils import run_bass_kernel_spmd

F32 = mybir.dt.float32
BF16 = mybir.dt.bfloat16
AF = mybir.ActivationFunctionType
ALU = mybir.AluOpType
AX = mybir.AxisListType

B, S, D = 16, 4096, 1024
NCORES = 8
SPC = B // NCORES
NT = S // 128
DC = D // 128
NE = 16
CHT = 16
NCH = NT // CHT
EPS = 1e-5
ISC = 1.0 / 32.0

ENG2SEM = {"tensor": "t", "vector": "v", "scalar": "s", "gpsimd": "g"}
QENG = {"qs": "sync", "qc": "scalar", "qg": "gpsimd"}


class Buf:
    __slots__ = ("w", "rs")

    def __init__(self):
        self.w = None
        self.rs = []


class Prog:
    def __init__(self, nc, ctx):
        self.nc = nc
        self.pend = {e: [] for e in ("sync", "tensor", "vector", "scalar", "gpsimd")}
        self.sems = {}
        self.cnt = {}
        self.hw = {e: {} for e in self.pend}
        for sn in ("t", "v", "s", "g", "qs", "qc", "qg"):
            self.sems[sn] = ctx.enter_context(nc.semaphore("sem_" + sn))
            self.cnt[sn] = 0
        self.nops = 0

    def _prune(self, eng, evs):
        best = {}
        for ev in evs:
            if ev is None:
                continue
            sn, val = ev
            if eng == "tensor" and sn == "t":
                continue
            if best.get(sn, 0) < val:
                best[sn] = val
        hw = self.hw[eng]
        out = []
        for sn, val in best.items():
            if hw.get(sn, 0) >= val:
                continue
            hw[sn] = val
            out.append((sn, val))
        return out

    def _gather(self, reads, writes, extra):
        evs = list(extra)
        for b in reads:
            evs.append(b.w)
        for b in writes:
            evs.append(b.w)
            evs.extend(b.rs)
        return evs

    def op(self, eng, emit, reads=(), writes=(), extra=()):
        waits = self._prune(eng, self._gather(reads, writes, extra))
        sn = ENG2SEM[eng]
        self.cnt[sn] += 1
        ev = (sn, self.cnt[sn])
        sems = self.sems

        def run(e, waits=waits, emit=emit, sem=sems[sn], sems=sems):
            for wn, wv in waits:
                e.wait_ge(sems[wn], wv)
            emit(e).then_inc(sem, 1)

        self.pend[eng].append(run)
        self.nops += 1
        for b in reads:
            b.rs.append(ev)
        for b in writes:
            b.w = ev
            b.rs = []
        return ev

    def dma(self, q, out, in_, reads=(), writes=(), extra=()):
        eng = QENG[q]
        waits = self._prune(eng, self._gather(reads, writes, extra))
        self.cnt[q] += 16
        ev = (q, self.cnt[q])
        sems = self.sems

        def run(e, waits=waits, out=out, in_=in_, sem=sems[q], sems=sems):
            for wn, wv in waits:
                e.wait_ge(sems[wn], wv)
            e.dma_start(out=out, in_=in_).then_inc(sem, 16)

        self.pend[eng].append(run)
        self.nops += 1
        for b in reads:
            b.rs.append(ev)
        for b in writes:
            b.w = ev
            b.rs = []
        return ev

    def barrier(self):
        cur = [(sn, v) for sn, v in self.cnt.items() if v > 0]
        sems = self.sems
        for eng in self.pend:
            waits = self._prune(eng, cur)
            if waits:
                def run(e, waits=waits, sems=sems):
                    for wn, wv in waits:
                        e.wait_ge(sems[wn], wv)
                self.pend[eng].append(run)

    def flush(self):
        pend = {e: lst for e, lst in self.pend.items() if lst}
        self.pend = {e: [] for e in self.pend}
        if not pend:
            return
        with self.nc.Block() as blk:
            for eng, lst in pend.items():
                getattr(blk, eng)(lambda e, lst=lst: [f(e) for f in lst])


def make_stats(ctx, nc, name):
    t = ctx.enter_context(nc.sbuf_tensor(name, [128, 18], F32))
    return dict(t=t, bn_a=Buf(), bn_b=Buf(), mv=Buf(), ve=Buf(), lnv=Buf(), rstd=Buf())


def emit_ln(P, st, src, src_bufs, dst, dst_buf, norm_eng="vector"):
    t = st["t"]
    P.op("vector", lambda e, o=t[:, 0:6], i=src[:, 0:512]: e.bn_stats(out=o, in_=i),
         reads=src_bufs, writes=[st["bn_a"]])
    P.op("vector", lambda e, o=t[:, 6:12], i=src[:, 512:1024]: e.bn_stats(out=o, in_=i),
         reads=src_bufs, writes=[st["bn_b"]])
    P.op("vector", lambda e, o=t[:, 12:14], i=t[:, 0:12]: e.bn_aggr(out=o, in_=i),
         reads=[st["bn_a"], st["bn_b"]], writes=[st["mv"]])
    P.op("vector", lambda e, o=t[:, 14:15], i=t[:, 13:14]:
         e.tensor_scalar(out=o, in0=i, scalar1=EPS, scalar2=None, op0=ALU.add),
         reads=[st["mv"]], writes=[st["ve"]])
    P.op("scalar", lambda e, o=t[:, 15:16], i=t[:, 14:15]:
         e.activation(out=o, in_=i, func=AF.Ln),
         reads=[st["ve"]], writes=[st["lnv"]])
    P.op("scalar", lambda e, o=t[:, 16:17], i=t[:, 15:16]:
         e.activation(out=o, in_=i, func=AF.Exp, scale=-0.5),
         reads=[st["lnv"]], writes=[st["rstd"]])
    P.op(norm_eng, lambda e, o=dst, i=src, m=t[:, 12:13], r=t[:, 16:17]:
         e.tensor_scalar(out=o, in0=i, scalar1=m, scalar2=r,
                         op0=ALU.subtract, op1=ALU.mult),
         reads=list(src_bufs) + [st["mv"], st["rstd"]], writes=[dst_buf])


def build(upto="full", taps=()):
    taps = set(taps)
    nc = bass.Bass("TRN2", target_bir_lowering=False)

    Xd = nc.dram_tensor("X", [SPC, S, D], F32, kind="ExternalInput").ap()
    Yd = nc.dram_tensor("Y", [SPC, S, D], F32, kind="ExternalOutput").ap()
    x1s = nc.dram_tensor("x1scr", [SPC, S, D], F32, kind="Internal").ap()
    uTs = nc.dram_tensor("uTscr", [SPC, NT, 128, DC * 128], BF16, kind="Internal").ap()
    wshs = nc.dram_tensor("wshscr", [128, DC * 1024], BF16, kind="Internal").ap()
    Wqd = nc.dram_tensor("Wq", [D, D], F32, kind="ExternalInput").ap()
    Wkd = nc.dram_tensor("Wk", [D, D], F32, kind="ExternalInput").ap()
    Wvd = nc.dram_tensor("Wv", [D, D], F32, kind="ExternalInput").ap()
    bqd = nc.dram_tensor("bq", [1, D], F32, kind="ExternalInput").ap()
    bkd = nc.dram_tensor("bk", [1, D], F32, kind="ExternalInput").ap()
    bvd = nc.dram_tensor("bv", [1, D], F32, kind="ExternalInput").ap()
    CTd = nc.dram_tensor("CT", [D, NE], F32, kind="ExternalInput").ap()
    ebd = nc.dram_tensor("eb", [1, NE], F32, kind="ExternalInput").ap()
    WrTd = nc.dram_tensor("WrT", [NE, D, D], F32, kind="ExternalInput").ap()
    brd = nc.dram_tensor("br", [NE, D], F32, kind="ExternalInput").ap()
    WsTd = nc.dram_tensor("WsT", [2, D, D], F32, kind="ExternalInput").ap()
    bsd = nc.dram_tensor("bs", [2, D], F32, kind="ExternalInput").ap()
    identbd = nc.dram_tensor("identb", [128, 128], BF16, kind="ExternalInput").ap()
    identfd = nc.dram_tensor("identf", [128, 128], F32, kind="ExternalInput").ap()
    maskd = nc.dram_tensor("mask", [128, 128], F32, kind="ExternalInput").ap()
    ones1d = nc.dram_tensor("ones1", [1, 128], BF16, kind="ExternalInput").ap()
    halvesd = nc.dram_tensor("halves", [1, 128], BF16, kind="ExternalInput").ap()

    tap_outs = {}

    def tap(P, name, ap, readbufs, shape, dtype):
        if name in taps:
            t = nc.dram_tensor("tap_" + name, list(shape), dtype, kind="ExternalOutput").ap()
            tap_outs[name] = t
            P.dma("qs", t, ap, reads=list(readbufs))

    with contextlib.ExitStack() as octx:
        P = Prog(nc, octx)
        OE = octx.enter_context

        identb = OE(nc.sbuf_tensor("identb_sb", [128, 128], BF16)); identb_b = Buf()
        ones1 = OE(nc.sbuf_tensor("ones1_sb", [1, 128], BF16)); ones1_b = Buf()
        halves = OE(nc.sbuf_tensor("halves_sb", [1, 128], BF16)); halves_b = Buf()
        wq = OE(nc.sbuf_tensor("wq_sb", [128, DC * 1024], BF16)); wq_b = Buf()
        bqr = OE(nc.sbuf_tensor("bqr_sb", [1, D], BF16)); bqr_b = Buf()
        maskt = OE(nc.sbuf_tensor("maskt_sb", [128, 128], F32)); mask_b = Buf()

        psA = OE(nc.psum_tensor("psA", [128, 1024], F32)); psA_b = Buf()
        psB = OE(nc.psum_tensor("psB", [128, 1024], F32)); psB_b = Buf()
        psT = OE(nc.psum_tensor("psT", [128, 1024], BF16)); psT_b = Buf()
        psT2 = OE(nc.psum_tensor("psT2", [128, 1024], BF16)); psT2_b = Buf()
        psS0 = OE(nc.psum_tensor("psS0", [128, 512], F32)); psS0_b = Buf()
        psS1 = OE(nc.psum_tensor("psS1", [128, 512], F32)); psS1_b = Buf()
        psS = [psS0, psS1]; psS_b = [psS0_b, psS1_b]

        P.dma("qs", identb[:], identbd, writes=[identb_b])
        P.dma("qs", ones1[:], ones1d, writes=[ones1_b])
        P.dma("qs", halves[:], halvesd, writes=[halves_b])
        P.dma("qs", maskt[:], maskd, writes=[mask_b])
        P.dma("qg", wq[:].rearrange("p (c n) -> p c n", n=1024),
              Wqd.rearrange("(c p) n -> p c n", p=128), writes=[wq_b])
        P.dma("qg", bqr[:], bqd, writes=[bqr_b])

        with contextlib.ExitStack() as wctx:
            wtmp = [wctx.enter_context(nc.sbuf_tensor(f"wtmpS{i}", [128, 1024], F32))
                    for i in range(4)]
            wtmp_b = [Buf() for _ in range(4)]
            wstg = [wctx.enter_context(nc.sbuf_tensor(f"wstgS{i}", [128, 1024], BF16))
                    for i in range(2)]
            wstg_b = [Buf(), Buf()]
            for c in range(DC):
                pr = c % 2
                P.dma("qs", wtmp[2 * pr][:], WsTd[0, c * 128:(c + 1) * 128, :],
                      writes=[wtmp_b[2 * pr]])
                P.dma("qc", wtmp[2 * pr + 1][:], WsTd[1, c * 128:(c + 1) * 128, :],
                      writes=[wtmp_b[2 * pr + 1]])
                P.op("vector", lambda e, a2=wtmp[2 * pr][:], b2=wtmp[2 * pr + 1][:]:
                     e.tensor_tensor(out=a2, in0=a2, in1=b2, op=ALU.add),
                     reads=[wtmp_b[2 * pr + 1]], writes=[wtmp_b[2 * pr]])
                P.op("vector", lambda e, o=wstg[pr][:], i=wtmp[2 * pr][:]:
                     e.tensor_scalar(out=o, in0=i, scalar1=0.5, scalar2=None, op0=ALU.mult),
                     reads=[wtmp_b[2 * pr]], writes=[wstg_b[pr]])
                P.dma("qc", wshs[:, c * 1024:(c + 1) * 1024], wstg[pr][:],
                      reads=[wstg_b[pr]])
            P.barrier()
            P.flush()

        for seq in range(SPC):
            with contextlib.ExitStack() as sctx:
                SE = sctx.enter_context
                KT = SE(nc.sbuf_tensor(f"KT{seq}", [128, DC * S], BF16))
                Vb = SE(nc.sbuf_tensor(f"V{seq}", [128, NT * D], BF16))
                kt_tb = [Buf() for _ in range(NT)]
                v_tb = [Buf() for _ in range(NT)]

                with contextlib.ExitStack() as actx:
                    AE = actx.enter_context
                    wk = AE(nc.sbuf_tensor(f"wk{seq}", [128, DC * 1024], BF16)); wk_b = Buf()
                    wv = AE(nc.sbuf_tensor(f"wv{seq}", [128, DC * 1024], BF16)); wv_b = Buf()
                    bkr = AE(nc.sbuf_tensor(f"bkr{seq}", [1, D], BF16)); bkr_b = Buf()
                    bvr = AE(nc.sbuf_tensor(f"bvr{seq}", [1, D], BF16)); bvr_b = Buf()
                    xt = [AE(nc.sbuf_tensor(f"xtA{seq}_{i}", [128, D], F32)) for i in range(2)]
                    xt_b = [Buf(), Buf()]
                    usb = [AE(nc.sbuf_tensor(f"uA{seq}_{i}", [128, D], BF16)) for i in range(2)]
                    u_b = [Buf(), Buf()]
                    uT = [AE(nc.sbuf_tensor(f"uTA{seq}_{i}", [128, DC * 128], BF16)) for i in range(2)]
                    uT_b = [Buf(), Buf()]
                    ktmp = [AE(nc.sbuf_tensor(f"ktmpA{seq}_{i}", [128, D], BF16)) for i in range(2)]
                    ktmp_b = [Buf(), Buf()]
                    stA = [make_stats(actx, nc, f"stA{seq}_{i}") for i in range(2)]

                    P.dma("qg", wk[:].rearrange("p (c n) -> p c n", n=1024),
                          Wkd.rearrange("(c p) n -> p c n", p=128), writes=[wk_b])
                    P.dma("qg", wv[:].rearrange("p (c n) -> p c n", n=1024),
                          Wvd.rearrange("(c p) n -> p c n", p=128), writes=[wv_b])
                    P.dma("qg", bkr[:], bkd, writes=[bkr_b])
                    P.dma("qg", bvr[:], bvd, writes=[bvr_b])
                    P.dma("qs", xt[0][:], Xd[seq, 0:128, :], writes=[xt_b[0]])

                    def proj_emit(ps, uTap, W, brow):
                        def em(e, ps=ps, uTap=uTap, W=W, brow=brow):
                            for c in range(DC):
                                for n in range(2):
                                    e.matmul(ps[:, n * 512:(n + 1) * 512],
                                             uTap[:, c * 128:(c + 1) * 128],
                                             W[:, c * 1024 + n * 512: c * 1024 + n * 512 + 512],
                                             start=(c == 0), stop=False)
                            for n in range(2):
                                last = e.matmul(ps[:, n * 512:(n + 1) * 512], ones1[:],
                                                brow[:, n * 512:n * 512 + 512],
                                                start=False, stop=True)
                            return last
                        return em

                    def tr8_emit(psd, src, ident):
                        def em(e, psd=psd, src=src, ident=ident):
                            for c in range(DC):
                                last = e.transpose(psd[:, c * 128:(c + 1) * 128],
                                                   src[:, c * 128:(c + 1) * 128], ident)
                            return last
                        return em

                    emit_ln(P, stA[0], xt[0][:], [xt_b[0]], usb[0][:], u_b[0])
                    for t in range(NT):
                        par = t % 2
                        if t + 1 < NT:
                            P.dma("qs", xt[1 - par][:], Xd[seq, (t + 1) * 128:(t + 2) * 128, :],
                                  writes=[xt_b[1 - par]])
                            emit_ln(P, stA[1 - par], xt[1 - par][:], [xt_b[1 - par]],
                                    usb[1 - par][:], u_b[1 - par])
                        P.op("tensor", tr8_emit(psT, usb[par][:], identb[:]),
                             reads=[u_b[par], identb_b], writes=[psT_b])
                        P.op("scalar", lambda e, o=uT[par][:], i=psT[:]: e.copy(out=o, in_=i),
                             reads=[psT_b], writes=[uT_b[par]])
                        P.dma("qc", uTs[seq, t], uT[par][:], reads=[uT_b[par]])
                        P.op("tensor", proj_emit(psA[:], uT[par][:], wk[:], bkr[:]),
                             reads=[uT_b[par], wk_b, ones1_b, bkr_b], writes=[psA_b])
                        P.op("scalar", lambda e, o=ktmp[par][:], i=psA[:]:
                             e.activation(out=o, in_=i, func=AF.Copy),
                             reads=[psA_b], writes=[ktmp_b[par]])
                        P.op("tensor", proj_emit(psB[:], uT[par][:], wv[:], bvr[:]),
                             reads=[uT_b[par], wv_b, ones1_b, bvr_b], writes=[psB_b])
                        P.op("tensor", tr8_emit(psT2, ktmp[par][:], identb[:]),
                             reads=[ktmp_b[par], identb_b], writes=[psT2_b])
                        P.op("scalar", lambda e,
                             o=KT[:].rearrange("p (c s) -> p c s", s=S)[:, :, t * 128:(t + 1) * 128],
                             i=psT2[:].rearrange("p (c q) -> p c q", q=128):
                             e.copy(out=o, in_=i),
                             reads=[psT2_b], writes=[kt_tb[t]])
                        P.op("scalar", lambda e, o=Vb[:, t * D:(t + 1) * D], i=psB[:]:
                             e.activation(out=o, in_=i, func=AF.Copy),
                             reads=[psB_b], writes=[v_tb[t]])

                    tap(P, f"KT{seq}", KT[:], kt_tb, [128, DC * S], BF16)
                    tap(P, f"V{seq}", Vb[:], v_tb, [128, NT * D], BF16)
                    P.barrier()
                    P.flush()

                if upto == "A":
                    P.barrier()
                    P.flush()
                    return nc, tap_outs

                with contextlib.ExitStack() as bctx:
                    BE = bctx.enter_context
                    scores = BE(nc.sbuf_tensor(f"scores{seq}", [128, S], F32))
                    sc_b = [Buf() for _ in range(8)]
                    prs = [BE(nc.sbuf_tensor(f"pr{seq}_{i}", [128, 1024], BF16)) for i in range(2)]
                    pr_b = [Buf(), Buf()]
                    PT = [BE(nc.sbuf_tensor(f"PT{seq}_{i}", [128, 1024], BF16)) for i in range(2)]
                    PT_b = [Buf(), Buf()]
                    xt = [BE(nc.sbuf_tensor(f"xtB{seq}_{i}", [128, D], F32)) for i in range(3)]
                    xt_b = [Buf(), Buf(), Buf()]
                    uTt = [BE(nc.sbuf_tensor(f"uTt{seq}_{i}", [128, DC * 128], BF16)) for i in range(2)]
                    uTt_b = [Buf(), Buf()]
                    Qsb = [BE(nc.sbuf_tensor(f"Q{seq}_{i}", [128, D], BF16)) for i in range(2)]
                    Q_b = [Buf(), Buf()]
                    QT = [BE(nc.sbuf_tensor(f"QT{seq}_{i}", [128, DC * 128], BF16)) for i in range(2)]
                    QT_b = [Buf(), Buf()]
                    att = BE(nc.sbuf_tensor(f"att{seq}", [128, D], F32)); att_b = Buf()
                    sm = [BE(nc.sbuf_tensor(f"smB{seq}_{i}", [128, 16], F32)) for i in range(2)]
                    pmg_b = [[Buf() for _ in range(8)], [Buf() for _ in range(8)]]
                    rm_b = [Buf(), Buf()]; nb_b = [Buf(), Buf()]
                    part_b = [Buf(), Buf()]; rs_b = [Buf(), Buf()]; rinv_b = [Buf(), Buf()]
                    stB2 = [make_stats(bctx, nc, f"stB2{seq}_{i}") for i in range(2)]

                    def frontLN(i):
                        P.dma("qs", xt[i % 3][:], Xd[seq, i * 128:(i + 1) * 128, :],
                              writes=[xt_b[i % 3]])
                        P.dma("qc", uTt[i % 2][:], uTs[seq, i], writes=[uTt_b[i % 2]])

                    def frontQ(i):
                        par = i % 2
                        P.op("tensor", proj_emit(psB[:], uTt[par][:], wq[:], bqr[:]),
                             reads=[uTt_b[par], wq_b, ones1_b, bqr_b], writes=[psB_b])
                        P.op("scalar", lambda e, o=Qsb[par][:], i=psB[:]:
                             e.activation(out=o, in_=i, func=AF.Copy),
                             reads=[psB_b], writes=[Q_b[par]])
                        P.op("tensor", tr8_emit(psT, Qsb[par][:], identb[:]),
                             reads=[Q_b[par], identb_b], writes=[psT_b])
                        P.op("vector", lambda e, o=QT[par][:], i=psT[:]: e.tensor_copy(out=o, in_=i),
                             reads=[psT_b], writes=[QT_b[par]])

                    frontLN(0)
                    frontLN(1)
                    frontQ(0)
                    for i in range(NT):
                        par = i % 2
                        keys = (i + 1) * 128
                        ngrp = (keys + 511) // 512
                        for g in range(ngrp):
                            w = min(512, keys - g * 512)
                            gs = g % 2

                            def sc_emit(e, ps=psS[gs], qt=QT[par][:], g=g, w=w):
                                for c in range(DC):
                                    last = e.matmul(
                                        ps[:, 0:w], qt[:, c * 128:(c + 1) * 128],
                                        KT[:, c * S + g * 512: c * S + g * 512 + w],
                                        start=(c == 0), stop=(c == DC - 1))
                                return last
                            P.op("tensor", sc_emit,
                                 reads=[QT_b[par]] + kt_tb[g * 4:g * 4 + (w + 127) // 128],
                                 writes=[psS_b[gs]])
                            if g < ngrp - 1:
                                P.op("scalar", lambda e, o=scores[:, g * 512:g * 512 + w],
                                     i=psS[gs][:, 0:w]: e.activation(out=o, in_=i, func=AF.Copy),
                                     reads=[psS_b[gs]], writes=[sc_b[g]])
                            else:
                                lead = w - 128
                                if lead:
                                    P.op("scalar", lambda e, o=scores[:, g * 512:g * 512 + lead],
                                         i=psS[gs][:, 0:lead]: e.activation(out=o, in_=i, func=AF.Copy),
                                         reads=[psS_b[gs]], writes=[sc_b[g]])
                                P.op("vector", lambda e, o=scores[:, g * 512 + lead:g * 512 + w],
                                     i=psS[gs][:, lead:w], m=maskt[:]:
                                     e.tensor_tensor(out=o, in0=i, in1=m, op=ALU.add),
                                     reads=[psS_b[gs], mask_b], writes=[sc_b[g]])
                            P.op("vector", lambda e, o=sm[par][:, g:g + 1],
                                 i=scores[:, g * 512:g * 512 + w]:
                                 e.reduce_max(out=o, in_=i, axis=AX.X),
                                 reads=[sc_b[g]], writes=[pmg_b[par][g]])
                        P.op("vector", lambda e, o=sm[par][:, 8:9], i=sm[par][:, 0:ngrp]:
                             e.reduce_max(out=o, in_=i, axis=AX.X),
                             reads=pmg_b[par][0:ngrp], writes=[rm_b[par]])
                        P.op("vector", lambda e, o=sm[par][:, 9:10], i=sm[par][:, 8:9]:
                             e.tensor_scalar(out=o, in0=i, scalar1=-ISC, scalar2=None, op0=ALU.mult),
                             reads=[rm_b[par]], writes=[nb_b[par]])
                        nchunk = (keys + 1023) // 1024
                        for ch in range(nchunk):
                            w2 = min(1024, keys - ch * 1024)
                            ps_ = ch % 2
                            P.op("scalar", lambda e, o=prs[ps_][:, 0:w2],
                                 i=scores[:, ch * 1024:ch * 1024 + w2], b=sm[par][:, 9:10]:
                                 e.activation(out=o, in_=i, func=AF.Exp, bias=b, scale=ISC),
                                 reads=sc_b[2 * ch:2 * ch + (w2 + 511) // 512] + [nb_b[par]],
                                 writes=[pr_b[ps_]])
                            P.op("vector", lambda e, o=sm[par][:, 10 + ch:11 + ch], i=prs[ps_][:, 0:w2]:
                                 e.reduce_sum(out=o, in_=i, axis=AX.X),
                                 reads=[pr_b[ps_]], writes=[part_b[par]])
                            if ch == 0:
                                if i + 2 < NT:
                                    frontLN(i + 2)
                                if i + 1 < NT:
                                    frontQ(i + 1)
                            nkt = w2 // 128

                            def ptr_emit(e, psd=psT2, src=prs[ps_][:], nkt=nkt):
                                for c in range(nkt):
                                    last = e.transpose(psd[:, c * 128:(c + 1) * 128],
                                                       src[:, c * 128:(c + 1) * 128], identb[:])
                                return last
                            P.op("tensor", ptr_emit, reads=[pr_b[ps_], identb_b], writes=[psT2_b])
                            P.op("vector", lambda e, o=PT[ps_][:, 0:nkt * 128], i=psT2[:, 0:nkt * 128]:
                                 e.tensor_copy(out=o, in_=i),
                                 reads=[psT2_b], writes=[PT_b[ps_]])

                            def pv_emit(e, pt=PT[ps_][:], ch=ch, nkt=nkt, i=i):
                                for jj in range(nkt):
                                    j = ch * 8 + jj
                                    for n in range(2):
                                        last = e.matmul(
                                            psA[:, n * 512:(n + 1) * 512],
                                            pt[:, jj * 128:(jj + 1) * 128],
                                            Vb[:, j * D + n * 512: j * D + n * 512 + 512],
                                            start=(j == 0), stop=(j == i))
                                return last
                            P.op("tensor", pv_emit,
                                 reads=[PT_b[ps_]] + v_tb[ch * 8:ch * 8 + nkt],
                                 writes=[psA_b])
                        P.op("vector", lambda e, o=sm[par][:, 14:15], i=sm[par][:, 10:10 + nchunk]:
                             e.reduce_sum(out=o, in_=i, axis=AX.X),
                             reads=[part_b[par]], writes=[rs_b[par]])
                        P.op("vector", lambda e, o=sm[par][:, 15:16], i=sm[par][:, 14:15]:
                             e.reciprocal(out=o, in_=i),
                             reads=[rs_b[par]], writes=[rinv_b[par]])
                        P.op("scalar", lambda e, o=att[:], i=psA[:], r=sm[par][:, 15:16]:
                             e.activation(out=o, in_=i, func=AF.Copy, scale=r),
                             reads=[psA_b, rinv_b[par]], writes=[att_b])
                        emit_ln(P, stB2[par], att[:], [att_b], att[:], att_b)
                        P.op("vector", lambda e, o=att[:], i=xt[i % 3][:]:
                             e.tensor_tensor(out=o, in0=o, in1=i, op=ALU.add),
                             reads=[xt_b[i % 3]], writes=[att_b])
                        P.dma("qs", x1s[seq, i * 128:(i + 1) * 128, :], att[:], reads=[att_b])

                    tap(P, f"x1_{seq}", x1s[seq], [], [S, D], F32)
                    P.barrier()
                    P.flush()

            if upto == "B":
                P.barrier()
                P.flush()
                return nc, tap_outs

            with contextlib.ExitStack() as cctx:
                CE = cctx.enter_context
                acc = CE(nc.sbuf_tensor(f"acc{seq}", [128, CHT * D], F32))
                acc_tb = [Buf() for _ in range(CHT)]
                u2T = CE(nc.sbuf_tensor(f"u2T{seq}", [128, CHT * DC * 128], BF16))
                u2T_tb = [Buf() for _ in range(CHT)]
                wr = [CE(nc.sbuf_tensor(f"wr{seq}_{i}", [128, DC * 1024], BF16)) for i in range(2)]
                wr_b = [Buf(), Buf()]
                x1td = [CE(nc.sbuf_tensor(f"x1td{seq}_{i}", [128, D], F32)) for i in range(2)]
                x1_b = [Buf(), Buf()]
                u2t = [CE(nc.sbuf_tensor(f"u2t{seq}_{i}", [128, D], BF16)) for i in range(2)]
                u2_b = [Buf(), Buf()]
                yt = [CE(nc.sbuf_tensor(f"yt{seq}_{i}", [128, D], F32)) for i in range(2)]
                yt_b = [Buf(), Buf()]
                g_ch = CE(nc.sbuf_tensor(f"g_ch{seq}", [128, CHT * NE], F32))
                g_tb = [Buf() for _ in range(CHT)]
                gT_ch = CE(nc.sbuf_tensor(f"gT_ch{seq}", [16, CHT * 128], BF16))
                gT_tb = [Buf() for _ in range(CHT // 4)]
                rl_ch = CE(nc.sbuf_tensor(f"rl_ch{seq}", [128, CHT * NE], F32))
                rl_tb = [Buf() for _ in range(CHT // 4)]
                sdash_ch = CE(nc.sbuf_tensor(f"sdash_ch{seq}", [128, CHT * NE], F32))
                sdc_tb = [Buf() for _ in range(CHT)]
                gtmp = [CE(nc.sbuf_tensor(f"gtmp{seq}_{i}", [128, NE], F32)) for i in range(2)]
                gtm_b = [Buf(), Buf()]
                gsm = [CE(nc.sbuf_tensor(f"gsm{seq}_{i}", [128, 6], F32)) for i in range(2)]
                nbr_b = [Buf(), Buf()]; zr_b = [Buf(), Buf()]; rz_b = [Buf(), Buf()]
                m1_b = [Buf(), Buf()]; m2_b = [Buf(), Buf()]
                CTs = CE(nc.sbuf_tensor(f"CTs{seq}", [128, DC * NE], BF16)); CT_b = Buf()
                brs = CE(nc.sbuf_tensor(f"brs{seq}", [16, D], BF16)); br_b = Buf()
                ebr = CE(nc.sbuf_tensor(f"ebr{seq}", [1, NE], BF16)); eb_b = Buf()
                bs0r = CE(nc.sbuf_tensor(f"bs0r{seq}", [1, D], BF16)); bs0_b = Buf()
                bs1r = CE(nc.sbuf_tensor(f"bs1r{seq}", [1, D], BF16)); bs1_b = Buf()
                identf = CE(nc.sbuf_tensor(f"identf_sb{seq}", [128, 128], F32)); identf_b = Buf()
                Wsh = CE(nc.sbuf_tensor(f"Wsh{seq}", [128, DC * 1024], BF16)); Wsh_b = Buf()
                stC = [make_stats(cctx, nc, f"stC{seq}_{i}") for i in range(2)]

                P.dma("qg", CTs[:].rearrange("p (c e) -> p c e", e=NE),
                      CTd.rearrange("(c p) e -> p c e", p=128), writes=[CT_b])
                P.dma("qg", brs[:], brd, writes=[br_b])
                P.dma("qg", ebr[:], ebd, writes=[eb_b])
                P.dma("qg", bs0r[:], bsd[0:1, :], writes=[bs0_b])
                P.dma("qg", bs1r[:], bsd[1:2, :], writes=[bs1_b])
                P.dma("qs", identf[:], identfd, writes=[identf_b])
                P.dma("qc", Wsh[:], wshs, writes=[Wsh_b])

                def emit_final(t, ch):
                    par = t % 2
                    gt = ch * CHT + t
                    P.dma("qs", x1td[par][:], x1s[seq, gt * 128:(gt + 1) * 128, :],
                          writes=[x1_b[par]])
                    st = stC[par]
                    tt = st["t"]
                    P.op("vector", lambda e, o=tt[:, 0:6], i=acc[:, t * D:t * D + 512]:
                         e.bn_stats(out=o, in_=i), reads=[acc_tb[t]], writes=[st["bn_a"]])
                    P.op("vector", lambda e, o=tt[:, 6:12], i=acc[:, t * D + 512:t * D + 1024]:
                         e.bn_stats(out=o, in_=i), reads=[acc_tb[t]], writes=[st["bn_b"]])
                    P.op("vector", lambda e, o=tt[:, 12:14], i=tt[:, 0:12]:
                         e.bn_aggr(out=o, in_=i),
                         reads=[st["bn_a"], st["bn_b"]], writes=[st["mv"]])
                    P.op("vector", lambda e, o=tt[:, 14:15], i=tt[:, 13:14]:
                         e.tensor_scalar(out=o, in0=i, scalar1=EPS, scalar2=None, op0=ALU.add),
                         reads=[st["mv"]], writes=[st["ve"]])
                    P.op("scalar", lambda e, o=tt[:, 15:16], i=tt[:, 14:15]:
                         e.activation(out=o, in_=i, func=AF.Ln),
                         reads=[st["ve"]], writes=[st["lnv"]])
                    P.op("scalar", lambda e, o=tt[:, 16:17], i=tt[:, 15:16]:
                         e.activation(out=o, in_=i, func=AF.Exp, scale=-0.5),
                         reads=[st["lnv"]], writes=[st["rstd"]])
                    P.op("vector", lambda e, o=yt[par][:], i=acc[:, t * D:(t + 1) * D],
                         m=tt[:, 12:13], r=tt[:, 16:17]:
                         e.tensor_scalar(out=o, in0=i, scalar1=m, scalar2=r,
                                         op0=ALU.subtract, op1=ALU.mult),
                         reads=[acc_tb[t], st["mv"], st["rstd"]], writes=[yt_b[par]])
                    P.op("vector", lambda e, o=yt[par][:], i=x1td[par][:]:
                         e.tensor_tensor(out=o, in0=o, in1=i, op=ALU.add),
                         reads=[x1_b[par]], writes=[yt_b[par]])
                    P.dma("qs", Yd[seq, gt * 128:(gt + 1) * 128, :], yt[par][:],
                          reads=[yt_b[par]])

                for ch in range(NCH):
                    base = ch * CHT
                    P.dma("qs", x1td[0][:], x1s[seq, base * 128:(base + 1) * 128, :],
                          writes=[x1_b[0]])
                    emit_ln(P, stC[0], x1td[0][:], [x1_b[0]], u2t[0][:], u2_b[0])
                    for t in range(CHT):
                        par = t % 2
                        gt = base + t
                        if t + 1 < CHT:
                            P.dma("qs", x1td[1 - par][:],
                                  x1s[seq, (gt + 1) * 128:(gt + 2) * 128, :],
                                  writes=[x1_b[1 - par]])
                            emit_ln(P, stC[1 - par], x1td[1 - par][:], [x1_b[1 - par]],
                                    u2t[1 - par][:], u2_b[1 - par])
                        P.op("tensor", tr8_emit(psT, u2t[par][:], identb[:]),
                             reads=[u2_b[par], identb_b], writes=[psT_b])
                        P.op("scalar", lambda e, o=u2T[:, t * 1024:(t + 1) * 1024], i=psT[:]:
                             e.copy(out=o, in_=i),
                             reads=[psT_b], writes=[u2T_tb[t]])
                        P.op("scalar", lambda e, o=acc[:, t * D:(t + 1) * D], i=u2t[par][:]:
                             e.copy(out=o, in_=i),
                             reads=[u2_b[par]], writes=[acc_tb[t]])
                        if t % 4 == 3:
                            t0 = t - 3

                            def rt4_emit(e, t0=t0):
                                for tt_ in range(t0, t0 + 4):
                                    sl = (tt_ - t0) * NE
                                    for c in range(DC):
                                        e.matmul(psS0[:, sl:sl + NE],
                                                 u2T[:, tt_ * 1024 + c * 128: tt_ * 1024 + (c + 1) * 128],
                                                 CTs[:, c * NE:(c + 1) * NE],
                                                 start=(c == 0), stop=False)
                                    last = e.matmul(psS0[:, sl:sl + NE], ones1[:], ebr[:],
                                                    start=False, stop=True)
                                return last
                            P.op("tensor", rt4_emit,
                                 reads=[u2T_tb[t0], u2T_tb[t0 + 1], u2T_tb[t0 + 2],
                                        u2T_tb[t0 + 3], CT_b, ones1_b, eb_b],
                                 writes=[psS0_b])
                            P.op("scalar", lambda e, o=rl_ch[:, t0 * NE:(t0 + 4) * NE],
                                 i=psS0[:, 0:4 * NE]: e.copy(out=o, in_=i),
                                 reads=[psS0_b], writes=[rl_tb[t0 // 4]])
                            for tt_ in range(t0, t0 + 4):
                                pr2 = tt_ % 2
                                P.op("vector", lambda e, o=gsm[pr2][:, 0:1],
                                     i=rl_ch[:, tt_ * NE:(tt_ + 1) * NE]:
                                     e.reduce_max(out=o, in_=i, axis=AX.X, negate=True),
                                     reads=[rl_tb[t0 // 4]], writes=[nbr_b[pr2]])
                                P.op("scalar", lambda e, o=sdash_ch[:, tt_ * NE:(tt_ + 1) * NE],
                                     i=rl_ch[:, tt_ * NE:(tt_ + 1) * NE], b=gsm[pr2][:, 0:1]:
                                     e.activation(out=o, in_=i, func=AF.Exp, bias=b),
                                     reads=[rl_tb[t0 // 4], nbr_b[pr2]], writes=[sdc_tb[tt_]])
                    P.dma("qg", wr[0][:].rearrange("p (c n) -> p c n", n=1024),
                          WrTd[0].rearrange("(c p) n -> p c n", p=128), writes=[wr_b[0]])
                    P.dma("qg", wr[1][:].rearrange("p (c n) -> p c n", n=1024),
                          WrTd[1].rearrange("(c p) n -> p c n", p=128), writes=[wr_b[1]])

                    def emit_accinit(t):
                        psx, psx_b = (psA, psA_b) if t % 2 == 0 else (psB, psB_b)

                        def ai_emit(e, ps=psx[:], t=t):
                            for c in range(DC):
                                for n in range(2):
                                    e.matmul(ps[:, n * 512:(n + 1) * 512],
                                             u2T[:, t * 1024 + c * 128: t * 1024 + (c + 1) * 128],
                                             Wsh[:, c * 1024 + n * 512: c * 1024 + n * 512 + 512],
                                             start=(c == 0), stop=False)
                            for n in range(2):
                                e.matmul(ps[:, n * 512:(n + 1) * 512],
                                         gT_ch[0:16, t * 128:(t + 1) * 128],
                                         brs[:, n * 512:n * 512 + 512], start=False, stop=False)
                                e.matmul(ps[:, n * 512:(n + 1) * 512], halves[:],
                                         bs0r[:, n * 512:n * 512 + 512], start=False, stop=False)
                                last = e.matmul(ps[:, n * 512:(n + 1) * 512], halves[:],
                                                bs1r[:, n * 512:n * 512 + 512],
                                                start=False, stop=True)
                            return last
                        P.op("tensor", ai_emit,
                             reads=[u2T_tb[t], Wsh_b, gT_tb[t // 4], br_b, halves_b,
                                    bs0_b, bs1_b],
                             writes=[psx_b])
                        P.op("vector", lambda e, o=acc[:, t * D:(t + 1) * D], i=psx[:]:
                             e.tensor_tensor(out=o, in0=i, in1=o, op=ALU.add),
                             reads=[psx_b], writes=[acc_tb[t]])

                    for t in range(CHT):
                        par = t % 2
                        sd_ap = sdash_ch[:, t * NE:(t + 1) * NE]
                        P.op("vector", lambda e, o=gsm[par][:, 1:2], i=sd_ap:
                             e.reduce_sum(out=o, in_=i, axis=AX.X),
                             reads=[sdc_tb[t]], writes=[zr_b[par]])
                        P.op("vector", lambda e, o=gsm[par][:, 2:3], i=gsm[par][:, 1:2]:
                             e.reciprocal(out=o, in_=i),
                             reads=[zr_b[par]], writes=[rz_b[par]])
                        P.op("vector", lambda e, o=gsm[par][:, 3:4], i=sd_ap:
                             e.reduce_max(out=o, in_=i, axis=AX.X),
                             reads=[sdc_tb[t]], writes=[m1_b[par]])
                        P.op("vector", lambda e, o=gtmp[par][:], i=sd_ap, m=gsm[par][:, 3:4]:
                             e.tensor_scalar(out=o, in0=i, scalar1=m, scalar2=-2.0,
                                             op0=ALU.is_ge, op1=ALU.mult),
                             reads=[sdc_tb[t], m1_b[par]], writes=[gtm_b[par]])
                        P.op("vector", lambda e, o=gtmp[par][:], i=sd_ap:
                             e.tensor_tensor(out=o, in0=i, in1=o, op=ALU.add),
                             reads=[sdc_tb[t]], writes=[gtm_b[par]])
                        P.op("vector", lambda e, o=gsm[par][:, 4:5], i=gtmp[par][:]:
                             e.reduce_max(out=o, in_=i, axis=AX.X),
                             reads=[gtm_b[par]], writes=[m2_b[par]])
                        P.op("vector", lambda e, o=gtmp[par][:], i=sd_ap, m=gsm[par][:, 4:5]:
                             e.tensor_scalar(out=o, in0=i, scalar1=m, scalar2=None, op0=ALU.is_ge),
                             reads=[sdc_tb[t], m2_b[par]], writes=[gtm_b[par]])
                        P.op("vector", lambda e, o=g_ch[:, t * NE:(t + 1) * NE], i=sd_ap,
                             r=gsm[par][:, 2:3], m=gtmp[par][:]:
                             e.scalar_tensor_tensor(out=o, in0=i, scalar=r, in1=m,
                                                    op0=ALU.mult, op1=ALU.mult),
                             reads=[sdc_tb[t], rz_b[par], gtm_b[par]], writes=[g_tb[t]])
                        if t % 4 == 3:
                            t0 = t - 3

                            def gt4_emit(e, t0=t0):
                                for tt_ in range(t0, t0 + 4):
                                    last = e.transpose(
                                        psS1[0:16, (tt_ - t0) * 128:(tt_ - t0 + 1) * 128],
                                        g_ch[:, tt_ * NE:(tt_ + 1) * NE], identf[:])
                                return last
                            P.op("tensor", gt4_emit,
                                 reads=[g_tb[t0], g_tb[t0 + 1], g_tb[t0 + 2], g_tb[t0 + 3],
                                        identf_b],
                                 writes=[psS1_b])
                            P.op("vector", lambda e, o=gT_ch[0:16, t0 * 128:(t0 + 4) * 128],
                                 i=psS1[0:16, 0:512]: e.tensor_copy(out=o, in_=i),
                                 reads=[psS1_b], writes=[gT_tb[t0 // 4]])
                            for t2 in range(t0, t0 + 4):
                                emit_accinit(t2)
                    for e_i in range(NE):
                        sl = e_i % 2
                        for t in range(CHT):
                            psx, psx_b = (psA, psA_b) if t % 2 == 0 else (psB, psB_b)

                            def ex_emit(e, ps=psx[:], t=t, sl=sl):
                                for c in range(DC):
                                    for n in range(2):
                                        last = e.matmul(
                                            ps[:, n * 512:(n + 1) * 512],
                                            u2T[:, t * 1024 + c * 128: t * 1024 + (c + 1) * 128],
                                            wr[sl][:, c * 1024 + n * 512: c * 1024 + n * 512 + 512],
                                            start=(c == 0), stop=(c == DC - 1))
                                return last
                            P.op("tensor", ex_emit,
                                 reads=[u2T_tb[t], wr_b[sl]], writes=[psx_b])
                            P.op("vector", lambda e, o=acc[:, t * D:(t + 1) * D], i=psx[:],
                                 gsc=g_ch[:, t * NE + e_i: t * NE + e_i + 1]:
                                 e.scalar_tensor_tensor(out=o, in0=i, scalar=gsc, in1=o,
                                                        op0=ALU.mult, op1=ALU.add),
                                 reads=[psx_b, g_tb[t]], writes=[acc_tb[t]])
                            if e_i == NE - 1:
                                emit_final(t, ch)
                        if e_i + 2 < NE:
                            P.dma("qg", wr[sl][:].rearrange("p (c n) -> p c n", n=1024),
                                  WrTd[e_i + 2].rearrange("(c p) n -> p c n", p=128),
                                  writes=[wr_b[sl]])

                    if f"acc{seq}_{ch}" in taps:
                        tap(P, f"acc{seq}_{ch}", acc[:], acc_tb, [128, CHT * D], F32)
                    if f"g{seq}_{ch}" in taps:
                        tap(P, f"g{seq}_{ch}", g_ch[:], g_tb, [128, CHT * NE], F32)
                    if ch == NCH - 1:
                        P.barrier()
                    P.flush()

        P.barrier()
        P.flush()
    return nc, tap_outs


def _host_inputs(inputs):
    f32 = np.float32
    bf16 = ml_dtypes.bfloat16
    X = np.ascontiguousarray(np.asarray(inputs["L0_attn_X"], f32))
    shared = {
        "Wq": np.ascontiguousarray(np.asarray(inputs["L0_attn_Wq"], f32)),
        "Wk": np.ascontiguousarray(np.asarray(inputs["L0_attn_Wk"], f32)),
        "Wv": np.ascontiguousarray(np.asarray(inputs["L0_attn_Wv"], f32)),
        "bq": np.ascontiguousarray(np.asarray(inputs["L0_attn_bq"], f32).reshape(1, D)),
        "bk": np.ascontiguousarray(np.asarray(inputs["L0_attn_bk"], f32).reshape(1, D)),
        "bv": np.ascontiguousarray(np.asarray(inputs["L0_attn_bv"], f32).reshape(1, D)),
        "CT": np.ascontiguousarray(np.asarray(inputs["L0_ffn_centroids"], f32).T),
        "eb": np.ascontiguousarray(np.asarray(inputs["L0_ffn_expert_biases"], f32).reshape(1, NE)),
        "WrT": np.ascontiguousarray(np.asarray(inputs["L0_ffn_Wr"], f32).transpose(0, 2, 1)),
        "br": np.ascontiguousarray(np.asarray(inputs["L0_ffn_br"], f32)),
        "WsT": np.ascontiguousarray(np.asarray(inputs["L0_ffn_Ws"], f32).transpose(0, 2, 1)),
        "bs": np.ascontiguousarray(np.asarray(inputs["L0_ffn_bs"], f32)),
        "identb": np.eye(128, dtype=bf16),
        "identf": np.eye(128, dtype=f32),
        "mask": np.triu(np.full((128, 128), -1.0e30, f32), 1),
        "ones1": np.ones((1, 128), dtype=bf16),
        "halves": np.full((1, 128), 0.5, dtype=bf16),
    }
    in_maps = []
    for c in range(NCORES):
        m = dict(shared)
        m["X"] = np.ascontiguousarray(X[c * SPC:(c + 1) * SPC])
        in_maps.append(m)
    return in_maps


def kernel(**inputs):
    in_maps = _host_inputs(inputs)
    nc, _ = build()
    res = run_bass_kernel_spmd(nc, in_maps, core_ids=list(range(NCORES)))
    out = np.concatenate([np.asarray(res.results[c]["Y"]) for c in range(NCORES)], axis=0)
    return out.astype(np.float32, copy=False)


# version 09
# speedup vs baseline: 1.5021x; 1.0304x over previous
import contextlib
import numpy as np
import ml_dtypes
import concourse.bass as bass
import concourse.mybir as mybir
from concourse.bass_utils import run_bass_kernel_spmd

F32 = mybir.dt.float32
BF16 = mybir.dt.bfloat16
AF = mybir.ActivationFunctionType
ALU = mybir.AluOpType
AX = mybir.AxisListType

B, S, D = 16, 4096, 1024
NCORES = 8
SPC = B // NCORES
NT = S // 128
DC = D // 128
NE = 16
CHT = 16
NCH = NT // CHT
EPS = 1e-5
ISC = 1.0 / 32.0

ENG2SEM = {"tensor": "t", "vector": "v", "scalar": "s", "gpsimd": "g"}
QENG = {"qs": "sync", "qc": "scalar", "qg": "gpsimd"}


class Buf:
    __slots__ = ("w", "rs")

    def __init__(self):
        self.w = None
        self.rs = []


class Prog:
    def __init__(self, nc, ctx):
        self.nc = nc
        self.pend = {e: [] for e in ("sync", "tensor", "vector", "scalar", "gpsimd")}
        self.sems = {}
        self.cnt = {}
        self.hw = {e: {} for e in self.pend}
        for sn in ("t", "v", "s", "g", "qs", "qc", "qg"):
            self.sems[sn] = ctx.enter_context(nc.semaphore("sem_" + sn))
            self.cnt[sn] = 0
        self.nops = 0

    def _prune(self, eng, evs):
        best = {}
        for ev in evs:
            if ev is None:
                continue
            sn, val = ev
            if eng == "tensor" and sn == "t":
                continue
            if best.get(sn, 0) < val:
                best[sn] = val
        hw = self.hw[eng]
        out = []
        for sn, val in best.items():
            if hw.get(sn, 0) >= val:
                continue
            hw[sn] = val
            out.append((sn, val))
        return out

    def _gather(self, reads, writes, extra):
        evs = list(extra)
        for b in reads:
            evs.append(b.w)
        for b in writes:
            evs.append(b.w)
            evs.extend(b.rs)
        return evs

    def op(self, eng, emit, reads=(), writes=(), extra=()):
        waits = self._prune(eng, self._gather(reads, writes, extra))
        sn = ENG2SEM[eng]
        self.cnt[sn] += 1
        ev = (sn, self.cnt[sn])
        sems = self.sems

        def run(e, waits=waits, emit=emit, sem=sems[sn], sems=sems):
            for wn, wv in waits:
                e.wait_ge(sems[wn], wv)
            emit(e).then_inc(sem, 1)

        self.pend[eng].append(run)
        self.nops += 1
        for b in reads:
            b.rs.append(ev)
        for b in writes:
            b.w = ev
            b.rs = []
        return ev

    def dma(self, q, out, in_, reads=(), writes=(), extra=()):
        eng = QENG[q]
        waits = self._prune(eng, self._gather(reads, writes, extra))
        self.cnt[q] += 16
        ev = (q, self.cnt[q])
        sems = self.sems

        def run(e, waits=waits, out=out, in_=in_, sem=sems[q], sems=sems):
            for wn, wv in waits:
                e.wait_ge(sems[wn], wv)
            e.dma_start(out=out, in_=in_).then_inc(sem, 16)

        self.pend[eng].append(run)
        self.nops += 1
        for b in reads:
            b.rs.append(ev)
        for b in writes:
            b.w = ev
            b.rs = []
        return ev

    def barrier(self):
        cur = [(sn, v) for sn, v in self.cnt.items() if v > 0]
        sems = self.sems
        for eng in self.pend:
            waits = self._prune(eng, cur)
            if waits:
                def run(e, waits=waits, sems=sems):
                    for wn, wv in waits:
                        e.wait_ge(sems[wn], wv)
                self.pend[eng].append(run)

    def flush(self):
        pend = {e: lst for e, lst in self.pend.items() if lst}
        self.pend = {e: [] for e in self.pend}
        if not pend:
            return
        with self.nc.Block() as blk:
            for eng, lst in pend.items():
                getattr(blk, eng)(lambda e, lst=lst: [f(e) for f in lst])


def make_stats(ctx, nc, name):
    t = ctx.enter_context(nc.sbuf_tensor(name, [128, 18], F32))
    return dict(t=t, bn_a=Buf(), bn_b=Buf(), mv=Buf(), ve=Buf(), lnv=Buf(), rstd=Buf())


def emit_ln(P, st, src, src_bufs, dst, dst_buf, norm_eng="vector"):
    t = st["t"]
    P.op("vector", lambda e, o=t[:, 0:6], i=src[:, 0:512]: e.bn_stats(out=o, in_=i),
         reads=src_bufs, writes=[st["bn_a"]])
    P.op("vector", lambda e, o=t[:, 6:12], i=src[:, 512:1024]: e.bn_stats(out=o, in_=i),
         reads=src_bufs, writes=[st["bn_b"]])
    P.op("vector", lambda e, o=t[:, 12:14], i=t[:, 0:12]: e.bn_aggr(out=o, in_=i),
         reads=[st["bn_a"], st["bn_b"]], writes=[st["mv"]])
    P.op("vector", lambda e, o=t[:, 14:15], i=t[:, 13:14]:
         e.tensor_scalar(out=o, in0=i, scalar1=EPS, scalar2=None, op0=ALU.add),
         reads=[st["mv"]], writes=[st["ve"]])
    P.op("scalar", lambda e, o=t[:, 15:16], i=t[:, 14:15]:
         e.activation(out=o, in_=i, func=AF.Ln),
         reads=[st["ve"]], writes=[st["lnv"]])
    P.op("scalar", lambda e, o=t[:, 16:17], i=t[:, 15:16]:
         e.activation(out=o, in_=i, func=AF.Exp, scale=-0.5),
         reads=[st["lnv"]], writes=[st["rstd"]])
    P.op(norm_eng, lambda e, o=dst, i=src, m=t[:, 12:13], r=t[:, 16:17]:
         e.tensor_scalar(out=o, in0=i, scalar1=m, scalar2=r,
                         op0=ALU.subtract, op1=ALU.mult),
         reads=list(src_bufs) + [st["mv"], st["rstd"]], writes=[dst_buf])


def build(upto="full", taps=()):
    taps = set(taps)
    nc = bass.Bass("TRN2", target_bir_lowering=False)

    Xd = nc.dram_tensor("X", [SPC, S, D], F32, kind="ExternalInput").ap()
    Yd = nc.dram_tensor("Y", [SPC, S, D], F32, kind="ExternalOutput").ap()
    x1s = nc.dram_tensor("x1scr", [SPC, S, D], F32, kind="Internal").ap()
    uTs = nc.dram_tensor("uTscr", [SPC, NT, 128, DC * 128], BF16, kind="Internal").ap()
    wshs = nc.dram_tensor("wshscr", [128, DC * 1024], BF16, kind="Internal").ap()
    Wqd = nc.dram_tensor("Wq", [D, D], F32, kind="ExternalInput").ap()
    Wkd = nc.dram_tensor("Wk", [D, D], F32, kind="ExternalInput").ap()
    Wvd = nc.dram_tensor("Wv", [D, D], F32, kind="ExternalInput").ap()
    bqd = nc.dram_tensor("bq", [1, D], F32, kind="ExternalInput").ap()
    bkd = nc.dram_tensor("bk", [1, D], F32, kind="ExternalInput").ap()
    bvd = nc.dram_tensor("bv", [1, D], F32, kind="ExternalInput").ap()
    CTd = nc.dram_tensor("CT", [D, NE], F32, kind="ExternalInput").ap()
    ebd = nc.dram_tensor("eb", [1, NE], F32, kind="ExternalInput").ap()
    WrTd = nc.dram_tensor("WrT", [NE, D, D], F32, kind="ExternalInput").ap()
    brd = nc.dram_tensor("br", [NE, D], F32, kind="ExternalInput").ap()
    WsTd = nc.dram_tensor("WsT", [2, D, D], F32, kind="ExternalInput").ap()
    bsd = nc.dram_tensor("bs", [2, D], F32, kind="ExternalInput").ap()
    identbd = nc.dram_tensor("identb", [128, 128], BF16, kind="ExternalInput").ap()
    identfd = nc.dram_tensor("identf", [128, 128], F32, kind="ExternalInput").ap()
    maskd = nc.dram_tensor("mask", [128, 128], F32, kind="ExternalInput").ap()
    ones1d = nc.dram_tensor("ones1", [1, 128], BF16, kind="ExternalInput").ap()
    halvesd = nc.dram_tensor("halves", [1, 128], BF16, kind="ExternalInput").ap()

    tap_outs = {}

    def tap(P, name, ap, readbufs, shape, dtype):
        if name in taps:
            t = nc.dram_tensor("tap_" + name, list(shape), dtype, kind="ExternalOutput").ap()
            tap_outs[name] = t
            P.dma("qs", t, ap, reads=list(readbufs))

    with contextlib.ExitStack() as octx:
        P = Prog(nc, octx)
        OE = octx.enter_context

        identb = OE(nc.sbuf_tensor("identb_sb", [128, 128], BF16)); identb_b = Buf()
        ones1 = OE(nc.sbuf_tensor("ones1_sb", [1, 128], BF16)); ones1_b = Buf()
        halves = OE(nc.sbuf_tensor("halves_sb", [1, 128], BF16)); halves_b = Buf()
        wq = OE(nc.sbuf_tensor("wq_sb", [128, DC * 1024], BF16)); wq_b = Buf()
        bqr = OE(nc.sbuf_tensor("bqr_sb", [1, D], BF16)); bqr_b = Buf()
        maskt = OE(nc.sbuf_tensor("maskt_sb", [128, 128], F32)); mask_b = Buf()

        psA = OE(nc.psum_tensor("psA", [128, 1024], F32)); psA_b = Buf()
        psB = OE(nc.psum_tensor("psB", [128, 1024], F32)); psB_b = Buf()
        psT = OE(nc.psum_tensor("psT", [128, 1024], BF16)); psT_b = Buf()
        psT2 = OE(nc.psum_tensor("psT2", [128, 1024], BF16)); psT2_b = Buf()
        psS0 = OE(nc.psum_tensor("psS0", [128, 512], F32)); psS0_b = Buf()
        psS1 = OE(nc.psum_tensor("psS1", [128, 512], F32)); psS1_b = Buf()
        psS = [psS0, psS1]; psS_b = [psS0_b, psS1_b]

        P.dma("qs", identb[:], identbd, writes=[identb_b])
        P.dma("qs", ones1[:], ones1d, writes=[ones1_b])
        P.dma("qs", halves[:], halvesd, writes=[halves_b])
        P.dma("qs", maskt[:], maskd, writes=[mask_b])
        P.dma("qg", wq[:].rearrange("p (c n) -> p c n", n=1024),
              Wqd.rearrange("(c p) n -> p c n", p=128), writes=[wq_b])
        P.dma("qg", bqr[:], bqd, writes=[bqr_b])

        with contextlib.ExitStack() as wctx:
            wtmp = [wctx.enter_context(nc.sbuf_tensor(f"wtmpS{i}", [128, 1024], F32))
                    for i in range(4)]
            wtmp_b = [Buf() for _ in range(4)]
            wstg = [wctx.enter_context(nc.sbuf_tensor(f"wstgS{i}", [128, 1024], BF16))
                    for i in range(2)]
            wstg_b = [Buf(), Buf()]
            for c in range(DC):
                pr = c % 2
                P.dma("qs", wtmp[2 * pr][:], WsTd[0, c * 128:(c + 1) * 128, :],
                      writes=[wtmp_b[2 * pr]])
                P.dma("qc", wtmp[2 * pr + 1][:], WsTd[1, c * 128:(c + 1) * 128, :],
                      writes=[wtmp_b[2 * pr + 1]])
                P.op("vector", lambda e, a2=wtmp[2 * pr][:], b2=wtmp[2 * pr + 1][:]:
                     e.tensor_tensor(out=a2, in0=a2, in1=b2, op=ALU.add),
                     reads=[wtmp_b[2 * pr + 1]], writes=[wtmp_b[2 * pr]])
                P.op("vector", lambda e, o=wstg[pr][:], i=wtmp[2 * pr][:]:
                     e.tensor_scalar(out=o, in0=i, scalar1=0.5, scalar2=None, op0=ALU.mult),
                     reads=[wtmp_b[2 * pr]], writes=[wstg_b[pr]])
                P.dma("qc", wshs[:, c * 1024:(c + 1) * 1024], wstg[pr][:],
                      reads=[wstg_b[pr]])
            P.barrier()
            P.flush()

        for seq in range(SPC):
            with contextlib.ExitStack() as sctx:
                SE = sctx.enter_context
                KT = SE(nc.sbuf_tensor(f"KT{seq}", [128, DC * S], BF16))
                Vb = SE(nc.sbuf_tensor(f"V{seq}", [128, NT * D], BF16))
                kt_tb = [Buf() for _ in range(NT)]
                v_tb = [Buf() for _ in range(NT)]

                with contextlib.ExitStack() as actx:
                    AE = actx.enter_context
                    wk = AE(nc.sbuf_tensor(f"wk{seq}", [128, DC * 1024], BF16)); wk_b = Buf()
                    wv = AE(nc.sbuf_tensor(f"wv{seq}", [128, DC * 1024], BF16)); wv_b = Buf()
                    bkr = AE(nc.sbuf_tensor(f"bkr{seq}", [1, D], BF16)); bkr_b = Buf()
                    bvr = AE(nc.sbuf_tensor(f"bvr{seq}", [1, D], BF16)); bvr_b = Buf()
                    xt = [AE(nc.sbuf_tensor(f"xtA{seq}_{i}", [128, D], F32)) for i in range(3)]
                    xt_b = [Buf(), Buf(), Buf()]
                    usb = [AE(nc.sbuf_tensor(f"uA{seq}_{i}", [128, D], BF16)) for i in range(2)]
                    u_b = [Buf(), Buf()]
                    uT = [AE(nc.sbuf_tensor(f"uTA{seq}_{i}", [128, DC * 128], BF16)) for i in range(2)]
                    uT_b = [Buf(), Buf()]
                    ktmp = [AE(nc.sbuf_tensor(f"ktmpA{seq}_{i}", [128, D], BF16)) for i in range(2)]
                    ktmp_b = [Buf(), Buf()]
                    stA = [make_stats(actx, nc, f"stA{seq}_{i}") for i in range(2)]

                    P.dma("qg", wk[:].rearrange("p (c n) -> p c n", n=1024),
                          Wkd.rearrange("(c p) n -> p c n", p=128), writes=[wk_b])
                    P.dma("qg", wv[:].rearrange("p (c n) -> p c n", n=1024),
                          Wvd.rearrange("(c p) n -> p c n", p=128), writes=[wv_b])
                    P.dma("qg", bkr[:], bkd, writes=[bkr_b])
                    P.dma("qg", bvr[:], bvd, writes=[bvr_b])
                    P.dma("qs", xt[0][:], Xd[seq, 0:128, :], writes=[xt_b[0]])
                    P.dma("qs", xt[1][:], Xd[seq, 128:256, :], writes=[xt_b[1]])

                    def proj_emit(ps, uTap, W, brow):
                        def em(e, ps=ps, uTap=uTap, W=W, brow=brow):
                            for c in range(DC):
                                for n in range(2):
                                    e.matmul(ps[:, n * 512:(n + 1) * 512],
                                             uTap[:, c * 128:(c + 1) * 128],
                                             W[:, c * 1024 + n * 512: c * 1024 + n * 512 + 512],
                                             start=(c == 0), stop=False)
                            for n in range(2):
                                last = e.matmul(ps[:, n * 512:(n + 1) * 512], ones1[:],
                                                brow[:, n * 512:n * 512 + 512],
                                                start=False, stop=True)
                            return last
                        return em

                    def tr8_emit(psd, src, ident):
                        def em(e, psd=psd, src=src, ident=ident):
                            for c in range(DC):
                                last = e.transpose(psd[:, c * 128:(c + 1) * 128],
                                                   src[:, c * 128:(c + 1) * 128], ident)
                            return last
                        return em

                    def emit_utr(t):
                        par = t % 2
                        P.op("tensor", tr8_emit(psT, usb[par][:], identb[:]),
                             reads=[u_b[par], identb_b], writes=[psT_b])
                        P.op("scalar", lambda e, o=uT[par][:], i=psT[:]: e.copy(out=o, in_=i),
                             reads=[psT_b], writes=[uT_b[par]])
                        P.dma("qc", uTs[seq, t], uT[par][:], reads=[uT_b[par]])

                    emit_ln(P, stA[0], xt[0][:], [xt_b[0]], usb[0][:], u_b[0])
                    emit_ln(P, stA[1], xt[1][:], [xt_b[1]], usb[1][:], u_b[1])
                    emit_utr(0)
                    for t in range(NT):
                        par = t % 2
                        if t + 2 < NT:
                            P.dma("qs", xt[(t + 2) % 3][:], Xd[seq, (t + 2) * 128:(t + 3) * 128, :],
                                  writes=[xt_b[(t + 2) % 3]])
                            emit_ln(P, stA[par], xt[(t + 2) % 3][:], [xt_b[(t + 2) % 3]],
                                    usb[par][:], u_b[par])
                        if t + 1 < NT:
                            emit_utr(t + 1)
                        P.op("tensor", proj_emit(psA[:], uT[par][:], wk[:], bkr[:]),
                             reads=[uT_b[par], wk_b, ones1_b, bkr_b], writes=[psA_b])
                        P.op("scalar", lambda e, o=ktmp[par][:], i=psA[:]:
                             e.activation(out=o, in_=i, func=AF.Copy),
                             reads=[psA_b], writes=[ktmp_b[par]])
                        P.op("tensor", proj_emit(psB[:], uT[par][:], wv[:], bvr[:]),
                             reads=[uT_b[par], wv_b, ones1_b, bvr_b], writes=[psB_b])
                        P.op("tensor", tr8_emit(psT2, ktmp[par][:], identb[:]),
                             reads=[ktmp_b[par], identb_b], writes=[psT2_b])
                        P.op("scalar", lambda e,
                             o=KT[:].rearrange("p (c s) -> p c s", s=S)[:, :, t * 128:(t + 1) * 128],
                             i=psT2[:].rearrange("p (c q) -> p c q", q=128):
                             e.copy(out=o, in_=i),
                             reads=[psT2_b], writes=[kt_tb[t]])
                        P.op("scalar", lambda e, o=Vb[:, t * D:(t + 1) * D], i=psB[:]:
                             e.activation(out=o, in_=i, func=AF.Copy),
                             reads=[psB_b], writes=[v_tb[t]])

                    tap(P, f"KT{seq}", KT[:], kt_tb, [128, DC * S], BF16)
                    tap(P, f"V{seq}", Vb[:], v_tb, [128, NT * D], BF16)
                    P.barrier()
                    P.flush()

                if upto == "A":
                    P.barrier()
                    P.flush()
                    return nc, tap_outs

                with contextlib.ExitStack() as bctx:
                    BE = bctx.enter_context
                    scores = BE(nc.sbuf_tensor(f"scores{seq}", [128, S], F32))
                    sc_b = [Buf() for _ in range(8)]
                    prs = [BE(nc.sbuf_tensor(f"pr{seq}_{i}", [128, 1024], BF16)) for i in range(2)]
                    pr_b = [Buf(), Buf()]
                    PT = [BE(nc.sbuf_tensor(f"PT{seq}_{i}", [128, 1024], BF16)) for i in range(2)]
                    PT_b = [Buf(), Buf()]
                    xt = [BE(nc.sbuf_tensor(f"xtB{seq}_{i}", [128, D], F32)) for i in range(3)]
                    xt_b = [Buf(), Buf(), Buf()]
                    uTt = [BE(nc.sbuf_tensor(f"uTt{seq}_{i}", [128, DC * 128], BF16)) for i in range(2)]
                    uTt_b = [Buf(), Buf()]
                    Qsb = [BE(nc.sbuf_tensor(f"Q{seq}_{i}", [128, D], BF16)) for i in range(2)]
                    Q_b = [Buf(), Buf()]
                    QT = [BE(nc.sbuf_tensor(f"QT{seq}_{i}", [128, DC * 128], BF16)) for i in range(2)]
                    QT_b = [Buf(), Buf()]
                    att = BE(nc.sbuf_tensor(f"att{seq}", [128, D], F32)); att_b = Buf()
                    sm = [BE(nc.sbuf_tensor(f"smB{seq}_{i}", [128, 16], F32)) for i in range(2)]
                    pmg_b = [[Buf() for _ in range(8)], [Buf() for _ in range(8)]]
                    rm_b = [Buf(), Buf()]; nb_b = [Buf(), Buf()]
                    part_b = [Buf(), Buf()]; rs_b = [Buf(), Buf()]; rinv_b = [Buf(), Buf()]
                    stB2 = [make_stats(bctx, nc, f"stB2{seq}_{i}") for i in range(2)]

                    def frontLN(i):
                        P.dma("qs", xt[i % 3][:], Xd[seq, i * 128:(i + 1) * 128, :],
                              writes=[xt_b[i % 3]])
                        P.dma("qc", uTt[i % 2][:], uTs[seq, i], writes=[uTt_b[i % 2]])

                    def frontQ(i):
                        par = i % 2
                        P.op("tensor", proj_emit(psB[:], uTt[par][:], wq[:], bqr[:]),
                             reads=[uTt_b[par], wq_b, ones1_b, bqr_b], writes=[psB_b])
                        P.op("scalar", lambda e, o=Qsb[par][:], i=psB[:]:
                             e.activation(out=o, in_=i, func=AF.Copy),
                             reads=[psB_b], writes=[Q_b[par]])
                        P.op("tensor", tr8_emit(psT, Qsb[par][:], identb[:]),
                             reads=[Q_b[par], identb_b], writes=[psT_b])
                        P.op("vector", lambda e, o=QT[par][:], i=psT[:]: e.tensor_copy(out=o, in_=i),
                             reads=[psT_b], writes=[QT_b[par]])

                    frontLN(0)
                    frontLN(1)
                    frontQ(0)
                    for i in range(NT):
                        par = i % 2
                        keys = (i + 1) * 128
                        ngrp = (keys + 511) // 512
                        for g in range(ngrp):
                            w = min(512, keys - g * 512)
                            gs = g % 2

                            def sc_emit(e, ps=psS[gs], qt=QT[par][:], g=g, w=w):
                                for c in range(DC):
                                    last = e.matmul(
                                        ps[:, 0:w], qt[:, c * 128:(c + 1) * 128],
                                        KT[:, c * S + g * 512: c * S + g * 512 + w],
                                        start=(c == 0), stop=(c == DC - 1))
                                return last
                            P.op("tensor", sc_emit,
                                 reads=[QT_b[par]] + kt_tb[g * 4:g * 4 + (w + 127) // 128],
                                 writes=[psS_b[gs]])
                            if g < ngrp - 1:
                                P.op("scalar", lambda e, o=scores[:, g * 512:g * 512 + w],
                                     i=psS[gs][:, 0:w]: e.activation(out=o, in_=i, func=AF.Copy),
                                     reads=[psS_b[gs]], writes=[sc_b[g]])
                            else:
                                lead = w - 128
                                if lead:
                                    P.op("vector", lambda e, o=scores[:, g * 512:g * 512 + lead],
                                         i=psS[gs][:, 0:lead]: e.tensor_copy(out=o, in_=i),
                                         reads=[psS_b[gs]], writes=[sc_b[g]])
                                P.op("vector", lambda e, o=scores[:, g * 512 + lead:g * 512 + w],
                                     i=psS[gs][:, lead:w], m=maskt[:]:
                                     e.tensor_tensor(out=o, in0=i, in1=m, op=ALU.add),
                                     reads=[psS_b[gs], mask_b], writes=[sc_b[g]])
                            P.op("vector", lambda e, o=sm[par][:, g:g + 1],
                                 i=scores[:, g * 512:g * 512 + w]:
                                 e.reduce_max(out=o, in_=i, axis=AX.X),
                                 reads=[sc_b[g]], writes=[pmg_b[par][g]])
                        P.op("vector", lambda e, o=sm[par][:, 8:9], i=sm[par][:, 0:ngrp]:
                             e.reduce_max(out=o, in_=i, axis=AX.X),
                             reads=pmg_b[par][0:ngrp], writes=[rm_b[par]])
                        P.op("vector", lambda e, o=sm[par][:, 9:10], i=sm[par][:, 8:9]:
                             e.tensor_scalar(out=o, in0=i, scalar1=-ISC, scalar2=None, op0=ALU.mult),
                             reads=[rm_b[par]], writes=[nb_b[par]])
                        nchunk = (keys + 1023) // 1024
                        for ch in range(nchunk):
                            w2 = min(1024, keys - ch * 1024)
                            ps_ = ch % 2
                            P.op("scalar", lambda e, o=prs[ps_][:, 0:w2],
                                 i=scores[:, ch * 1024:ch * 1024 + w2], b=sm[par][:, 9:10]:
                                 e.activation(out=o, in_=i, func=AF.Exp, bias=b, scale=ISC),
                                 reads=sc_b[2 * ch:2 * ch + (w2 + 511) // 512] + [nb_b[par]],
                                 writes=[pr_b[ps_]])
                            P.op("vector", lambda e, o=sm[par][:, 10 + ch:11 + ch], i=prs[ps_][:, 0:w2]:
                                 e.reduce_sum(out=o, in_=i, axis=AX.X),
                                 reads=[pr_b[ps_]], writes=[part_b[par]])
                            if ch == 0:
                                if i + 2 < NT:
                                    frontLN(i + 2)
                                if i + 1 < NT:
                                    frontQ(i + 1)
                            nkt = w2 // 128

                            def ptr_emit(e, psd=psT2, src=prs[ps_][:], nkt=nkt):
                                for c in range(nkt):
                                    last = e.transpose(psd[:, c * 128:(c + 1) * 128],
                                                       src[:, c * 128:(c + 1) * 128], identb[:])
                                return last
                            P.op("tensor", ptr_emit, reads=[pr_b[ps_], identb_b], writes=[psT2_b])
                            P.op("vector", lambda e, o=PT[ps_][:, 0:nkt * 128], i=psT2[:, 0:nkt * 128]:
                                 e.tensor_copy(out=o, in_=i),
                                 reads=[psT2_b], writes=[PT_b[ps_]])

                            def pv_emit(e, pt=PT[ps_][:], ch=ch, nkt=nkt, i=i):
                                for jj in range(nkt):
                                    j = ch * 8 + jj
                                    for n in range(2):
                                        last = e.matmul(
                                            psA[:, n * 512:(n + 1) * 512],
                                            pt[:, jj * 128:(jj + 1) * 128],
                                            Vb[:, j * D + n * 512: j * D + n * 512 + 512],
                                            start=(j == 0), stop=(j == i))
                                return last
                            P.op("tensor", pv_emit,
                                 reads=[PT_b[ps_]] + v_tb[ch * 8:ch * 8 + nkt],
                                 writes=[psA_b])
                        P.op("vector", lambda e, o=sm[par][:, 14:15], i=sm[par][:, 10:10 + nchunk]:
                             e.reduce_sum(out=o, in_=i, axis=AX.X),
                             reads=[part_b[par]], writes=[rs_b[par]])
                        P.op("vector", lambda e, o=sm[par][:, 15:16], i=sm[par][:, 14:15]:
                             e.reciprocal(out=o, in_=i),
                             reads=[rs_b[par]], writes=[rinv_b[par]])
                        P.op("scalar", lambda e, o=att[:], i=psA[:], r=sm[par][:, 15:16]:
                             e.activation(out=o, in_=i, func=AF.Copy, scale=r),
                             reads=[psA_b, rinv_b[par]], writes=[att_b])
                        emit_ln(P, stB2[par], att[:], [att_b], att[:], att_b)
                        P.op("vector", lambda e, o=att[:], i=xt[i % 3][:]:
                             e.tensor_tensor(out=o, in0=o, in1=i, op=ALU.add),
                             reads=[xt_b[i % 3]], writes=[att_b])
                        P.dma("qs", x1s[seq, i * 128:(i + 1) * 128, :], att[:], reads=[att_b])

                    tap(P, f"x1_{seq}", x1s[seq], [], [S, D], F32)
                    P.barrier()
                    P.flush()

            if upto == "B":
                P.barrier()
                P.flush()
                return nc, tap_outs

            with contextlib.ExitStack() as cctx:
                CE = cctx.enter_context
                acc = CE(nc.sbuf_tensor(f"acc{seq}", [128, CHT * D], F32))
                acc_tb = [Buf() for _ in range(CHT)]
                u2T = CE(nc.sbuf_tensor(f"u2T{seq}", [128, CHT * DC * 128], BF16))
                u2T_tb = [Buf() for _ in range(CHT)]
                wr = [CE(nc.sbuf_tensor(f"wr{seq}_{i}", [128, DC * 1024], BF16)) for i in range(2)]
                wr_b = [Buf(), Buf()]
                x1td = [CE(nc.sbuf_tensor(f"x1td{seq}_{i}", [128, D], F32)) for i in range(2)]
                x1_b = [Buf(), Buf()]
                u2t = [CE(nc.sbuf_tensor(f"u2t{seq}_{i}", [128, D], BF16)) for i in range(2)]
                u2_b = [Buf(), Buf()]
                yt = [CE(nc.sbuf_tensor(f"yt{seq}_{i}", [128, D], F32)) for i in range(2)]
                yt_b = [Buf(), Buf()]
                g_ch = CE(nc.sbuf_tensor(f"g_ch{seq}", [128, CHT * NE], F32))
                g_tb = [Buf() for _ in range(CHT)]
                gT_ch = CE(nc.sbuf_tensor(f"gT_ch{seq}", [16, CHT * 128], BF16))
                gT_tb = [Buf() for _ in range(CHT // 4)]
                rl_ch = CE(nc.sbuf_tensor(f"rl_ch{seq}", [128, CHT * NE], F32))
                rl_tb = [Buf() for _ in range(CHT // 4)]
                sdash_ch = CE(nc.sbuf_tensor(f"sdash_ch{seq}", [128, CHT * NE], F32))
                sdc_tb = [Buf() for _ in range(CHT)]
                gtmp = [CE(nc.sbuf_tensor(f"gtmp{seq}_{i}", [128, NE], F32)) for i in range(2)]
                gtm_b = [Buf(), Buf()]
                gsm = [CE(nc.sbuf_tensor(f"gsm{seq}_{i}", [128, 6], F32)) for i in range(2)]
                nbr_b = [Buf(), Buf()]; zr_b = [Buf(), Buf()]; rz_b = [Buf(), Buf()]
                m1_b = [Buf(), Buf()]; m2_b = [Buf(), Buf()]
                CTs = CE(nc.sbuf_tensor(f"CTs{seq}", [128, DC * NE], BF16)); CT_b = Buf()
                brs = CE(nc.sbuf_tensor(f"brs{seq}", [16, D], BF16)); br_b = Buf()
                ebr = CE(nc.sbuf_tensor(f"ebr{seq}", [1, NE], BF16)); eb_b = Buf()
                bs0r = CE(nc.sbuf_tensor(f"bs0r{seq}", [1, D], BF16)); bs0_b = Buf()
                bs1r = CE(nc.sbuf_tensor(f"bs1r{seq}", [1, D], BF16)); bs1_b = Buf()
                identf = CE(nc.sbuf_tensor(f"identf_sb{seq}", [128, 128], F32)); identf_b = Buf()
                Wsh = CE(nc.sbuf_tensor(f"Wsh{seq}", [128, DC * 1024], BF16)); Wsh_b = Buf()
                stC = [make_stats(cctx, nc, f"stC{seq}_{i}") for i in range(2)]

                P.dma("qg", CTs[:].rearrange("p (c e) -> p c e", e=NE),
                      CTd.rearrange("(c p) e -> p c e", p=128), writes=[CT_b])
                P.dma("qg", brs[:], brd, writes=[br_b])
                P.dma("qg", ebr[:], ebd, writes=[eb_b])
                P.dma("qg", bs0r[:], bsd[0:1, :], writes=[bs0_b])
                P.dma("qg", bs1r[:], bsd[1:2, :], writes=[bs1_b])
                P.dma("qs", identf[:], identfd, writes=[identf_b])
                P.dma("qc", Wsh[:], wshs, writes=[Wsh_b])

                def emit_final(t, ch):
                    par = t % 2
                    gt = ch * CHT + t
                    P.dma("qs", x1td[par][:], x1s[seq, gt * 128:(gt + 1) * 128, :],
                          writes=[x1_b[par]])
                    st = stC[par]
                    tt = st["t"]
                    P.op("vector", lambda e, o=tt[:, 0:6], i=acc[:, t * D:t * D + 512]:
                         e.bn_stats(out=o, in_=i), reads=[acc_tb[t]], writes=[st["bn_a"]])
                    P.op("vector", lambda e, o=tt[:, 6:12], i=acc[:, t * D + 512:t * D + 1024]:
                         e.bn_stats(out=o, in_=i), reads=[acc_tb[t]], writes=[st["bn_b"]])
                    P.op("vector", lambda e, o=tt[:, 12:14], i=tt[:, 0:12]:
                         e.bn_aggr(out=o, in_=i),
                         reads=[st["bn_a"], st["bn_b"]], writes=[st["mv"]])
                    P.op("vector", lambda e, o=tt[:, 14:15], i=tt[:, 13:14]:
                         e.tensor_scalar(out=o, in0=i, scalar1=EPS, scalar2=None, op0=ALU.add),
                         reads=[st["mv"]], writes=[st["ve"]])
                    P.op("scalar", lambda e, o=tt[:, 15:16], i=tt[:, 14:15]:
                         e.activation(out=o, in_=i, func=AF.Ln),
                         reads=[st["ve"]], writes=[st["lnv"]])
                    P.op("scalar", lambda e, o=tt[:, 16:17], i=tt[:, 15:16]:
                         e.activation(out=o, in_=i, func=AF.Exp, scale=-0.5),
                         reads=[st["lnv"]], writes=[st["rstd"]])
                    P.op("vector", lambda e, o=yt[par][:], i=acc[:, t * D:(t + 1) * D],
                         m=tt[:, 12:13], r=tt[:, 16:17]:
                         e.tensor_scalar(out=o, in0=i, scalar1=m, scalar2=r,
                                         op0=ALU.subtract, op1=ALU.mult),
                         reads=[acc_tb[t], st["mv"], st["rstd"]], writes=[yt_b[par]])
                    P.op("vector", lambda e, o=yt[par][:], i=x1td[par][:]:
                         e.tensor_tensor(out=o, in0=o, in1=i, op=ALU.add),
                         reads=[x1_b[par]], writes=[yt_b[par]])
                    P.dma("qs", Yd[seq, gt * 128:(gt + 1) * 128, :], yt[par][:],
                          reads=[yt_b[par]])

                for ch in range(NCH):
                    base = ch * CHT
                    P.dma("qs", x1td[0][:], x1s[seq, base * 128:(base + 1) * 128, :],
                          writes=[x1_b[0]])
                    emit_ln(P, stC[0], x1td[0][:], [x1_b[0]], u2t[0][:], u2_b[0])
                    for t in range(CHT):
                        par = t % 2
                        gt = base + t
                        if t + 1 < CHT:
                            P.dma("qs", x1td[1 - par][:],
                                  x1s[seq, (gt + 1) * 128:(gt + 2) * 128, :],
                                  writes=[x1_b[1 - par]])
                            emit_ln(P, stC[1 - par], x1td[1 - par][:], [x1_b[1 - par]],
                                    u2t[1 - par][:], u2_b[1 - par])
                        P.op("tensor", tr8_emit(psT, u2t[par][:], identb[:]),
                             reads=[u2_b[par], identb_b], writes=[psT_b])
                        P.op("scalar", lambda e, o=u2T[:, t * 1024:(t + 1) * 1024], i=psT[:]:
                             e.copy(out=o, in_=i),
                             reads=[psT_b], writes=[u2T_tb[t]])
                        P.op("scalar", lambda e, o=acc[:, t * D:(t + 1) * D], i=u2t[par][:]:
                             e.copy(out=o, in_=i),
                             reads=[u2_b[par]], writes=[acc_tb[t]])
                        if t % 4 == 3:
                            t0 = t - 3

                            def rt4_emit(e, t0=t0):
                                for tt_ in range(t0, t0 + 4):
                                    sl = (tt_ - t0) * NE
                                    for c in range(DC):
                                        e.matmul(psS0[:, sl:sl + NE],
                                                 u2T[:, tt_ * 1024 + c * 128: tt_ * 1024 + (c + 1) * 128],
                                                 CTs[:, c * NE:(c + 1) * NE],
                                                 start=(c == 0), stop=False)
                                    last = e.matmul(psS0[:, sl:sl + NE], ones1[:], ebr[:],
                                                    start=False, stop=True)
                                return last
                            P.op("tensor", rt4_emit,
                                 reads=[u2T_tb[t0], u2T_tb[t0 + 1], u2T_tb[t0 + 2],
                                        u2T_tb[t0 + 3], CT_b, ones1_b, eb_b],
                                 writes=[psS0_b])
                            P.op("scalar", lambda e, o=rl_ch[:, t0 * NE:(t0 + 4) * NE],
                                 i=psS0[:, 0:4 * NE]: e.copy(out=o, in_=i),
                                 reads=[psS0_b], writes=[rl_tb[t0 // 4]])
                            for tt_ in range(t0, t0 + 4):
                                pr2 = tt_ % 2
                                P.op("vector", lambda e, o=gsm[pr2][:, 0:1],
                                     i=rl_ch[:, tt_ * NE:(tt_ + 1) * NE]:
                                     e.reduce_max(out=o, in_=i, axis=AX.X, negate=True),
                                     reads=[rl_tb[t0 // 4]], writes=[nbr_b[pr2]])
                                P.op("scalar", lambda e, o=sdash_ch[:, tt_ * NE:(tt_ + 1) * NE],
                                     i=rl_ch[:, tt_ * NE:(tt_ + 1) * NE], b=gsm[pr2][:, 0:1]:
                                     e.activation(out=o, in_=i, func=AF.Exp, bias=b),
                                     reads=[rl_tb[t0 // 4], nbr_b[pr2]], writes=[sdc_tb[tt_]])
                    P.dma("qg", wr[0][:].rearrange("p (c n) -> p c n", n=1024),
                          WrTd[0].rearrange("(c p) n -> p c n", p=128), writes=[wr_b[0]])
                    P.dma("qg", wr[1][:].rearrange("p (c n) -> p c n", n=1024),
                          WrTd[1].rearrange("(c p) n -> p c n", p=128), writes=[wr_b[1]])

                    def emit_accinit(t):
                        psx, psx_b = (psA, psA_b) if t % 2 == 0 else (psB, psB_b)

                        def ai_emit(e, ps=psx[:], t=t):
                            for c in range(DC):
                                for n in range(2):
                                    e.matmul(ps[:, n * 512:(n + 1) * 512],
                                             u2T[:, t * 1024 + c * 128: t * 1024 + (c + 1) * 128],
                                             Wsh[:, c * 1024 + n * 512: c * 1024 + n * 512 + 512],
                                             start=(c == 0), stop=False)
                            for n in range(2):
                                e.matmul(ps[:, n * 512:(n + 1) * 512],
                                         gT_ch[0:16, t * 128:(t + 1) * 128],
                                         brs[:, n * 512:n * 512 + 512], start=False, stop=False)
                                e.matmul(ps[:, n * 512:(n + 1) * 512], halves[:],
                                         bs0r[:, n * 512:n * 512 + 512], start=False, stop=False)
                                last = e.matmul(ps[:, n * 512:(n + 1) * 512], halves[:],
                                                bs1r[:, n * 512:n * 512 + 512],
                                                start=False, stop=True)
                            return last
                        P.op("tensor", ai_emit,
                             reads=[u2T_tb[t], Wsh_b, gT_tb[t // 4], br_b, halves_b,
                                    bs0_b, bs1_b],
                             writes=[psx_b])
                        P.op("vector", lambda e, o=acc[:, t * D:(t + 1) * D], i=psx[:]:
                             e.tensor_tensor(out=o, in0=i, in1=o, op=ALU.add),
                             reads=[psx_b], writes=[acc_tb[t]])

                    def emit_expert_tile(e_i, t):
                        sl = e_i % 2
                        psx, psx_b = (psA, psA_b) if t % 2 == 0 else (psB, psB_b)

                        def ex_emit(e, ps=psx[:], t=t, sl=sl):
                            for c in range(DC):
                                for n in range(2):
                                    last = e.matmul(
                                        ps[:, n * 512:(n + 1) * 512],
                                        u2T[:, t * 1024 + c * 128: t * 1024 + (c + 1) * 128],
                                        wr[sl][:, c * 1024 + n * 512: c * 1024 + n * 512 + 512],
                                        start=(c == 0), stop=(c == DC - 1))
                            return last
                        P.op("tensor", ex_emit,
                             reads=[u2T_tb[t], wr_b[sl]], writes=[psx_b])
                        P.op("vector", lambda e, o=acc[:, t * D:(t + 1) * D], i=psx[:],
                             gsc=g_ch[:, t * NE + e_i: t * NE + e_i + 1]:
                             e.scalar_tensor_tensor(out=o, in0=i, scalar=gsc, in1=o,
                                                    op0=ALU.mult, op1=ALU.add),
                             reads=[psx_b, g_tb[t]], writes=[acc_tb[t]])

                    for t in range(CHT):
                        par = t % 2
                        sd_ap = sdash_ch[:, t * NE:(t + 1) * NE]
                        P.op("vector", lambda e, o=gsm[par][:, 1:2], i=sd_ap:
                             e.reduce_sum(out=o, in_=i, axis=AX.X),
                             reads=[sdc_tb[t]], writes=[zr_b[par]])
                        P.op("vector", lambda e, o=gsm[par][:, 2:3], i=gsm[par][:, 1:2]:
                             e.reciprocal(out=o, in_=i),
                             reads=[zr_b[par]], writes=[rz_b[par]])
                        P.op("vector", lambda e, o=gsm[par][:, 3:4], i=sd_ap:
                             e.reduce_max(out=o, in_=i, axis=AX.X),
                             reads=[sdc_tb[t]], writes=[m1_b[par]])
                        P.op("vector", lambda e, o=gtmp[par][:], i=sd_ap, m=gsm[par][:, 3:4]:
                             e.tensor_scalar(out=o, in0=i, scalar1=m, scalar2=-2.0,
                                             op0=ALU.is_ge, op1=ALU.mult),
                             reads=[sdc_tb[t], m1_b[par]], writes=[gtm_b[par]])
                        P.op("vector", lambda e, o=gtmp[par][:], i=sd_ap:
                             e.tensor_tensor(out=o, in0=i, in1=o, op=ALU.add),
                             reads=[sdc_tb[t]], writes=[gtm_b[par]])
                        P.op("vector", lambda e, o=gsm[par][:, 4:5], i=gtmp[par][:]:
                             e.reduce_max(out=o, in_=i, axis=AX.X),
                             reads=[gtm_b[par]], writes=[m2_b[par]])
                        P.op("vector", lambda e, o=gtmp[par][:], i=sd_ap, m=gsm[par][:, 4:5]:
                             e.tensor_scalar(out=o, in0=i, scalar1=m, scalar2=None, op0=ALU.is_ge),
                             reads=[sdc_tb[t], m2_b[par]], writes=[gtm_b[par]])
                        P.op("vector", lambda e, o=g_ch[:, t * NE:(t + 1) * NE], i=sd_ap,
                             r=gsm[par][:, 2:3], m=gtmp[par][:]:
                             e.scalar_tensor_tensor(out=o, in0=i, scalar=r, in1=m,
                                                    op0=ALU.mult, op1=ALU.mult),
                             reads=[sdc_tb[t], rz_b[par], gtm_b[par]], writes=[g_tb[t]])
                        if t % 4 == 3:
                            t0 = t - 3

                            def gt4_emit(e, t0=t0):
                                for tt_ in range(t0, t0 + 4):
                                    last = e.transpose(
                                        psS1[0:16, (tt_ - t0) * 128:(tt_ - t0 + 1) * 128],
                                        g_ch[:, tt_ * NE:(tt_ + 1) * NE], identf[:])
                                return last
                            P.op("tensor", gt4_emit,
                                 reads=[g_tb[t0], g_tb[t0 + 1], g_tb[t0 + 2], g_tb[t0 + 3],
                                        identf_b],
                                 writes=[psS1_b])
                            P.op("vector", lambda e, o=gT_ch[0:16, t0 * 128:(t0 + 4) * 128],
                                 i=psS1[0:16, 0:512]: e.tensor_copy(out=o, in_=i),
                                 reads=[psS1_b], writes=[gT_tb[t0 // 4]])
                            for t2 in range(t0, t0 + 4):
                                emit_accinit(t2)
                            for t2 in range(t0, t0 + 4):
                                emit_expert_tile(0, t2)
                    P.dma("qg", wr[0][:].rearrange("p (c n) -> p c n", n=1024),
                          WrTd[2].rearrange("(c p) n -> p c n", p=128), writes=[wr_b[0]])
                    for e_i in range(1, NE):
                        sl = e_i % 2
                        for t in range(CHT):
                            emit_expert_tile(e_i, t)
                            if e_i == NE - 1:
                                emit_final(t, ch)
                        if e_i + 2 < NE:
                            P.dma("qg", wr[sl][:].rearrange("p (c n) -> p c n", n=1024),
                                  WrTd[e_i + 2].rearrange("(c p) n -> p c n", p=128),
                                  writes=[wr_b[sl]])

                    if f"acc{seq}_{ch}" in taps:
                        tap(P, f"acc{seq}_{ch}", acc[:], acc_tb, [128, CHT * D], F32)
                    if f"g{seq}_{ch}" in taps:
                        tap(P, f"g{seq}_{ch}", g_ch[:], g_tb, [128, CHT * NE], F32)
                    if ch == NCH - 1:
                        P.barrier()
                    P.flush()

        P.barrier()
        P.flush()
    return nc, tap_outs


def _host_inputs(inputs):
    f32 = np.float32
    bf16 = ml_dtypes.bfloat16
    X = np.ascontiguousarray(np.asarray(inputs["L0_attn_X"], f32))
    shared = {
        "Wq": np.ascontiguousarray(np.asarray(inputs["L0_attn_Wq"], f32)),
        "Wk": np.ascontiguousarray(np.asarray(inputs["L0_attn_Wk"], f32)),
        "Wv": np.ascontiguousarray(np.asarray(inputs["L0_attn_Wv"], f32)),
        "bq": np.ascontiguousarray(np.asarray(inputs["L0_attn_bq"], f32).reshape(1, D)),
        "bk": np.ascontiguousarray(np.asarray(inputs["L0_attn_bk"], f32).reshape(1, D)),
        "bv": np.ascontiguousarray(np.asarray(inputs["L0_attn_bv"], f32).reshape(1, D)),
        "CT": np.ascontiguousarray(np.asarray(inputs["L0_ffn_centroids"], f32).T),
        "eb": np.ascontiguousarray(np.asarray(inputs["L0_ffn_expert_biases"], f32).reshape(1, NE)),
        "WrT": np.ascontiguousarray(np.asarray(inputs["L0_ffn_Wr"], f32).transpose(0, 2, 1)),
        "br": np.ascontiguousarray(np.asarray(inputs["L0_ffn_br"], f32)),
        "WsT": np.ascontiguousarray(np.asarray(inputs["L0_ffn_Ws"], f32).transpose(0, 2, 1)),
        "bs": np.ascontiguousarray(np.asarray(inputs["L0_ffn_bs"], f32)),
        "identb": np.eye(128, dtype=bf16),
        "identf": np.eye(128, dtype=f32),
        "mask": np.triu(np.full((128, 128), -1.0e30, f32), 1),
        "ones1": np.ones((1, 128), dtype=bf16),
        "halves": np.full((1, 128), 0.5, dtype=bf16),
    }
    in_maps = []
    for c in range(NCORES):
        m = dict(shared)
        m["X"] = np.ascontiguousarray(X[c * SPC:(c + 1) * SPC])
        in_maps.append(m)
    return in_maps


def kernel(**inputs):
    in_maps = _host_inputs(inputs)
    nc, _ = build()
    res = run_bass_kernel_spmd(nc, in_maps, core_ids=list(range(NCORES)))
    out = np.concatenate([np.asarray(res.results[c]["Y"]) for c in range(NCORES)], axis=0)
    return out.astype(np.float32, copy=False)
